# version 27
# baseline (speedup 1.0000x reference)
import numpy as np
from contextlib import ExitStack

import concourse.bass as bass
import concourse.mybir as mybir
from concourse.bass_utils import run_bass_kernel_spmd

F32 = mybir.dt.float32
BF16 = mybir.dt.bfloat16
AF = mybir.ActivationFunctionType
ALU = mybir.AluOpType
AX = mybir.AxisListType

S, D, H, NN = 2048, 1024, 4, 8
NSL = S // 128
NCH, LCH = 32, 64
EPS = 1e-5
NTILE = 4
NEXP = 16
FEXP = 512

_HOST_CONST = {}
def _host_consts():
    if _HOST_CONST:
        return _HOST_CONST
    _HOST_CONST["c_ident"] = np.eye(128, dtype=np.float32)
    _HOST_CONST["c_ones"] = np.ones((1, 128), dtype=np.float32)
    _HOST_CONST["c_iota"] = np.broadcast_to(
        np.tile(np.arange(8, dtype=np.float32), 64), (128, 512)).copy()
    ida = np.zeros((8, 9), dtype=np.float32)
    ida[:, :8] = np.eye(8, dtype=np.float32)
    _HOST_CONST["c_idaug"] = np.broadcast_to(ida.reshape(-1), (128, 72)).copy()
    oneh = np.zeros((16, 2048), dtype=np.float32)
    for e_ in range(16):
        oneh[e_, e_ * 128:(e_ + 1) * 128] = 1.0
    _HOST_CONST["c_oneh"] = oneh
    pm = np.zeros((128, 128), dtype=np.float32)
    for m in range(128):
        h_, c_ = m // 32, m % 32
        pm[c_ * 4 + h_, m] = 1.0
    _HOST_CONST["c_perm"] = pm
    return _HOST_CONST


class _Sem:
    def __init__(self, name):
        self.name = name
        self.n = 0
        self.h = None


_DRAIN_OPS = {
    "tensor_tensor", "tensor_add", "tensor_sub", "tensor_mul", "tensor_max",
    "tensor_copy", "tensor_scalar", "tensor_single_scalar", "tensor_scalar_add",
    "tensor_scalar_sub", "tensor_scalar_mul", "tensor_scalar_max", "tensor_scalar_min",
    "tensor_relu", "tensor_reduce", "reduce_max", "reduce_sum", "scalar_tensor_tensor",
    "tensor_tensor_scan", "memset", "memzero", "reciprocal", "select", "transpose",
    "activation", "sign", "square", "sqrt", "mul", "copy", "add",
}


class _DrainAfter:
    def __init__(self, e):
        self._e = e

    @property
    def raw(self):
        return self._e

    def __getattr__(self, k):
        f = getattr(self._e, k)
        if k in _DRAIN_OPS:
            def g(*a, **kw):
                r = f(*a, **kw)
                self._e.drain()
                return r
            return g
        return f


class _Prog:
    def __init__(self):
        self.ops = {k: [] for k in ("sync", "gpsimd", "tensor", "vector", "scalar")}

    def emit(self, eng, fn, waits=(), inc=None, k=1):
        waits = tuple((s, v) for (s, v) in waits)
        def cl(e, fn=fn, waits=waits, inc=inc, k=k):
            for (s, v) in waits:
                e.wait_ge(s.h, v)
            r = fn(e)
            if inc is not None:
                r.then_inc(inc.h, k)
        self.ops[eng].append(cl)
        if inc is not None:
            inc.n += k
        return inc.n if inc is not None else None


def build(dbg=0):
    nc = bass.Bass("TRN2", target_bir_lowering=False)
    I = lambda name, shape: nc.dram_tensor(name, list(shape), F32, kind="ExternalInput").ap()

    d_x = I("x", (S, D))
    d_lnw = I("ln_w", (D,)); d_lnb = I("ln_b", (D,))
    d_signW = I("sign_W", (D, 32)); d_signb = I("sign_b", (32,))
    d_permW = I("perm_W", (D, 256)); d_permb = I("perm_b", (256,))
    d_gateW = I("gate_W", (D, 32)); d_gateb = I("gate_b", (32,))
    d_inW = I("in_W", (D, 32)); d_inb = I("in_b", (32,))
    d_outW = I("out_W", (32, D)); d_outb = I("out_b", (D,))
    d_wr = I("Wr", (D, NEXP))
    d_wg = I("Wg", (NEXP, D, FEXP)); d_wu = I("Wu", (NEXP, D, FEXP)); d_wd = I("Wd", (NEXP, FEXP, D))
    d_sg = I("Sg", (D, D)); d_su = I("Su", (D, D)); d_sd = I("Sd", (D, D))
    d_ident = I("c_ident", (128, 128)); d_ones = I("c_ones", (1, 128))
    d_iota = I("c_iota", (128, 512)); d_idaug = I("c_idaug", (128, 72))
    d_oneh = I("c_oneh", (16, 2048))
    d_perm = I("c_perm", (128, 128))

    d_y = nc.dram_tensor("y", [S, D], F32, kind="ExternalOutput").ap()
    d_scrG = nc.dram_tensor("scrG", [18, 128, 8, FEXP], BF16, kind="Internal").ap()
    d_scrU = nc.dram_tensor("scrU", [18, 128, 8, FEXP], BF16, kind="Internal").ap()
    d_scrD = nc.dram_tensor("scrD", [18, 128, 4, D], BF16, kind="Internal").ap()
    dbg_t = {}
    if dbg:
        def DBG(name, shape, dt=F32):
            dbg_t[name] = nc.dram_tensor(name, list(shape), dt, kind="ExternalOutput").ap()
        DBG("dbg_xnf", (128, 8, S), BF16)
        DBG("dbg_projC", (128, LCH, 88))
        DBG("dbg_A", (128, 4096))
        DBG("dbg_hard", (128, 4096))
        DBG("dbg_dt", (128, 512)); DBG("dbg_ct", (128, 512))
        DBG("dbg_mall", (128, 9 * LCH * 8))
        DBG("dbg_hs", (128, 8))
        DBG("dbg_ys", (128, 512), BF16)
        DBG("dbg_xattn", (128, NSL, D))
        DBG("dbg_xnf3", (128, 8, S), BF16)
        DBG("dbg_cwt", (NEXP, S), BF16)
        DBG("dbg_rank", (128, 512)); DBG("dbg_aidx", (128, 512)); DBG("dbg_rmx", (128, 512))

    P = _Prog()
    LDX = _Sem("LDX"); LDP = _Sem("LDP"); GDK = _Sem("GDK")
    GPJ = [_Sem("GPJ0"), _Sem("GPJ1")]; SPJ = [_Sem("SPJ0"), _Sem("SPJ1")]; GSC = _Sem("GSC")
    FWG = [[_Sem("FWG0a"), _Sem("FWG0b")], [_Sem("FWG1a"), _Sem("FWG1b")]]
    FWD = [_Sem("FWD0"), _Sem("FWD1")]
    SCG = [_Sem("SCG0"), _Sem("SCG1")]; SCD = [_Sem("SCD0"), _Sem("SCD1")]
    FWGS = [[_Sem("FWGS0a"), _Sem("FWGS0b")], [_Sem("FWGS1a"), _Sem("FWGS1b")]]
    FWDS = [_Sem("FWDS0"), _Sem("FWDS1")]
    TE = _Sem("TE"); VE = _Sem("VE"); SE = _Sem("SE"); OUT = _Sem("OUT"); OUT2 = _Sem("OUT2")
    GP = _Sem("GP")
    sems = ([LDX, LDP, GDK] + GPJ + SPJ + [GSC] + FWG[0] + FWG[1] + FWD + SCG + SCD +
            FWGS[0] + FWGS[1] + FWDS + [TE, VE, SE, OUT, OUT2, GP])

    with ExitStack() as ctx:
        def SB(name, shape, dt=F32):
            return ctx.enter_context(nc.sbuf_tensor(name, shape, dt))
        XT = SB("XT", [128, NSL, D])
        XNF = SB("XNF", [128, 8, S], BF16)
        ARN = 24064
        ARH = SB("ARENA", [128, ARN])
        AR = ARH[:]
        def carve(off, nw, shape=None, dt=F32, rr=None, **kw):
            v = AR[:, off:off + nw]
            if dt is not F32:
                v = v.bitcast(dt)
            if rr is not None:
                v = v.rearrange(rr, **kw)
            return v
        A = carve(0, 4096)
        SQS = carve(0, 1024)
        HARD4 = carve(4096, 4096, rr="p (t i j) -> p t i j", t=LCH, i=NN)
        HARD = carve(4096, 4096)
        WP = carve(4096, 1408, dt=BF16, rr="p (k w) -> p k w", k=8)
        Y2BF = carve(4096 + 1408, 1024, dt=BF16, rr="p (a d) -> p a d", a=2)
        MALL = carve(8192, 4608, rr="p (t i k) -> p t i k", t=LCH, i=NN)
        WPF = carve(8192, 2816, rr="p (k w) -> p k w", k=8)
        DT = carve(12800, 512, rr="p (t i) -> p t i", t=LCH)
        CT = carve(13312, 512, rr="p (t i) -> p t i", t=LCH)
        PJT = carve(12800, 704, rr="p (a w) -> p a w", a=2)
        PJC = carve(13824, 5632, rr="p (t w) -> p t w", t=LCH)
        SCR = 19456
        SG = carve(SCR + 0, 512)
        RANK = carve(SCR + 0, 512)
        GM = carve(SCR + 512, 512)
        CAV = carve(SCR + 512, 512)
        RMX = carve(SCR + 1024, 512)
        T512a = carve(SCR + 1536, 512)
        T512b = carve(SCR + 2048, 512)
        RS = carve(SCR + 2560, 512)
        AIDX = carve(SCR + 2560, 512)
        ACCR = carve(SCR + 2560, 512)
        CS = carve(SCR + 3072, 512)
        SROW = carve(SCR + 3072, 512)
        YS = carve(SCR + 3072, 256, dt=BF16, rr="p (i t) -> p i t", i=NN)
        CV = carve(SCR + 3584, 512)
        YS2 = carve(SCR + 3584, 256, dt=BF16, rr="p (i t) -> p i t", i=NN)
        IOTA = carve(11008, 512)
        M64 = carve(SCR + 4096, 64)
        CIDX = carve(SCR + 4160, 64)
        PFX = carve(SCR + 4224, 72, rr="p (i k) -> p i k", i=NN)
        PFS = carve(SCR + 4296, 72, rr="p (i k) -> p i k", i=NN)
        NEWP = carve(SCR + 4368, 72, rr="p (i k) -> p i k", i=NN)
        IDA = carve(SCR + 4440, 72)
        HSB = carve(SCR + 4512, 9)
        UCOL = carve(SCR + 4522, 8)
        PFM = carve(SCR + 4532, 64)
        SLN = carve(8192, 512)
        MALL9 = carve(8192, 5184, rr="p (t i k) -> p t i k", t=LCH, i=9)
        AH9 = carve(13824, 4608, rr="p (t i j) -> p t i j", t=LCH, i=NN)
        PRD9 = carve(18432, 648)
        PRD = carve(14976, 648)
        PRDR = carve(0, 4608)
        GATED = carve(0, 9216, dt=BF16, rr="p (e k f) -> p e k f", e=9, k=4)
        WGr = carve(9216, 4096, dt=BF16, rr="p (r k f) -> p r k f", r=2, k=8)
        WUr = carve(13312, 4096, dt=BF16, rr="p (r k f) -> p r k f", r=2, k=8)
        WDr = carve(17408, 4096, dt=BF16, rr="p (r k f) -> p r k f", r=2, k=4)
        SILU = carve(21504, 1536, rr="p (a f) -> p a f", a=3)
        ST = SB("ST", [128, 16, NSL])
        LNWC = SB("LNWC", [128, 8]); LNBC = SB("LNBC", [128, 8])
        BROW = SB("BROW", [1, 352]); BROWB = SB("BROWB", [1, 352], BF16)
        OWB = SB("OWB", [32, D], BF16); OBB = SB("OBB", [1, D], BF16)
        WRB = SB("WRB", [128, 8, NEXP], BF16)
        IDT = SB("IDT", [128, 128], BF16); ONES = SB("ONESb", [1, 128], BF16)
        ONEH = SB("ONEH", [16, 2048], BF16); PERMB = SB("PERMB", [128, 128], BF16)
        E16 = SB("E16", [128, NEXP]); P16 = SB("P16", [128, NEXP]); PW1 = SB("PW1", [128, NEXP])
        OH = SB("OH", [128, NEXP]); CWS = SB("CWS", [128, NEXP]); CWBF = SB("CWBF", [128, NEXP], BF16)
        MX1 = SB("MX1", [128, 1]); MX2 = SB("MX2", [128, 1]); SM1 = SB("SM1", [128, 1])
        CWT = SB("CWT", [NEXP, S], BF16)
        YST = ARH[:][0:32, 8192:9216].bitcast(BF16)
        B = [ctx.enter_context(nc.psum_tensor(f"pb{i}", [128, 512], F32)) for i in range(8)]

        for s_ in sems:
            s_.h = ctx.enter_context(nc.semaphore(s_.name))
        blk = ctx.enter_context(nc.Block())

        iSUM, iNMU, iSSQ, iCMB, iNMC, iA1, iNM1, iS3, iNM3, iSQ3, iIS3, iNI3 = range(12)

        em = P.emit
        def dbgw():
            return [(OUT, OUT.n)] if dbg else []
        for q in range(4):
            em("sync", lambda e, q=q: e.dma_start(out=XT[:, 4*q:4*q+4, :],
                in_=d_x.rearrange("(s p) d -> p s d", p=128)[:, 4*q:4*q+4, :]), inc=LDX, k=16)
        LDX_N = LDX.n

        def wdma(dst_off, width, src, kc):
            def f(e):
                return e.dma_start(
                    out=WPF[:, kc, :].rearrange("p (h w) -> p h w", h=H)[:, :, dst_off:dst_off + width],
                    in_=src[kc * 128:(kc + 1) * 128, :].rearrange("p (h w) -> p h w", h=H))
            return f
        for kc in range(8):
            em("sync", wdma(0, 8, d_signW, kc), inc=LDP, k=16)
            em("sync", wdma(8, 64, d_permW, kc), inc=LDP, k=16)
            em("sync", wdma(72, 8, d_gateW, kc), inc=LDP, k=16)
            em("sync", wdma(80, 8, d_inW, kc), inc=LDP, k=16)
        em("sync", lambda e: e.dma_start(out=LNWC[:], in_=d_lnw.rearrange("(k p) -> p k", p=128), allow_slow_non_contiguous=True), inc=LDP, k=16)
        em("sync", lambda e: e.dma_start(out=LNBC[:], in_=d_lnb.rearrange("(k p) -> p k", p=128), allow_slow_non_contiguous=True), inc=LDP, k=16)
        em("sync", lambda e: e.dma_start(
            out=BROW[:].rearrange("o (h w) -> o h w", h=H)[:, :, 0:8],
            in_=d_signb[None, :].rearrange("o (h w) -> o h w", h=H)), inc=LDP, k=16)
        em("sync", lambda e: e.dma_start(
            out=BROW[:].rearrange("o (h w) -> o h w", h=H)[:, :, 8:72],
            in_=d_permb[None, :].rearrange("o (h w) -> o h w", h=H)), inc=LDP, k=16)
        em("sync", lambda e: e.dma_start(
            out=BROW[:].rearrange("o (h w) -> o h w", h=H)[:, :, 72:80],
            in_=d_gateb[None, :].rearrange("o (h w) -> o h w", h=H)), inc=LDP, k=16)
        em("sync", lambda e: e.dma_start(
            out=BROW[:].rearrange("o (h w) -> o h w", h=H)[:, :, 80:88],
            in_=d_inb[None, :].rearrange("o (h w) -> o h w", h=H)), inc=LDP, k=16)
        em("sync", lambda e: e.dma_start(out=IOTA[:], in_=d_iota), inc=LDP, k=16)
        em("sync", lambda e: e.dma_start(out=IDA[:], in_=d_idaug), inc=LDP, k=16)
        LDW = LDP.n

        em("gpsimd", lambda e: e.dma_start(out=IDT[:], in_=d_ident), inc=GDK, k=16)
        em("gpsimd", lambda e: e.dma_start(out=ONES[:], in_=d_ones), inc=GDK, k=16)
        em("gpsimd", lambda e: e.dma_start(out=OWB[:], in_=d_outW), inc=GDK, k=16)
        em("gpsimd", lambda e: e.dma_start(out=OBB[:], in_=d_outb[None, :]), inc=GDK, k=16)
        em("gpsimd", lambda e: e.dma_start(out=WRB[:], in_=d_wr.rearrange("(k p) n -> p k n", p=128)), inc=GDK, k=16)
        em("gpsimd", lambda e: e.dma_start(out=ONEH[:], in_=d_oneh), inc=GDK, k=16)
        em("gpsimd", lambda e: e.dma_start(out=PERMB[:], in_=d_perm), inc=GDK, k=16)
        GDC = GDK.n

        for kc in range(8):
            em("vector", lambda e, kc=kc: e.tensor_scalar(
                WP[:, kc, :], WPF[:, kc, :], LNWC[:, kc:kc + 1], None, op0=ALU.mult),
                waits=[(LDP, LDW)], inc=VE)
        VE_WP = VE.n
        for kc in range(8):
            em("tensor", lambda e, kc=kc: e.matmul(
                B[7][0:1, 0:352], LNBC[:, kc:kc + 1], WPF[:, kc, :],
                start=(kc == 0), stop=(kc == 7)),
                waits=[(LDP, LDW)] if kc == 0 else (), inc=TE if kc == 7 else None)
        TE_BP = TE.n
        em("vector", lambda e: e.tensor_add(BROW[0:1, :], BROW[0:1, :], B[7][0:1, 0:352]),
           waits=[(TE, TE_BP)], inc=VE)
        em("vector", lambda e: e.tensor_copy(BROWB[0:1, :], BROW[0:1, :]), inc=VE)
        VE_BROW = VE.n

        inv_d = 1.0 / D
        def ln1sums(e):
            e.tensor_reduce(ST[:, iSUM, :], XT[:], axis=AX.X, op=ALU.add)
            e.tensor_scalar(ST[:, iNMU, :], ST[:, iSUM, :], -inv_d, None, op0=ALU.mult)
            return e.raw.engine_nop()
        em("vector", ln1sums, waits=[(LDX, LDX_N)], inc=VE)
        VE_MU = VE.n
        for s in range(NSL):
            em("scalar", lambda e, s=s: e.activation(
                SQS[:], XT[:, s, :], AF.Square, bias=ST[:, iNMU, s:s + 1], scale=1.0,
                accum_out=ST[:, iSSQ, s:s + 1]),
                waits=[(VE, VE_MU)] if s == 0 else (), inc=SE)
        SE_SQ = SE.n
        stv = ST[:, iSSQ, :]
        def stats1(e):
            e.tensor_scalar(ST[:, iSQ3, :], stv, inv_d, EPS, op0=ALU.mult, op1=ALU.add)
            e.drain()
            e.reciprocal(ST[:, iIS3, :], ST[:, iSQ3, :])
            e.drain()
            e.tensor_scalar(ST[:, iNI3, :], stv, inv_d, None, op0=ALU.mult)
            e.tensor_mul(ST[:, iNI3, :], ST[:, iNI3, :], ST[:, iIS3, :])
            e.tensor_scalar(ST[:, iNI3, :], ST[:, iNI3, :], EPS, None, op0=ALU.add)
            e.drain()
            return e.engine_nop()
        em("vector", stats1, waits=[(SE, SE_SQ)], inc=VE)
        def stats2(e):
            e.sqrt(ST[:, iSQ3, :], ST[:, iSQ3, :])
            e.sqrt(ST[:, iNI3, :], ST[:, iNI3, :])
            return e.copy(ST[:, iS3, :], ST[:, iNI3, :])
        em("scalar", stats2, waits=[(VE, VE.n)], inc=SE)
        def stats3(e):
            e.reciprocal(ST[:, iA1, :], ST[:, iSQ3, :])
            e.drain()
            e.reciprocal(ST[:, iCMB, :], ST[:, iNI3, :])
            e.drain()
            e.tensor_mul(ST[:, iCMB, :], ST[:, iCMB, :], ST[:, iA1, :])
            e.tensor_mul(ST[:, iNM1, :], ST[:, iNMU, :], ST[:, iA1, :])
            e.tensor_scalar(ST[:, iA1, :], ST[:, iA1, :], 1.0, None, op0=ALU.add)
            e.tensor_mul(ST[:, iNMC, :], ST[:, iNMU, :], ST[:, iCMB, :])
            e.drain()
            return e.engine_nop()
        em("vector", stats3, waits=[(SE, SE.n)], inc=VE)
        VE_ST1 = VE.n

        TE_TRS = {}
        for s in range(NSL):
            em("scalar", lambda e, s=s: e.raw.activation(
                Y2BF[:, s % 2, :], XT[:, s, :], AF.Identity,
                bias=ST[:, iNMC, s:s + 1], scale=ST[:, iCMB, s:s + 1]),
                waits=[(VE, VE_ST1)] + ([(TE, TE_TRS[s - 2])] if s >= 2 else []), inc=SE)
            se_y = SE.n
            for kc in range(8):
                pb = B[5 + (kc % 2)]
                tw = ([(SE, se_y), (GDK, GDC)] if kc == 0 else [])
                if kc >= 2 or s > 0:
                    tw = tw + [(VE, VE.n)]
                em("tensor", lambda e, s=s, kc=kc, pb=pb: e.transpose(
                    pb[:].bitcast(BF16)[:, 0:128], Y2BF[:, s % 2, kc * 128:(kc + 1) * 128], IDT[:]),
                    waits=tw, inc=TE)
                te_tr = TE.n
                em("vector", lambda e, s=s, kc=kc, pb=pb: e.raw.tensor_copy(
                    XNF[:, kc, s * 128:(s + 1) * 128], pb[:].bitcast(BF16)[:, 0:128]),
                    waits=[(TE, te_tr)], inc=VE)
            TE_TRS[s] = TE.n
        VE_XNF = VE.n
        if dbg:
            em("sync", lambda e: e.dma_start(out=dbg_t["dbg_xnf"], in_=XNF[:]),
               waits=[(VE, VE_XNF)], inc=OUT, k=16)

        SE_PJ = {}
        GD_PJd = {}
        for s in range(NSL):
            pb = B[0 + (s % 2)]
            for kc in range(8):
                em("tensor", lambda e, s=s, kc=kc, pb=pb: e.matmul(
                    pb[:, 0:352], XNF[:, kc, s * 128:(s + 1) * 128], WP[:, kc, :],
                    start=(kc == 0), stop=False),
                    waits=([(VE, VE_XNF), (VE, VE_WP), (VE, VE_BROW), (GDK, GDC)]
                           if s == 0 and kc == 0 else []) +
                          ([(SE, SE_PJ[s - 2])] if s >= 2 and kc == 0 else []))
            em("tensor", lambda e, pb=pb: e.matmul(
                pb[:, 0:352], ONES[0:1, :], BROWB[0:1, :], start=False, stop=True), inc=TE)
            te_p = TE.n
            em("scalar", lambda e, s=s, pb=pb: e.raw.copy(PJT[:, s % 2, :], pb[:, 0:352]),
               waits=[(TE, te_p)] + ([(GPJ[s % 2], GD_PJd[s - 2][0]), (SPJ[s % 2], GD_PJd[s - 2][1])] if s >= 2 else []), inc=SE)
            SE_PJ[s] = SE.n
            for h in range(4):
                qeng = "gpsimd" if h < 2 else "sync"
                qsem = GPJ[s % 2] if h < 2 else SPJ[s % 2]
                for a in range(2):
                    em(qeng, lambda e, s=s, h=h, a=a: e.dma_start(
                        out=PJC[8 * s + 4 * a + h: 8 * s + 4 * a + h + 1],
                        in_=PJT[a * 64:(a + 1) * 64, s % 2, h * 88:(h + 1) * 88]),
                        waits=[(SE, SE_PJ[s])] if a == 0 and h in (0, 2) else (), inc=qsem, k=16)
            GD_PJd[s] = (GPJ[s % 2].n, SPJ[s % 2].n)
        GD_PJ0 = GPJ[0].n; GD_PJ1 = GPJ[1].n; SD_PJ0 = SPJ[0].n; SD_PJ1 = SPJ[1].n
        if dbg:
            em("sync", lambda e: e.dma_start(out=dbg_t["dbg_projC"], in_=PJC[:]),
               waits=[(GPJ[0], GD_PJ0), (GPJ[1], GD_PJ1), (SPJ[0], SD_PJ0), (SPJ[1], SD_PJ1)], inc=OUT, k=16)

        em("scalar", lambda e: e.activation(SG[:].rearrange("p (t i) -> p t i", i=8),
                                            PJC[:, :, 0:8], AF.Sigmoid),
           waits=[(GPJ[0], GD_PJ0), (GPJ[1], GD_PJ1), (SPJ[0], SD_PJ0), (SPJ[1], SD_PJ1)], inc=SE)
        em("scalar", lambda e: e.activation(GM[:].rearrange("p (t i) -> p t i", i=8),
                                            PJC[:, :, 72:80], AF.Sigmoid), inc=SE)
        em("scalar", lambda e: e.activation(A[:].rearrange("p (t ij) -> p t ij", ij=64),
                                            PJC[:, :, 8:72], AF.Exp), inc=SE)
        em("scalar", lambda e: e.copy(RMX[:].rearrange("p (t i) -> p t i", i=8),
                                      PJC[:, :, 80:88]), inc=SE)
        SE_SIG = SE.n
        def dcalc_g(e):
            e.tensor_scalar(T512a[:], SG[:], 2.0, 1.0, op0=ALU.mult, op1=ALU.subtract)
            e.tensor_tensor(DT.rearrange("p t i -> p (t i)"), GM[:], T512a[:], op=ALU.mult)
            e.tensor_scalar(T512b[:], GM[:], -1.0, 1.0, op0=ALU.mult, op1=ALU.add)
            e.tensor_tensor(CT.rearrange("p t i -> p (t i)"), RMX[:], T512b[:], op=ALU.mult)
            return e.raw.nop()
        em("gpsimd", dcalc_g, waits=[(SE, SE_SIG)] + dbgw(), inc=GP)
        GP_DC = GP.n
        if dbg:
            em("sync", lambda e: e.dma_start(out=dbg_t["dbg_dt"],
               in_=DT.rearrange("p t i -> p (t i)")), waits=[(GP, GP_DC)], inc=OUT, k=16)
            em("sync", lambda e: e.dma_start(out=dbg_t["dbg_ct"],
               in_=CT.rearrange("p t i -> p (t i)")), waits=[(GP, GP_DC)], inc=OUT, k=16)

        Av = A[:].rearrange("p (t i j) -> p t i j", i=8, j=8)
        RSt = RS[:].rearrange("p (t i) -> p t i", i=8)
        CSt = CS[:].rearrange("p (t j) -> p t j", j=8)
        SH = 144
        def s_recip(buf):
            def f(e):
                e.activation(SLN[:, 0:512 - SH], buf[:, SH:512], AF.Ln)
                e.activation(buf[:, SH:512], SLN[:, 0:512 - SH], AF.Exp, scale=-1.0)
                return e.raw.nop()
            return f
        for it in range(5):
            em("vector", lambda e: e.tensor_reduce(
                RSt, Av.rearrange("p t i j -> p (t i) j"), axis=AX.X, op=ALU.add),
                waits=([(SE, SE_SIG)] + dbgw()) if it == 0 else [], inc=VE)
            v_rr = VE.n
            em("scalar", s_recip(RS[:]), waits=[(VE, v_rr)], inc=SE)
            s_rr = SE.n
            em("vector", lambda e: e.reciprocal(RS[:, 0:SH], RS[:, 0:SH]), inc=VE)
            em("vector", lambda e: e.tensor_tensor(
                Av.rearrange("p t i j -> p (t i) j"),
                Av.rearrange("p t i j -> p (t i) j"),
                RS[:][:, :, None].broadcast_to([128, 512, 8]), op=ALU.mult),
                waits=[(SE, s_rr)], inc=VE)
            em("vector", lambda e: e.tensor_reduce(
                CSt, Av.rearrange("p t i j -> p t j i"), axis=AX.X, op=ALU.add), inc=VE)
            v_cr = VE.n
            em("scalar", s_recip(CS[:]), waits=[(VE, v_cr)], inc=SE)
            s_cr = SE.n
            em("vector", lambda e: e.reciprocal(CS[:, 0:SH], CS[:, 0:SH]), inc=VE)
            em("vector", lambda e: e.tensor_tensor(
                Av, Av, CSt[:, :, None, :].broadcast_to([128, LCH, 8, 8]), op=ALU.mult),
                waits=[(SE, s_cr)], inc=VE)
        GP_SK = GP.n
        VE_SK = VE.n
        if dbg:
            em("sync", lambda e: e.dma_start(out=dbg_t["dbg_A"], in_=A[:]),
               waits=[(VE, VE_SK), (GP, GP_SK)], inc=OUT, k=16)

        TG = 56
        c_tg = TG * 8
        ng = LCH - TG
        SROW7 = carve(13824, 3584)
        GT1 = carve(17408, ng * 32, rr="p (t i j) -> p t i j", t=ng, i=4)
        GT2 = carve(17408 + ng * 32, ng * 16, rr="p (t i j) -> p t i j", t=ng, i=2)
        def grank(e):
            e.tensor_reduce(RMX[:], Av.rearrange("p t i j -> p (t i) j"), axis=AX.X, op=ALU.max)
            e.memset(RANK[:], 0.0)
            Rv = RMX[:].rearrange("p (t i) -> p t i", i=8)
            Kv = RANK[:].rearrange("p (t i) -> p t i", i=8)
            Ta = T512a[:].rearrange("p (t i) -> p t i", i=8)
            Tb = T512b[:].rearrange("p (t i) -> p t i", i=8)
            for dd in range(1, 8):
                e.tensor_tensor(Ta[:, :, 0:8 - dd], Rv[:, :, 0:8 - dd], Rv[:, :, dd:8],
                                op=ALU.is_ge)
                e.tensor_tensor(Kv[:, :, dd:8], Kv[:, :, dd:8], Ta[:, :, 0:8 - dd], op=ALU.add)
                e.tensor_tensor(Tb[:, :, 0:8 - dd], Rv[:, :, dd:8], Rv[:, :, 0:8 - dd],
                                op=ALU.is_gt)
                e.tensor_tensor(Kv[:, :, 0:8 - dd], Kv[:, :, 0:8 - dd], Tb[:, :, 0:8 - dd],
                                op=ALU.add)
            e.memset(CAV[:], 1.0)
            e.memset(AIDX[:], 0.0)
            return e.raw.engine_nop()
        em("vector", grank, waits=[(VE, VE_SK), (GP, GP_DC), (LDP, LDW)] + dbgw(), inc=VE)
        gm_done = {}
        for r in range(7):
            mbuf = (T512a, T512b)[r % 2]
            em("vector", lambda e, r=r, mbuf=mbuf: e.tensor_single_scalar(
                mbuf[:], RANK[:], float(r), op=ALU.is_equal),
                waits=[(GP, gm_done[r - 2])] if r >= 2 else [], inc=VE)
            v_mk = VE.n
            def gext(e, r=r, mbuf=mbuf):
                Hg = HARD4[:, TG:LCH]
                e.tensor_tensor(Hg.rearrange("p t i j -> p (t i) j"),
                                Av[:, TG:LCH].rearrange("p t i j -> p (t i) j"),
                                mbuf[:, c_tg:512][:, :, None].broadcast_to([128, ng * 8, 8]),
                                op=ALU.mult)
                e.tensor_tensor(GT1, Hg[:, :, 0:4, :], Hg[:, :, 4:8, :], op=ALU.add)
                e.tensor_tensor(GT2, GT1[:, :, 0:2, :], GT1[:, :, 2:4, :], op=ALU.add)
                e.tensor_tensor(SROW7[:, r * 512 + c_tg:(r + 1) * 512]
                                .rearrange("p (t j) -> p t j", j=8),
                                GT2[:, :, 0, :], GT2[:, :, 1, :], op=ALU.add)
                return e.raw.nop()
            em("gpsimd", gext, waits=[(VE, v_mk)], inc=GP)
            gm_done[r] = GP.n
            em("vector", lambda e, mbuf=mbuf: e.tensor_tensor(
                HARD4[:, 0:TG].rearrange("p t i j -> p (t i) j"),
                Av[:, 0:TG].rearrange("p t i j -> p (t i) j"),
                mbuf[:, 0:c_tg][:, :, None].broadcast_to([128, c_tg, 8]), op=ALU.mult), inc=VE)
            em("vector", lambda e, r=r: e.tensor_reduce(
                SROW7[:, r * 512:r * 512 + c_tg].rearrange("p (t j) -> p t j", j=8),
                HARD4[:, 0:TG].rearrange("p t i j -> p t j i"), axis=AX.X, op=ALU.max),
                inc=VE)
        def grounds(e):
            BIG = 1.0e30
            srow = SROW[:].rearrange("p (t j) -> p t j", j=8)
            cv = CV[:].rearrange("p (t j) -> p t j", j=8)
            Ta = T512a[:].rearrange("p (t i) -> p t i", i=8)
            Tb = T512b[:].rearrange("p (t i) -> p t i", i=8)
            for r in range(8):
                if r < 7:
                    e.tensor_single_scalar(T512a[:], RANK[:], float(r), op=ALU.is_equal)
                    e.tensor_scalar(CV[:], CAV[:], 1.0, BIG, op0=ALU.subtract, op1=ALU.mult)
                    e.tensor_add(CV[:], CV[:], SROW7[:, r * 512:(r + 1) * 512])
                    e.tensor_reduce(M64[:], cv, axis=AX.X, op=ALU.max)
                    e.tensor_tensor(Tb, cv, M64[:][:, :, None].broadcast_to([128, LCH, 8]),
                                    op=ALU.is_equal)
                else:
                    e.tensor_single_scalar(T512a[:], RANK[:], 7.0, op=ALU.is_equal)
                    e.tensor_copy(T512b[:], CAV[:])
                e.tensor_mul(SROW[:], T512b[:], IOTA[:])
                e.tensor_reduce(CIDX[:], srow, axis=AX.X, op=ALU.add)
                e.tensor_tensor(cv, Ta, CIDX[:][:, :, None].broadcast_to([128, LCH, 8]),
                                op=ALU.mult)
                e.tensor_add(AIDX[:], AIDX[:], CV[:])
                if r < 7:
                    e.tensor_sub(CAV[:], CAV[:], T512b[:])
            e.drain()
            for j in range(8):
                e.tensor_single_scalar(
                    AH9[:, :, :, j].rearrange("p t i -> p (t i)"),
                    AIDX[:], float(j), op=ALU.is_equal)
            e.drain()
            e.tensor_tensor(AH9[:, 0:TG, :, 0:8].rearrange("p t i j -> p (t i) j"),
                            AH9[:, 0:TG, :, 0:8].rearrange("p t i j -> p (t i) j"),
                            DT.rearrange("p t i -> p (t i)")[:, 0:c_tg][:, :, None]
                            .broadcast_to([128, c_tg, 8]),
                            op=ALU.mult)
            e.drain()
            return e.raw.engine_nop()
        em("vector", grounds, waits=[(GP, gm_done[6])], inc=VE)
        v_oh = VE.n
        em("gpsimd", lambda e: e.tensor_tensor(
            AH9[:, TG:LCH, :, 0:8].rearrange("p t i j -> p (t i) j"),
            AH9[:, TG:LCH, :, 0:8].rearrange("p t i j -> p (t i) j"),
            DT.rearrange("p t i -> p (t i)")[:, c_tg:512][:, :, None]
            .broadcast_to([128, ng * 8, 8]),
            op=ALU.mult), waits=[(VE, v_oh)], inc=GP)
        GP_GR = GP.n
        VE_GR = VE.n
        if dbg:
            em("sync", lambda e: e.dma_start(out=dbg_t["dbg_hard"],
               in_=AH9[:, :, :, 0:8].rearrange("p t i j -> p (t i) j")),
               waits=[(VE, VE_GR), (GP, GP_GR)], inc=OUT, k=16)
            em("sync", lambda e: e.dma_start(out=dbg_t["dbg_rank"], in_=RANK[:]),
               waits=[(VE, VE_GR), (GP, GP_GR)], inc=OUT, k=16)
            em("sync", lambda e: e.dma_start(out=dbg_t["dbg_aidx"], in_=AIDX[:]),
               waits=[(VE, VE_GR), (GP, GP_GR)], inc=OUT, k=16)
            em("sync", lambda e: e.dma_start(out=dbg_t["dbg_rmx"], in_=RMX[:]),
               waits=[(VE, VE_GR), (GP, GP_GR)], inc=OUT, k=16)

        def compose(e):
            e.tensor_copy(AH9[:, :, :, 8].rearrange("p t i -> p (t i)"),
                          CT.rearrange("p t i -> p (t i)"))
            e.memset(MALL9[:, :, 8, 0:8], 0.0)
            e.memset(MALL9[:, :, 8, 8], 1.0)
            e.tensor_copy(MALL9[:, 0, 0:8, :], AH9[:, 0, :, :])
            e.drain()
            for t in range(1, LCH):
                e.tensor_tensor(PRD9[:, 0:648].rearrange("p (i j k) -> p i j k", i=8, j=9),
                                AH9[:, t, :, :][:, :, :, None].broadcast_to([128, 8, 9, 9]),
                                MALL9[:, t - 1][:, None, :, :].broadcast_to([128, 8, 9, 9]),
                                op=ALU.mult)
                e.tensor_reduce(MALL9[:, t, 0:8, :],
                                PRD9[:, 0:648].rearrange("p (i j k) -> p i k j", i=8, j=9),
                                axis=AX.X, op=ALU.add)
            e.drain()
            e.tensor_copy(PFX[:], MALL9[:, LCH - 1, 0:8, :].rearrange("p i k -> p (i k)"))
            e.tensor_copy(PFM.rearrange("p (a j) -> p a j", a=8), MALL9[:, LCH - 1, 0:8, 0:8])
            return e.raw.engine_nop()
        em("vector", compose, waits=[(VE, VE_GR), (GP, GP_GR)] + dbgw(), inc=VE)
        VE_CMP = VE.n

        ve_prev = VE_CMP
        for lvl in range(5):
            dlt = 4 * (1 << lvl)
            em("gpsimd", lambda e, dlt=dlt: e.dma_start(out=PFS[dlt:128, :, :], in_=PFX[0:128 - dlt, :, :]),
               waits=[(VE, ve_prev)], inc=GSC, k=16)
            em("gpsimd", lambda e, dlt=dlt: e.dma_start(
                out=PFS[0:dlt, :, :].rearrange("p k i -> p (k i)"), in_=IDA[0:dlt, :]),
               inc=GSC, k=16)
            gd_sh = GSC.n
            def hslvl(e):
                e.tensor_tensor(PRD[:, 0:576].rearrange("p (x k) -> p x k", k=9),
                                PFM[:, :, None].broadcast_to([128, 64, 9]),
                                PFS[:].rearrange("p j k -> p (j k)")[:, None, :]
                                .broadcast_to([128, 8, 72]),
                                op=ALU.mult)
                e.tensor_reduce(NEWP[:],
                                PRD[:, 0:576].rearrange("p (a j k) -> p a k j", a=8, j=8),
                                axis=AX.X, op=ALU.add)
                e.tensor_add(NEWP[:, :, 8], NEWP[:, :, 8], PFX[:, :, 8])
                e.tensor_copy(PFX[:], NEWP[:])
                e.tensor_copy(PFM.rearrange("p (a j) -> p a j", a=8), NEWP[:, :, 0:8])
                return e.raw.engine_nop()
            em("vector", hslvl, waits=[(GSC, gd_sh)], inc=VE)
            ve_prev = VE.n
        VE_HS = VE.n

        em("vector", lambda e: e.tensor_copy(UCOL[:], PFX[:, :, 8]), waits=[(VE, VE_HS)], inc=VE)
        VE_UC = VE.n
        em("gpsimd", lambda e: e.dma_start(out=HSB[4:128, 0:8], in_=UCOL[0:124, :]),
           waits=[(VE, VE_UC)], inc=GSC, k=16)
        GD_HS = GSC.n
        def hszero(e):
            e.memset(HSB[0:4, 0:8], 0.0)
            e.memset(HSB[:, 8:9], 1.0)
            return e.raw.engine_nop()
        em("vector", hszero, waits=[(VE, VE_HS)], inc=VE)
        VE_HS0 = VE.n
        if dbg:
            em("sync", lambda e: e.dma_start(out=dbg_t["dbg_mall"],
               in_=MALL9[:, :, 0:8, :].rearrange("p t i k -> p t (i k)")),
               waits=[(VE, VE_CMP)], inc=OUT, k=16)
            em("sync", lambda e: e.dma_start(out=dbg_t["dbg_hs"], in_=HSB[:, 0:8]),
               waits=[(GSC, GD_HS), (VE, VE_HS0)], inc=OUT, k=16)

        def recon(e):
            e.tensor_tensor(PRDR[:, 0:4608].rearrange("p (t i k) -> p t i k", t=LCH, i=NN),
                            MALL9[:, :, 0:8, :],
                            HSB[:][:, None, None, :].broadcast_to([128, LCH, 8, 9]),
                            op=ALU.mult)
            e.tensor_reduce(ACCR[:],
                            PRDR[:, 0:4608].rearrange("p (t i k) -> p t i k", t=LCH, i=NN),
                            axis=AX.X, op=ALU.add)
            e.tensor_copy(YS, ACCR[:].rearrange("p (t i) -> p i t", i=NN))
            return e.raw.engine_nop()
        em("vector", recon, waits=[(GSC, GD_HS), (VE, VE_HS0)] + dbgw(), inc=VE)
        VE_YS = VE.n
        if dbg:
            em("sync", lambda e: e.dma_start(out=dbg_t["dbg_ys"],
               in_=YS.rearrange("p i t -> p (i t)")), waits=[(VE, VE_YS)], inc=OUT, k=16)

        em("tensor", lambda e: e.matmul(
            B[5][:, 0:512], PERMB[:], YS.rearrange("p i t -> p (i t)"), start=True, stop=True),
            waits=[(VE, VE_YS), (GDK, GDC)], inc=TE)
        te_pm = TE.n
        em("vector", lambda e: e.raw.tensor_copy(YS2.rearrange("p i t -> p (i t)"), B[5][:, 0:512]),
           waits=[(TE, te_pm)], inc=VE)
        VE_YS2 = VE.n
        for h in range(4):
            for i in range(NN):
                qeng = "gpsimd" if h < 2 else "sync"
                qsem = GSC if h < 2 else SPJ[0]
                em(qeng, lambda e, h=h, i=i: e.dma_start(
                    out=YST[h * 8 + i: h * 8 + i + 1, :].rearrange("q (c t) -> q c t", c=32),
                    in_=YS2[h * 32:(h + 1) * 32, i, :]),
                    waits=([(VE, VE_YS2)] + dbgw()) if i == 0 and h in (0, 2) else (),
                    inc=qsem, k=16)
        GD_YST = (GSC.n, SPJ[0].n)

        VE_RES = {}
        TE_TR3 = {}
        VE_CWT_T = {}
        out2_tile = {}
        RT7 = carve(23040, 64); RE7 = carve(23104, 64); ROH7 = carve(23168, 64)
        RPW7 = carve(23232, 64); RCW7 = carve(23296, 64)
        CWB7 = carve(23360, 32, dt=BF16)
        RM4 = carve(23392, 4); CI4 = carve(23396, 4)
        SQDMP = carve(1024, 1024)
        ve_bank_free = {i: VE.n for i in range(8)}
        ve_cwt_prev = VE.n
        ev56 = {5: (VE, 0), 6: (VE, 0)}
        te_slot_free_gu = {0: TE.n, 1: TE.n}
        te_slot_free_d = {0: TE.n, 1: TE.n}
        gu_loads = {}; gu_loads_u = {}
        gi_counter = 0
        scg_w = {0: 0, 1: 0}; scd_w = {0: 0, 1: 0}
        scg_done = {}; scd_done = {}

        def wsrcs(kind, idx):
            if kind == "e":
                return (d_wg[idx].rearrange("(k p) f -> p k f", p=128),
                        d_wu[idx].rearrange("(k p) f -> p k f", p=128),
                        d_wd[idx].rearrange("(k p) f -> p k f", p=128))
            hf = idx
            return (d_sg[:, hf * 512:(hf + 1) * 512].rearrange("(k p) f -> p k f", p=128),
                    d_su[:, hf * 512:(hf + 1) * 512].rearrange("(k p) f -> p k f", p=128),
                    d_sd[hf * 512:(hf + 1) * 512, :].rearrange("(k p) f -> p k f", p=128))
        groups = [[("sh", 0)] + [("e", i) for i in range(8)],
                  [("sh", 1)] + [("e", i) for i in range(8, 16)]]

        for s in range(NSL):
            pb0, pb1 = (B[2], B[3]) if s % 2 == 0 else (B[5], B[6])
            em("tensor", lambda e, s=s, pb0=pb0: e.matmul(
                pb0[:, 0:512], YST[:, s * 128:(s + 1) * 128], OWB[:, 0:512], start=True, stop=False),
                waits=[(GSC, GD_YST[0]), (SPJ[0], GD_YST[1]), (GDK, GDC)] if s == 0 else
                      ([(VE, VE_RES[s - 2])] if s >= 2 else []))
            em("tensor", lambda e, pb0=pb0: e.matmul(
                pb0[:, 0:512], ONES[0:1, :], OBB[0:1, 0:512], start=False, stop=True))
            em("tensor", lambda e, s=s, pb1=pb1: e.matmul(
                pb1[:, 0:512], YST[:, s * 128:(s + 1) * 128], OWB[:, 512:1024], start=True, stop=False))
            em("tensor", lambda e, pb1=pb1: e.matmul(
                pb1[:, 0:512], ONES[0:1, :], OBB[0:1, 512:1024], start=False, stop=True), inc=TE)
            te_o = TE.n
            def resid(e, s=s, pb0=pb0, pb1=pb1):
                for nh in range(2):
                    pb = pb0 if nh == 0 else pb1
                    e.scalar_tensor_tensor(SQS[:, nh * 512:(nh + 1) * 512],
                                           XT[:, s, nh * 512:(nh + 1) * 512], ST[:, iA1, s:s + 1],
                                           pb[:, 0:512], op0=ALU.mult, op1=ALU.add)
                    e.tensor_scalar(XT[:, s, nh * 512:(nh + 1) * 512],
                                    SQS[:, nh * 512:(nh + 1) * 512], ST[:, iNM1, s:s + 1], None,
                                    op0=ALU.add)
                return e.raw.engine_nop()
            em("vector", resid, waits=[(TE, te_o)] + (dbgw() if s == 0 else []), inc=VE)
            VE_RES[s] = VE.n
            def ln3s(e, s=s):
                e.activation(SQDMP[:], XT[:, s, :], AF.Identity,
                             accum_out=ST[:, iS3, s:s + 1])
                e.activation(ST[:, iNM3, s:s + 1], ST[:, iS3, s:s + 1], AF.Identity,
                             scale=-inv_d)
                e.activation(SQDMP[:], XT[:, s, :], AF.Square,
                             bias=ST[:, iNM3, s:s + 1], scale=1.0,
                             accum_out=ST[:, iSQ3, s:s + 1])
                return e.raw.nop()
            em("scalar", ln3s, waits=[(VE, VE_RES[s])], inc=SE)
        SE_SQ3 = SE.n
        ve_bank_free = {i: VE.n for i in range(8)}
        if dbg:
            em("sync", lambda e: e.dma_start(out=dbg_t["dbg_xattn"], in_=XT[:]),
               waits=[(VE, VE_RES[NSL - 1])], inc=OUT, k=16)

        def p7_tile(tt):
            nonlocal ve_cwt_prev
            Y2B7 = Y2BF
            rb = (tt % 2) * 64
            if tt == 0:
                te_slot_free_gu[0] = te_slot_free_gu[1] = TE.n
                te_slot_free_d[0] = te_slot_free_d[1] = TE.n

            for si in range(4):
                s = tt * 4 + si
                em("scalar", lambda e, s=s, Y2B7=Y2B7: e.raw.activation(
                    Y2B7[:, s % 2, :], XT[:, s, :], AF.Identity,
                    bias=ST[:, iNI3, s:s + 1], scale=ST[:, iIS3, s:s + 1]),
                    waits=[(VE, VE_ST3)] + ([(TE, TE_TR3[s - 2])] if s >= 2 else []) + (dbgw() if s == 0 else []), inc=SE)
                se_y = SE.n
                for kc in range(8):
                    pb = B[5 + (kc % 2)]
                    tw = ([(SE, se_y)] if kc == 0 else [])
                    if kc >= 2 or s > 0:
                        tw = tw + [ev56[5 + (kc % 2)]]
                    em("tensor", lambda e, s=s, kc=kc, pb=pb, Y2B7=Y2B7: e.transpose(
                        pb[:].bitcast(BF16)[:, 0:128], Y2B7[:, s % 2, kc * 128:(kc + 1) * 128], IDT[:]),
                        waits=tw, inc=TE)
                    te_tr = TE.n
                    if kc % 2 == 0 or kc == 7:
                        em("vector", lambda e, s=s, kc=kc, pb=pb: e.raw.tensor_copy(
                            XNF[:, kc, s * 128:(s + 1) * 128], pb[:].bitcast(BF16)[:, 0:128]),
                            waits=[(TE, te_tr)], inc=VE)
                    else:
                        em("scalar", lambda e, s=s, kc=kc, pb=pb: e.raw.copy(
                            XNF[:, kc, s * 128:(s + 1) * 128], pb[:].bitcast(BF16)[:, 0:128]),
                            waits=[(TE, te_tr)], inc=SE)
                    ev56[5 + (kc % 2)] = (VE, VE.n) if (kc % 2 == 0 or kc == 7) else (SE, SE.n)
                TE_TR3[s] = TE.n
                ve_tr = VE.n
                se_tr = SE.n
                ve_bank_free[5] = VE.n
                ve_bank_free[6] = VE.n
                for kc in range(8):
                    em("tensor", lambda e, s=s, si=si, kc=kc, rb=rb: e.matmul(
                        B[4][:, rb + si * 16:rb + (si + 1) * 16],
                        XNF[:, kc, s * 128:(s + 1) * 128], WRB[:, kc, :],
                        start=(kc == 0), stop=(kc == 7)),
                        waits=(([(VE, ve_tr), (SE, se_tr)] +
                                ([(VE, r1done[tt - 2])] if si == 0 and tt >= 2 else []))
                               if kc == 0 else ()),
                        inc=TE if kc == 7 else None)
            te_rt_d[tt] = TE.n
        def p7_b(tt):
            nonlocal ve_cwt_prev
            rb = (tt % 2) * 64
            def rsoft1(e, rb=rb):
                e.tensor_reduce(RM4[:], B[4][:, rb:rb + 64].rearrange("p (s x) -> p s x", x=16),
                                axis=AX.X, op=ALU.max)
                e.tensor_tensor(RT7[:].rearrange("p (s x) -> p s x", x=16),
                                B[4][:, rb:rb + 64].rearrange("p (s x) -> p s x", x=16),
                                RM4[:][:, :, None].broadcast_to([128, 4, 16]),
                                op=ALU.subtract)
                return e.raw.engine_nop()
            em("vector", rsoft1, waits=[(TE, te_rt_d[tt])], inc=VE)
            r1done[tt] = VE.n
            em("scalar", lambda e: e.activation(RE7[:], RT7[:], AF.Exp),
               waits=[(VE, VE.n)], inc=SE)
            SE_REXP = SE.n
            def rsoft2(e):
                ev = lambda a: a[:].rearrange("p (s x) -> p s x", x=16)
                bc = lambda m: m[:][:, :, None].broadcast_to([128, 4, 16])
                e.tensor_reduce(CI4[:], ev(RE7), axis=AX.X, op=ALU.add)
                e.reciprocal(CI4[:], CI4[:])
                e.tensor_tensor(ev(RE7), ev(RE7), bc(CI4), op=ALU.mult)
                e.tensor_reduce(RM4[:], ev(RE7), axis=AX.X, op=ALU.max)
                e.tensor_tensor(ev(ROH7), ev(RE7), bc(RM4), op=ALU.is_equal)
                e.tensor_mul(RPW7[:], RE7[:], ROH7[:])
                e.tensor_sub(RPW7[:], RE7[:], RPW7[:])
                e.tensor_reduce(RM4[:], ev(RPW7), axis=AX.X, op=ALU.max)
                e.tensor_tensor(ev(RCW7), ev(RPW7), bc(RM4), op=ALU.is_equal)
                e.tensor_add(ROH7[:], ROH7[:], RCW7[:])
                e.tensor_mul(RCW7[:], RE7[:], ROH7[:])
                return e.raw.engine_nop()
            em("vector", rsoft2, waits=[(SE, SE_REXP)], inc=VE)
            ve_bank_free[4] = VE.n
            em("scalar", lambda e: e.raw.copy(CWB7[:], RCW7[:]), waits=[(VE, VE.n)], inc=SE)
            SE_CWB = SE.n
            for si in range(4):
                s = tt * 4 + si
                em("tensor", lambda e, si=si: e.transpose(
                    B[7][:].bitcast(BF16)[0:NEXP, 0:128], CWB7[:, si * 16:(si + 1) * 16], IDT[:]),
                    waits=[(SE, SE_CWB), (VE, ve_cwt_prev)] +
                          ([(VE, ve_bank_free[7])] if si == 0 else []), inc=TE)
                te_cw = TE.n
                em("vector", lambda e, s=s: e.raw.tensor_copy(
                    CWT[0:NEXP, s * 128:(s + 1) * 128],
                    B[7][:].bitcast(BF16)[0:NEXP, 0:128]), waits=[(TE, te_cw)], inc=VE)
                ve_cwt_prev = VE.n
            ve_bank_free[7] = VE.n
            VE_CWT_T[tt] = VE.n

        def ffn_tile(tt):
            nonlocal gi_counter
            tok = slice(tt * 512, (tt + 1) * 512)
            for gidx, grp in enumerate(groups):
                for ei, (kind, idx) in enumerate(grp):
                    gslot = gi_counter % 2
                    wg_s, wu_s, wd_s = wsrcs(kind, idx)
                    s_idx = gidx * 9 + ei
                    if tt == 0:
                        em("gpsimd", lambda e, wg_s=wg_s, gslot=gslot: e.dma_start(
                            out=WGr[:, gslot, :, :], in_=wg_s),
                            waits=[(TE, te_slot_free_gu[gslot]),
                                   (SCG[gslot], scg_w[gslot])] +
                                  (dbgw() if gi_counter == 0 else []),
                            inc=FWG[gslot][0], k=16)
                        em("gpsimd", lambda e, wu_s=wu_s, gslot=gslot: e.dma_start(
                            out=WUr[:, gslot, :, :], in_=wu_s), inc=FWG[gslot][0], k=16)
                        for hf_ in range(2):
                            gu_loads[(gi_counter, hf_)] = (FWG[gslot][0], FWG[gslot][0].n)
                            gu_loads_u[(gi_counter, hf_)] = gu_loads[(gi_counter, hf_)]
                        em("sync", lambda e, s_idx=s_idx, gslot=gslot: e.dma_start(
                            out=d_scrG[s_idx], in_=WGr[:, gslot, :, :]),
                            waits=[gu_loads[(gi_counter, 0)], gu_loads[(gi_counter, 1)]],
                            inc=SCG[gslot], k=16)
                        em("sync", lambda e, s_idx=s_idx, gslot=gslot: e.dma_start(
                            out=d_scrU[s_idx], in_=WUr[:, gslot, :, :]), inc=SCG[gslot], k=16)
                        scg_w[gslot] = SCG[gslot].n
                        scg_done[s_idx] = (SCG[gslot], SCG[gslot].n)
                    else:
                        for hf_ in range(2):
                            em("sync", lambda e, s_idx=s_idx, gslot=gslot, hf_=hf_: e.dma_start(
                                out=WGr[:, gslot, :, hf_ * 256:(hf_ + 1) * 256],
                                in_=d_scrG[s_idx][:, :, hf_ * 256:(hf_ + 1) * 256]),
                                waits=([(TE, te_slot_free_gu[gslot])] +
                                       ([(SCG[gslot], scg_w[gslot])] if tt == 1 else [])) if hf_ == 0 else (),
                                inc=FWGS[gslot][hf_], k=16)
                            em("sync", lambda e, s_idx=s_idx, gslot=gslot, hf_=hf_: e.dma_start(
                                out=WUr[:, gslot, :, hf_ * 256:(hf_ + 1) * 256],
                                in_=d_scrU[s_idx][:, :, hf_ * 256:(hf_ + 1) * 256]),
                                inc=FWGS[gslot][hf_], k=16)
                            gu_loads[(gi_counter, hf_)] = (FWGS[gslot][hf_], FWGS[gslot][hf_].n)
                            gu_loads_u[(gi_counter, hf_)] = gu_loads[(gi_counter, hf_)]
                    has_cw = (kind == "e")
                    bcb = 4 if gi_counter % 2 == 0 else 7
                    if has_cw:
                        em("tensor", lambda e, idx=idx, tok=tok, bcb=bcb: e.matmul(
                            B[bcb][:, 0:512], ONEH[:, idx * 128:(idx + 1) * 128], CWT[0:NEXP, tok],
                            start=True, stop=True),
                            waits=[(VE, VE_CWT_T[tt]), (VE, ve_bank_free[bcb])], inc=TE)
                    te_bc = TE.n
                    for mc in range(4):
                        rr = (gi_counter * 4 + mc) % 3
                        gb = (0, 1, 5)[rr]
                        ub = (2, 3, 6)[rr]
                        for kc in range(8):
                            em("tensor", lambda e, gslot=gslot, kc=kc, mc=mc, tok=tok, gb=gb: e.matmul(
                                B[gb][:, 0:512], WGr[:, gslot, kc, mc * 128:(mc + 1) * 128],
                                XNF[:, kc, tok], start=(kc == 0), stop=(kc == 7)),
                                waits=(([gu_loads[(gi_counter, mc // 2)],
                                         (VE, ve_bank_free[gb])] +
                                        ([(VE, VE_CWT_T[tt])] if ei == 0 and mc == 0 else []))
                                       if kc == 0 else ()),
                                inc=TE if kc == 7 else None)
                        te_g = TE.n
                        for kc in range(8):
                            em("tensor", lambda e, gslot=gslot, kc=kc, mc=mc, tok=tok, ub=ub: e.matmul(
                                B[ub][:, 0:512], WUr[:, gslot, kc, mc * 128:(mc + 1) * 128],
                                XNF[:, kc, tok], start=(kc == 0), stop=(kc == 7)),
                                waits=[(VE, ve_bank_free[ub])] if kc == 0 else (),
                                inc=TE if kc == 7 else None)
                        te_u = TE.n
                        em("scalar", lambda e, rr=rr, gb=gb: e.raw.activation(
                            SILU[:, rr, :], B[gb][:, 0:512], AF.Silu),
                            waits=[(TE, te_g)], inc=SE)
                        se_si = SE.n
                        def gevac(e, mc=mc, ub=ub, ei=ei, has_cw=has_cw, bcb=bcb, rr=rr):
                            sc = SILU[:, rr, :]
                            e.tensor_tensor(sc, sc, B[ub][:, 0:512], op=ALU.mult)
                            if has_cw:
                                e.raw.tensor_tensor(GATED[:, ei, mc, :], sc, B[bcb][:, 0:512],
                                                    op=ALU.mult)
                            else:
                                e.raw.tensor_copy(GATED[:, ei, mc, :], sc)
                            return e.raw.engine_nop()
                        em("vector", gevac, waits=[(SE, se_si), (TE, te_u)] +
                           ([(TE, te_bc)] if has_cw and mc == 0 else []) +
                           (dbgw() if gi_counter == 0 and mc == 0 else []), inc=VE)
                        ve_bank_free[gb] = VE.n
                        ve_bank_free[ub] = VE.n
                        if has_cw:
                            ve_bank_free[bcb] = VE.n
                    te_slot_free_gu[gslot] = TE.n
                    gi_counter += 1
                ve_gated = VE.n
                for ei, (kind, idx) in enumerate(grp):
                    dslot = ei % 2
                    wg_s, wu_s, wd_s = wsrcs(kind, idx)
                    s_idx = gidx * 9 + ei
                    if tt == 0:
                        em("gpsimd", lambda e, wd_s=wd_s, dslot=dslot: e.dma_start(
                            out=WDr[:, dslot, :, :], in_=wd_s),
                            waits=[(TE, te_slot_free_d[dslot]), (SCD[dslot], scd_w[dslot])],
                            inc=FWD[dslot], k=16)
                        gd_wd = (FWD[dslot], FWD[dslot].n)
                        em("sync", lambda e, s_idx=s_idx, dslot=dslot: e.dma_start(
                            out=d_scrD[s_idx], in_=WDr[:, dslot, :, :]),
                            waits=[gd_wd], inc=SCD[dslot], k=16)
                        scd_w[dslot] = SCD[dslot].n
                        scd_done[s_idx] = (SCD[dslot], SCD[dslot].n)
                    else:
                        em("sync", lambda e, s_idx=s_idx, dslot=dslot: e.dma_start(
                            out=WDr[:, dslot, :, :], in_=d_scrD[s_idx]),
                            waits=[(TE, te_slot_free_d[dslot])] +
                                  ([(SCD[dslot], scd_w[dslot])] if tt == 1 else []),
                            inc=FWDS[dslot], k=16)
                        gd_wd = (FWDS[dslot], FWDS[dslot].n)
                    for sl in range(4):
                        for nh in range(2):
                            bank = sl * 2 + nh
                            for kc in range(4):
                                last = (sl == 3 and nh == 1 and kc == 3)
                                em("tensor", lambda e, ei=ei, sl=sl, nh=nh, kc=kc, dslot=dslot, bank=bank: e.matmul(
                                    B[bank][:, 0:512],
                                    GATED[:, ei, kc, sl * 128:(sl + 1) * 128],
                                    WDr[:, dslot, kc, nh * 512:(nh + 1) * 512],
                                    start=(ei == 0 and kc == 0), stop=(ei == 8 and kc == 3)),
                                    waits=(([gd_wd, (VE, ve_gated)] if sl == 0 and nh == 0 else []) +
                                           ([(VE, ve_bank_free[bank])] if ei == 0 else []))
                                    if kc == 0 else (),
                                    inc=TE if last else None)
                    te_slot_free_d[dslot] = TE.n
                te_dn = TE.n
                for bank in (0, 1, 5, 2, 3, 6, 4, 7):
                    sl, nh = bank // 2, bank % 2
                    s_ = tt * 4 + sl
                    em("vector", lambda e, s_=s_, nh=nh, bank=bank: e.raw.tensor_tensor(
                        XT[:, s_, nh * 512:(nh + 1) * 512], B[bank][:, 0:512],
                        XT[:, s_, nh * 512:(nh + 1) * 512], op=ALU.add),
                        waits=(([(TE, te_dn)] + dbgw()) if bank == 0 else ()), inc=VE)
                    ve_bank_free[bank] = VE.n
                if gidx == 1:
                    ve_fin = VE.n
                    for sl in range(4):
                        s = tt * 4 + sl
                        em("gpsimd", lambda e, s=s: e.dma_start(
                            out=d_y.rearrange("(s p) d -> p s d", p=128)[:, s, :], in_=XT[:, s, :]),
                            waits=[(VE, ve_fin)] if sl == 0 else (), inc=OUT2, k=16)
                    out2_tile[tt] = OUT2.n

        def stats4(e):
            e.tensor_scalar(ST[:, iIS3, :], ST[:, iSQ3, :], inv_d, EPS, op0=ALU.mult, op1=ALU.add)
            e.drain()
            return e.engine_nop()
        em("vector", stats4, waits=[(SE, SE_SQ3)], inc=VE)
        em("scalar", lambda e: e.sqrt(ST[:, iIS3, :], ST[:, iIS3, :]), waits=[(VE, VE.n)], inc=SE)
        def stats5(e):
            e.reciprocal(ST[:, iIS3, :], ST[:, iIS3, :])
            e.drain()
            e.tensor_mul(ST[:, iNI3, :], ST[:, iNM3, :], ST[:, iIS3, :])
            e.drain()
            return e.engine_nop()
        em("vector", stats5, waits=[(SE, SE.n)], inc=VE)
        VE_ST3 = VE.n
        te_rt_d = {}
        r1done = {}
        for tt in range(NTILE):
            p7_tile(tt)
            if tt >= 1:
                p7_b(tt - 1)
        p7_b(NTILE - 1)
        for tt in range(NTILE):
            ffn_tile(tt)

        if dbg:
            em("sync", lambda e: e.dma_start(out=dbg_t["dbg_xnf3"], in_=XNF[:]),
               waits=[(VE, VE_CWT_T[NTILE - 1])], inc=OUT, k=16)
            em("sync", lambda e: e.dma_start(out=dbg_t["dbg_cwt"], in_=CWT[:]),
               waits=[(VE, VE_CWT_T[NTILE - 1])], inc=OUT, k=16)

        def _fin(e):
            e.wait_ge(OUT.h, OUT.n)
            e.wait_ge(OUT2.h, OUT2.n)
        P.ops["sync"].append(_fin)

        for eng_name, deco in (("sync", blk.sync), ("gpsimd", blk.gpsimd), ("tensor", blk.tensor),
                               ("vector", blk.vector), ("scalar", blk.scalar)):
            ops = P.ops[eng_name]
            wrap = eng_name in ("vector", "scalar", "gpsimd")
            def make(ops=ops, wrap=wrap):
                def body(e):
                    if wrap:
                        e = _DrainAfter(e)
                    for op in ops:
                        op(e)
                return body
            deco(make())

    return nc, dbg_t


_NC_CACHE = {}

def _get_nc(dbg=0):
    if dbg not in _NC_CACHE:
        _NC_CACHE[dbg] = build(dbg)
    return _NC_CACHE[dbg]


def kernel(**inputs):
    nc, _ = _get_nc(0)
    hc = _host_consts()
    x = np.ascontiguousarray(inputs["L0_attn_x"])
    base = {
        "ln_w": inputs["L0_attn_ln_w"], "ln_b": inputs["L0_attn_ln_b"],
        "sign_W": inputs["L0_attn_sign_W"], "sign_b": inputs["L0_attn_sign_b"],
        "perm_W": inputs["L0_attn_perm_W"], "perm_b": inputs["L0_attn_perm_b"],
        "gate_W": inputs["L0_attn_gate_W"], "gate_b": inputs["L0_attn_gate_b"],
        "in_W": inputs["L0_attn_in_W"], "in_b": inputs["L0_attn_in_b"],
        "out_W": inputs["L0_attn_out_W"], "out_b": inputs["L0_attn_out_b"],
        "Wr": inputs["L0_ffn_Wr"], "Wg": inputs["L0_ffn_Wg"], "Wu": inputs["L0_ffn_Wu"],
        "Wd": inputs["L0_ffn_Wd"], "Sg": inputs["L0_ffn_Sg"], "Su": inputs["L0_ffn_Su"],
        "Sd": inputs["L0_ffn_Sd"],
    }
    base = {k: np.ascontiguousarray(v) for k, v in base.items()}
    base.update(hc)
    in_maps = [dict(base, x=np.ascontiguousarray(x[b])) for b in range(8)]
    res = run_bass_kernel_spmd(nc, in_maps, core_ids=list(range(8)))
    out = np.stack([res.results[b]["y"] for b in range(8)], axis=0)
    return out.astype(np.float32)


# version 28
# speedup vs baseline: 1.0001x; 1.0001x over previous
import numpy as np
from contextlib import ExitStack

import concourse.bass as bass
import concourse.mybir as mybir
from concourse.bass_utils import run_bass_kernel_spmd

F32 = mybir.dt.float32
BF16 = mybir.dt.bfloat16
AF = mybir.ActivationFunctionType
ALU = mybir.AluOpType
AX = mybir.AxisListType

S, D, H, NN = 2048, 1024, 4, 8
NSL = S // 128
NCH, LCH = 32, 64
EPS = 1e-5
NTILE = 4
NEXP = 16
FEXP = 512

_HOST_CONST = {}
def _host_consts():
    if _HOST_CONST:
        return _HOST_CONST
    _HOST_CONST["c_ident"] = np.eye(128, dtype=np.float32)
    _HOST_CONST["c_ones"] = np.ones((1, 128), dtype=np.float32)
    _HOST_CONST["c_iota"] = np.broadcast_to(
        np.tile(np.arange(8, dtype=np.float32), 64), (128, 512)).copy()
    ida = np.zeros((8, 9), dtype=np.float32)
    ida[:, :8] = np.eye(8, dtype=np.float32)
    _HOST_CONST["c_idaug"] = np.broadcast_to(ida.reshape(-1), (128, 72)).copy()
    oneh = np.zeros((16, 2048), dtype=np.float32)
    for e_ in range(16):
        oneh[e_, e_ * 128:(e_ + 1) * 128] = 1.0
    _HOST_CONST["c_oneh"] = oneh
    pm = np.zeros((128, 128), dtype=np.float32)
    for m in range(128):
        h_, c_ = m // 32, m % 32
        pm[c_ * 4 + h_, m] = 1.0
    _HOST_CONST["c_perm"] = pm
    return _HOST_CONST


class _Sem:
    def __init__(self, name):
        self.name = name
        self.n = 0
        self.h = None


_DRAIN_OPS = {
    "tensor_tensor", "tensor_add", "tensor_sub", "tensor_mul", "tensor_max",
    "tensor_copy", "tensor_scalar", "tensor_single_scalar", "tensor_scalar_add",
    "tensor_scalar_sub", "tensor_scalar_mul", "tensor_scalar_max", "tensor_scalar_min",
    "tensor_relu", "tensor_reduce", "reduce_max", "reduce_sum", "scalar_tensor_tensor",
    "tensor_tensor_scan", "memset", "memzero", "reciprocal", "select", "transpose",
    "activation", "sign", "square", "sqrt", "mul", "copy", "add",
}


class _DrainAfter:
    def __init__(self, e):
        self._e = e

    @property
    def raw(self):
        return self._e

    def __getattr__(self, k):
        f = getattr(self._e, k)
        if k in _DRAIN_OPS:
            def g(*a, **kw):
                r = f(*a, **kw)
                self._e.drain()
                return r
            return g
        return f


class _Prog:
    def __init__(self):
        self.ops = {k: [] for k in ("sync", "gpsimd", "tensor", "vector", "scalar")}

    def emit(self, eng, fn, waits=(), inc=None, k=1):
        waits = tuple((s, v) for (s, v) in waits)
        def cl(e, fn=fn, waits=waits, inc=inc, k=k):
            for (s, v) in waits:
                e.wait_ge(s.h, v)
            r = fn(e)
            if inc is not None:
                r.then_inc(inc.h, k)
        self.ops[eng].append(cl)
        if inc is not None:
            inc.n += k
        return inc.n if inc is not None else None


def build(dbg=0):
    nc = bass.Bass("TRN2", target_bir_lowering=False)
    I = lambda name, shape: nc.dram_tensor(name, list(shape), F32, kind="ExternalInput").ap()

    d_x = I("x", (S, D))
    d_lnw = I("ln_w", (D,)); d_lnb = I("ln_b", (D,))
    d_signW = I("sign_W", (D, 32)); d_signb = I("sign_b", (32,))
    d_permW = I("perm_W", (D, 256)); d_permb = I("perm_b", (256,))
    d_gateW = I("gate_W", (D, 32)); d_gateb = I("gate_b", (32,))
    d_inW = I("in_W", (D, 32)); d_inb = I("in_b", (32,))
    d_outW = I("out_W", (32, D)); d_outb = I("out_b", (D,))
    d_wr = I("Wr", (D, NEXP))
    d_wg = I("Wg", (NEXP, D, FEXP)); d_wu = I("Wu", (NEXP, D, FEXP)); d_wd = I("Wd", (NEXP, FEXP, D))
    d_sg = I("Sg", (D, D)); d_su = I("Su", (D, D)); d_sd = I("Sd", (D, D))
    d_ident = I("c_ident", (128, 128)); d_ones = I("c_ones", (1, 128))
    d_iota = I("c_iota", (128, 512)); d_idaug = I("c_idaug", (128, 72))
    d_oneh = I("c_oneh", (16, 2048))
    d_perm = I("c_perm", (128, 128))

    d_y = nc.dram_tensor("y", [S, D], F32, kind="ExternalOutput").ap()
    d_scrG = nc.dram_tensor("scrG", [18, 128, 8, FEXP], BF16, kind="Internal").ap()
    d_scrU = nc.dram_tensor("scrU", [18, 128, 8, FEXP], BF16, kind="Internal").ap()
    d_scrD = nc.dram_tensor("scrD", [18, 128, 4, D], BF16, kind="Internal").ap()
    dbg_t = {}
    if dbg:
        def DBG(name, shape, dt=F32):
            dbg_t[name] = nc.dram_tensor(name, list(shape), dt, kind="ExternalOutput").ap()
        DBG("dbg_xnf", (128, 8, S), BF16)
        DBG("dbg_projC", (128, LCH, 88))
        DBG("dbg_A", (128, 4096))
        DBG("dbg_hard", (128, 4096))
        DBG("dbg_dt", (128, 512)); DBG("dbg_ct", (128, 512))
        DBG("dbg_mall", (128, 9 * LCH * 8))
        DBG("dbg_hs", (128, 8))
        DBG("dbg_ys", (128, 512), BF16)
        DBG("dbg_xattn", (128, NSL, D))
        DBG("dbg_xnf3", (128, 8, S), BF16)
        DBG("dbg_cwt", (NEXP, S), BF16)
        DBG("dbg_rank", (128, 512)); DBG("dbg_aidx", (128, 512)); DBG("dbg_rmx", (128, 512))

    P = _Prog()
    LDX = _Sem("LDX"); LDP = _Sem("LDP"); GDK = _Sem("GDK")
    GPJ = [_Sem("GPJ0"), _Sem("GPJ1")]; SPJ = [_Sem("SPJ0"), _Sem("SPJ1")]; GSC = _Sem("GSC")
    FWG = [[_Sem("FWG0a"), _Sem("FWG0b")], [_Sem("FWG1a"), _Sem("FWG1b")]]
    FWD = [_Sem("FWD0"), _Sem("FWD1")]
    SCG = [_Sem("SCG0"), _Sem("SCG1")]; SCD = [_Sem("SCD0"), _Sem("SCD1")]
    FWGS = [[_Sem("FWGS0a"), _Sem("FWGS0b")], [_Sem("FWGS1a"), _Sem("FWGS1b")]]
    FWDS = [_Sem("FWDS0"), _Sem("FWDS1")]
    TE = _Sem("TE"); VE = _Sem("VE"); SE = _Sem("SE"); OUT = _Sem("OUT"); OUT2 = _Sem("OUT2")
    GP = _Sem("GP")
    sems = ([LDX, LDP, GDK] + GPJ + SPJ + [GSC] + FWG[0] + FWG[1] + FWD + SCG + SCD +
            FWGS[0] + FWGS[1] + FWDS + [TE, VE, SE, OUT, OUT2, GP])

    with ExitStack() as ctx:
        def SB(name, shape, dt=F32):
            return ctx.enter_context(nc.sbuf_tensor(name, shape, dt))
        XT = SB("XT", [128, NSL, D])
        XNF = SB("XNF", [128, 8, S], BF16)
        ARN = 24064
        ARH = SB("ARENA", [128, ARN])
        AR = ARH[:]
        def carve(off, nw, shape=None, dt=F32, rr=None, **kw):
            v = AR[:, off:off + nw]
            if dt is not F32:
                v = v.bitcast(dt)
            if rr is not None:
                v = v.rearrange(rr, **kw)
            return v
        A = carve(0, 4096)
        SQS = carve(0, 1024)
        HARD4 = carve(4096, 4096, rr="p (t i j) -> p t i j", t=LCH, i=NN)
        HARD = carve(4096, 4096)
        WP = carve(4096, 1408, dt=BF16, rr="p (k w) -> p k w", k=8)
        Y2BF = carve(4096 + 1408, 1024, dt=BF16, rr="p (a d) -> p a d", a=2)
        MALL = carve(8192, 4608, rr="p (t i k) -> p t i k", t=LCH, i=NN)
        WPF = carve(8192, 2816, rr="p (k w) -> p k w", k=8)
        DT = carve(12800, 512, rr="p (t i) -> p t i", t=LCH)
        CT = carve(13312, 512, rr="p (t i) -> p t i", t=LCH)
        PJT = carve(12800, 704, rr="p (a w) -> p a w", a=2)
        PJC = carve(13824, 5632, rr="p (t w) -> p t w", t=LCH)
        SCR = 19456
        SG = carve(SCR + 0, 512)
        RANK = carve(SCR + 0, 512)
        GM = carve(SCR + 512, 512)
        CAV = carve(SCR + 512, 512)
        RMX = carve(SCR + 1024, 512)
        T512a = carve(SCR + 1536, 512)
        T512b = carve(SCR + 2048, 512)
        RS = carve(SCR + 2560, 512)
        AIDX = carve(SCR + 2560, 512)
        ACCR = carve(SCR + 2560, 512)
        CS = carve(SCR + 3072, 512)
        SROW = carve(SCR + 3072, 512)
        YS = carve(SCR + 3072, 256, dt=BF16, rr="p (i t) -> p i t", i=NN)
        CV = carve(SCR + 3584, 512)
        YS2 = carve(SCR + 3584, 256, dt=BF16, rr="p (i t) -> p i t", i=NN)
        IOTA = carve(11008, 512)
        M64 = carve(SCR + 4096, 64)
        CIDX = carve(SCR + 4160, 64)
        PFX = carve(SCR + 4224, 72, rr="p (i k) -> p i k", i=NN)
        PFS = carve(SCR + 4296, 72, rr="p (i k) -> p i k", i=NN)
        NEWP = carve(SCR + 4368, 72, rr="p (i k) -> p i k", i=NN)
        IDA = carve(SCR + 4440, 72)
        HSB = carve(SCR + 4512, 9)
        UCOL = carve(SCR + 4522, 8)
        PFM = carve(SCR + 4532, 64)
        SLN = carve(8192, 512)
        MALL9 = carve(8192, 5184, rr="p (t i k) -> p t i k", t=LCH, i=9)
        AH9 = carve(13824, 4608, rr="p (t i j) -> p t i j", t=LCH, i=NN)
        PRD9 = carve(18432, 648)
        PRD = carve(14976, 648)
        PRDR = carve(0, 4608)
        GATED = carve(0, 9216, dt=BF16, rr="p (e k f) -> p e k f", e=9, k=4)
        WGr = carve(9216, 4096, dt=BF16, rr="p (r k f) -> p r k f", r=2, k=8)
        WUr = carve(13312, 4096, dt=BF16, rr="p (r k f) -> p r k f", r=2, k=8)
        WDr = carve(17408, 4096, dt=BF16, rr="p (r k f) -> p r k f", r=2, k=4)
        SILU = carve(21504, 1536, rr="p (a f) -> p a f", a=3)
        ST = SB("ST", [128, 16, NSL])
        LNWC = SB("LNWC", [128, 8]); LNBC = SB("LNBC", [128, 8])
        BROW = SB("BROW", [1, 352]); BROWB = SB("BROWB", [1, 352], BF16)
        OWB = SB("OWB", [32, D], BF16); OBB = SB("OBB", [1, D], BF16)
        WRB = SB("WRB", [128, 8, NEXP], BF16)
        IDT = SB("IDT", [128, 128], BF16); ONES = SB("ONESb", [1, 128], BF16)
        ONEH = SB("ONEH", [16, 2048], BF16); PERMB = SB("PERMB", [128, 128], BF16)
        E16 = SB("E16", [128, NEXP]); P16 = SB("P16", [128, NEXP]); PW1 = SB("PW1", [128, NEXP])
        OH = SB("OH", [128, NEXP]); CWS = SB("CWS", [128, NEXP]); CWBF = SB("CWBF", [128, NEXP], BF16)
        MX1 = SB("MX1", [128, 1]); MX2 = SB("MX2", [128, 1]); SM1 = SB("SM1", [128, 1])
        CWT = SB("CWT", [NEXP, S], BF16)
        YST = ARH[:][0:32, 8192:9216].bitcast(BF16)
        B = [ctx.enter_context(nc.psum_tensor(f"pb{i}", [128, 512], F32)) for i in range(8)]

        for s_ in sems:
            s_.h = ctx.enter_context(nc.semaphore(s_.name))
        blk = ctx.enter_context(nc.Block())

        iSUM, iNMU, iSSQ, iCMB, iNMC, iA1, iNM1, iS3, iNM3, iSQ3, iIS3, iNI3 = range(12)

        em = P.emit
        def dbgw():
            return [(OUT, OUT.n)] if dbg else []
        for q in range(4):
            em("sync", lambda e, q=q: e.dma_start(out=XT[:, 4*q:4*q+4, :],
                in_=d_x.rearrange("(s p) d -> p s d", p=128)[:, 4*q:4*q+4, :]), inc=LDX, k=16)
        LDX_N = LDX.n

        def wdma(dst_off, width, src, kc):
            def f(e):
                return e.dma_start(
                    out=WPF[:, kc, :].rearrange("p (h w) -> p h w", h=H)[:, :, dst_off:dst_off + width],
                    in_=src[kc * 128:(kc + 1) * 128, :].rearrange("p (h w) -> p h w", h=H))
            return f
        for kc in range(8):
            em("sync", wdma(0, 8, d_signW, kc), inc=LDP, k=16)
            em("sync", wdma(8, 64, d_permW, kc), inc=LDP, k=16)
            em("sync", wdma(72, 8, d_gateW, kc), inc=LDP, k=16)
            em("sync", wdma(80, 8, d_inW, kc), inc=LDP, k=16)
        em("sync", lambda e: e.dma_start(out=LNWC[:], in_=d_lnw.rearrange("(k p) -> p k", p=128), allow_slow_non_contiguous=True), inc=LDP, k=16)
        em("sync", lambda e: e.dma_start(out=LNBC[:], in_=d_lnb.rearrange("(k p) -> p k", p=128), allow_slow_non_contiguous=True), inc=LDP, k=16)
        em("sync", lambda e: e.dma_start(
            out=BROW[:].rearrange("o (h w) -> o h w", h=H)[:, :, 0:8],
            in_=d_signb[None, :].rearrange("o (h w) -> o h w", h=H)), inc=LDP, k=16)
        em("sync", lambda e: e.dma_start(
            out=BROW[:].rearrange("o (h w) -> o h w", h=H)[:, :, 8:72],
            in_=d_permb[None, :].rearrange("o (h w) -> o h w", h=H)), inc=LDP, k=16)
        em("sync", lambda e: e.dma_start(
            out=BROW[:].rearrange("o (h w) -> o h w", h=H)[:, :, 72:80],
            in_=d_gateb[None, :].rearrange("o (h w) -> o h w", h=H)), inc=LDP, k=16)
        em("sync", lambda e: e.dma_start(
            out=BROW[:].rearrange("o (h w) -> o h w", h=H)[:, :, 80:88],
            in_=d_inb[None, :].rearrange("o (h w) -> o h w", h=H)), inc=LDP, k=16)
        em("sync", lambda e: e.dma_start(out=IOTA[:], in_=d_iota), inc=LDP, k=16)
        em("sync", lambda e: e.dma_start(out=IDA[:], in_=d_idaug), inc=LDP, k=16)
        LDW = LDP.n

        em("gpsimd", lambda e: e.dma_start(out=IDT[:], in_=d_ident), inc=GDK, k=16)
        em("gpsimd", lambda e: e.dma_start(out=ONES[:], in_=d_ones), inc=GDK, k=16)
        em("gpsimd", lambda e: e.dma_start(out=OWB[:], in_=d_outW), inc=GDK, k=16)
        em("gpsimd", lambda e: e.dma_start(out=OBB[:], in_=d_outb[None, :]), inc=GDK, k=16)
        em("gpsimd", lambda e: e.dma_start(out=WRB[:], in_=d_wr.rearrange("(k p) n -> p k n", p=128)), inc=GDK, k=16)
        em("gpsimd", lambda e: e.dma_start(out=ONEH[:], in_=d_oneh), inc=GDK, k=16)
        em("gpsimd", lambda e: e.dma_start(out=PERMB[:], in_=d_perm), inc=GDK, k=16)
        GDC = GDK.n

        for kc in range(8):
            em("vector", lambda e, kc=kc: e.tensor_scalar(
                WP[:, kc, :], WPF[:, kc, :], LNWC[:, kc:kc + 1], None, op0=ALU.mult),
                waits=[(LDP, LDW)], inc=VE)
        VE_WP = VE.n
        for kc in range(8):
            em("tensor", lambda e, kc=kc: e.matmul(
                B[7][0:1, 0:352], LNBC[:, kc:kc + 1], WPF[:, kc, :],
                start=(kc == 0), stop=(kc == 7)),
                waits=[(LDP, LDW)] if kc == 0 else (), inc=TE if kc == 7 else None)
        TE_BP = TE.n
        em("vector", lambda e: e.tensor_add(BROW[0:1, :], BROW[0:1, :], B[7][0:1, 0:352]),
           waits=[(TE, TE_BP)], inc=VE)
        em("vector", lambda e: e.tensor_copy(BROWB[0:1, :], BROW[0:1, :]), inc=VE)
        VE_BROW = VE.n

        inv_d = 1.0 / D
        def ln1sums(e):
            e.tensor_reduce(ST[:, iSUM, :], XT[:], axis=AX.X, op=ALU.add)
            e.tensor_scalar(ST[:, iNMU, :], ST[:, iSUM, :], -inv_d, None, op0=ALU.mult)
            return e.raw.engine_nop()
        em("vector", ln1sums, waits=[(LDX, LDX_N)], inc=VE)
        VE_MU = VE.n
        for s in range(NSL):
            em("scalar", lambda e, s=s: e.activation(
                SQS[:], XT[:, s, :], AF.Square, bias=ST[:, iNMU, s:s + 1], scale=1.0,
                accum_out=ST[:, iSSQ, s:s + 1]),
                waits=[(VE, VE_MU)] if s == 0 else (), inc=SE)
        SE_SQ = SE.n
        stv = ST[:, iSSQ, :]
        def stats1(e):
            e.tensor_scalar(ST[:, iSQ3, :], stv, inv_d, EPS, op0=ALU.mult, op1=ALU.add)
            e.drain()
            e.reciprocal(ST[:, iIS3, :], ST[:, iSQ3, :])
            e.drain()
            e.tensor_scalar(ST[:, iNI3, :], stv, inv_d, None, op0=ALU.mult)
            e.tensor_mul(ST[:, iNI3, :], ST[:, iNI3, :], ST[:, iIS3, :])
            e.tensor_scalar(ST[:, iNI3, :], ST[:, iNI3, :], EPS, None, op0=ALU.add)
            e.drain()
            return e.engine_nop()
        em("vector", stats1, waits=[(SE, SE_SQ)], inc=VE)
        def stats2(e):
            e.sqrt(ST[:, iSQ3, :], ST[:, iSQ3, :])
            e.sqrt(ST[:, iNI3, :], ST[:, iNI3, :])
            return e.copy(ST[:, iS3, :], ST[:, iNI3, :])
        em("scalar", stats2, waits=[(VE, VE.n)], inc=SE)
        def stats3(e):
            e.reciprocal(ST[:, iA1, :], ST[:, iSQ3, :])
            e.drain()
            e.reciprocal(ST[:, iCMB, :], ST[:, iNI3, :])
            e.drain()
            e.tensor_mul(ST[:, iCMB, :], ST[:, iCMB, :], ST[:, iA1, :])
            e.tensor_mul(ST[:, iNM1, :], ST[:, iNMU, :], ST[:, iA1, :])
            e.tensor_scalar(ST[:, iA1, :], ST[:, iA1, :], 1.0, None, op0=ALU.add)
            e.tensor_mul(ST[:, iNMC, :], ST[:, iNMU, :], ST[:, iCMB, :])
            e.drain()
            return e.engine_nop()
        em("vector", stats3, waits=[(SE, SE.n)], inc=VE)
        VE_ST1 = VE.n

        TE_TRS = {}
        for s in range(NSL):
            em("scalar", lambda e, s=s: e.raw.activation(
                Y2BF[:, s % 2, :], XT[:, s, :], AF.Identity,
                bias=ST[:, iNMC, s:s + 1], scale=ST[:, iCMB, s:s + 1]),
                waits=[(VE, VE_ST1)] + ([(TE, TE_TRS[s - 2])] if s >= 2 else []), inc=SE)
            se_y = SE.n
            for kc in range(8):
                pb = B[5 + (kc % 2)]
                tw = ([(SE, se_y), (GDK, GDC)] if kc == 0 else [])
                if kc >= 2 or s > 0:
                    tw = tw + [(VE, VE.n)]
                em("tensor", lambda e, s=s, kc=kc, pb=pb: e.transpose(
                    pb[:].bitcast(BF16)[:, 0:128], Y2BF[:, s % 2, kc * 128:(kc + 1) * 128], IDT[:]),
                    waits=tw, inc=TE)
                te_tr = TE.n
                em("vector", lambda e, s=s, kc=kc, pb=pb: e.raw.tensor_copy(
                    XNF[:, kc, s * 128:(s + 1) * 128], pb[:].bitcast(BF16)[:, 0:128]),
                    waits=[(TE, te_tr)], inc=VE)
            TE_TRS[s] = TE.n
        VE_XNF = VE.n
        if dbg:
            em("sync", lambda e: e.dma_start(out=dbg_t["dbg_xnf"], in_=XNF[:]),
               waits=[(VE, VE_XNF)], inc=OUT, k=16)

        SE_PJ = {}
        GD_PJd = {}
        for s in range(NSL):
            pb = B[0 + (s % 2)]
            for kc in range(8):
                em("tensor", lambda e, s=s, kc=kc, pb=pb: e.matmul(
                    pb[:, 0:352], XNF[:, kc, s * 128:(s + 1) * 128], WP[:, kc, :],
                    start=(kc == 0), stop=False),
                    waits=([(VE, VE_XNF), (VE, VE_WP), (VE, VE_BROW), (GDK, GDC)]
                           if s == 0 and kc == 0 else []) +
                          ([(SE, SE_PJ[s - 2])] if s >= 2 and kc == 0 else []))
            em("tensor", lambda e, pb=pb: e.matmul(
                pb[:, 0:352], ONES[0:1, :], BROWB[0:1, :], start=False, stop=True), inc=TE)
            te_p = TE.n
            em("scalar", lambda e, s=s, pb=pb: e.raw.copy(PJT[:, s % 2, :], pb[:, 0:352]),
               waits=[(TE, te_p)] + ([(GPJ[s % 2], GD_PJd[s - 2][0]), (SPJ[s % 2], GD_PJd[s - 2][1])] if s >= 2 else []), inc=SE)
            SE_PJ[s] = SE.n
            for h in range(4):
                qeng = "gpsimd" if h < 2 else "sync"
                qsem = GPJ[s % 2] if h < 2 else SPJ[s % 2]
                for a in range(2):
                    em(qeng, lambda e, s=s, h=h, a=a: e.dma_start(
                        out=PJC[8 * s + 4 * a + h: 8 * s + 4 * a + h + 1],
                        in_=PJT[a * 64:(a + 1) * 64, s % 2, h * 88:(h + 1) * 88]),
                        waits=[(SE, SE_PJ[s])] if a == 0 and h in (0, 2) else (), inc=qsem, k=16)
            GD_PJd[s] = (GPJ[s % 2].n, SPJ[s % 2].n)
        GD_PJ0 = GPJ[0].n; GD_PJ1 = GPJ[1].n; SD_PJ0 = SPJ[0].n; SD_PJ1 = SPJ[1].n
        if dbg:
            em("sync", lambda e: e.dma_start(out=dbg_t["dbg_projC"], in_=PJC[:]),
               waits=[(GPJ[0], GD_PJ0), (GPJ[1], GD_PJ1), (SPJ[0], SD_PJ0), (SPJ[1], SD_PJ1)], inc=OUT, k=16)

        em("scalar", lambda e: e.activation(SG[:].rearrange("p (t i) -> p t i", i=8),
                                            PJC[:, :, 0:8], AF.Sigmoid),
           waits=[(GPJ[0], GD_PJ0), (GPJ[1], GD_PJ1), (SPJ[0], SD_PJ0), (SPJ[1], SD_PJ1)], inc=SE)
        em("scalar", lambda e: e.activation(GM[:].rearrange("p (t i) -> p t i", i=8),
                                            PJC[:, :, 72:80], AF.Sigmoid), inc=SE)
        em("scalar", lambda e: e.activation(A[:].rearrange("p (t ij) -> p t ij", ij=64),
                                            PJC[:, :, 8:72], AF.Exp), inc=SE)
        em("scalar", lambda e: e.copy(RMX[:].rearrange("p (t i) -> p t i", i=8),
                                      PJC[:, :, 80:88]), inc=SE)
        SE_SIG = SE.n
        def dcalc_g(e):
            e.tensor_scalar(T512a[:], SG[:], 2.0, 1.0, op0=ALU.mult, op1=ALU.subtract)
            e.tensor_tensor(DT.rearrange("p t i -> p (t i)"), GM[:], T512a[:], op=ALU.mult)
            e.tensor_scalar(T512b[:], GM[:], -1.0, 1.0, op0=ALU.mult, op1=ALU.add)
            e.tensor_tensor(CT.rearrange("p t i -> p (t i)"), RMX[:], T512b[:], op=ALU.mult)
            return e.raw.nop()
        em("gpsimd", dcalc_g, waits=[(SE, SE_SIG)] + dbgw(), inc=GP)
        GP_DC = GP.n
        def wsrcs(kind, idx):
            if kind == "e":
                return (d_wg[idx].rearrange("(k p) f -> p k f", p=128),
                        d_wu[idx].rearrange("(k p) f -> p k f", p=128),
                        d_wd[idx].rearrange("(k p) f -> p k f", p=128))
            hf = idx
            return (d_sg[:, hf * 512:(hf + 1) * 512].rearrange("(k p) f -> p k f", p=128),
                    d_su[:, hf * 512:(hf + 1) * 512].rearrange("(k p) f -> p k f", p=128),
                    d_sd[hf * 512:(hf + 1) * 512, :].rearrange("(k p) f -> p k f", p=128))
        groups = [[("sh", 0)] + [("e", i) for i in range(8)],
                  [("sh", 1)] + [("e", i) for i in range(8, 16)]]
        PCS = FWD[0]
        def precast(lo, hi):
            for s_idx in range(lo, hi):
                gx, ei = divmod(s_idx, 9)
                kind, idx = ("sh", gx) if ei == 0 else ("e", gx * 8 + ei - 1)
                wg_s, wu_s, wd_s = wsrcs(kind, idx)
                em("gpsimd", lambda e, wg_s=wg_s, s_idx=s_idx: e.dma_start(
                    out=d_scrG[s_idx], in_=wg_s), inc=PCS, k=16)
                em("gpsimd", lambda e, wu_s=wu_s, s_idx=s_idx: e.dma_start(
                    out=d_scrU[s_idx], in_=wu_s), inc=PCS, k=16)
                em("gpsimd", lambda e, wd_s=wd_s, s_idx=s_idx: e.dma_start(
                    out=d_scrD[s_idx], in_=wd_s), inc=PCS, k=16)
        precast(0, 9)
        if dbg:
            em("sync", lambda e: e.dma_start(out=dbg_t["dbg_dt"],
               in_=DT.rearrange("p t i -> p (t i)")), waits=[(GP, GP_DC)], inc=OUT, k=16)
            em("sync", lambda e: e.dma_start(out=dbg_t["dbg_ct"],
               in_=CT.rearrange("p t i -> p (t i)")), waits=[(GP, GP_DC)], inc=OUT, k=16)

        Av = A[:].rearrange("p (t i j) -> p t i j", i=8, j=8)
        RSt = RS[:].rearrange("p (t i) -> p t i", i=8)
        CSt = CS[:].rearrange("p (t j) -> p t j", j=8)
        SH = 144
        def s_recip(buf):
            def f(e):
                e.activation(SLN[:, 0:512 - SH], buf[:, SH:512], AF.Ln)
                e.activation(buf[:, SH:512], SLN[:, 0:512 - SH], AF.Exp, scale=-1.0)
                return e.raw.nop()
            return f
        for it in range(5):
            em("vector", lambda e: e.tensor_reduce(
                RSt, Av.rearrange("p t i j -> p (t i) j"), axis=AX.X, op=ALU.add),
                waits=([(SE, SE_SIG)] + dbgw()) if it == 0 else [], inc=VE)
            v_rr = VE.n
            em("scalar", s_recip(RS[:]), waits=[(VE, v_rr)], inc=SE)
            s_rr = SE.n
            em("vector", lambda e: e.reciprocal(RS[:, 0:SH], RS[:, 0:SH]), inc=VE)
            em("vector", lambda e: e.tensor_tensor(
                Av.rearrange("p t i j -> p (t i) j"),
                Av.rearrange("p t i j -> p (t i) j"),
                RS[:][:, :, None].broadcast_to([128, 512, 8]), op=ALU.mult),
                waits=[(SE, s_rr)], inc=VE)
            em("vector", lambda e: e.tensor_reduce(
                CSt, Av.rearrange("p t i j -> p t j i"), axis=AX.X, op=ALU.add), inc=VE)
            v_cr = VE.n
            em("scalar", s_recip(CS[:]), waits=[(VE, v_cr)], inc=SE)
            s_cr = SE.n
            em("vector", lambda e: e.reciprocal(CS[:, 0:SH], CS[:, 0:SH]), inc=VE)
            em("vector", lambda e: e.tensor_tensor(
                Av, Av, CSt[:, :, None, :].broadcast_to([128, LCH, 8, 8]), op=ALU.mult),
                waits=[(SE, s_cr)], inc=VE)
        GP_SK = GP.n
        VE_SK = VE.n
        if dbg:
            em("sync", lambda e: e.dma_start(out=dbg_t["dbg_A"], in_=A[:]),
               waits=[(VE, VE_SK), (GP, GP_SK)], inc=OUT, k=16)

        TG = 56
        c_tg = TG * 8
        ng = LCH - TG
        SROW7 = carve(13824, 3584)
        GT1 = carve(17408, ng * 32, rr="p (t i j) -> p t i j", t=ng, i=4)
        GT2 = carve(17408 + ng * 32, ng * 16, rr="p (t i j) -> p t i j", t=ng, i=2)
        def grank(e):
            e.tensor_reduce(RMX[:], Av.rearrange("p t i j -> p (t i) j"), axis=AX.X, op=ALU.max)
            e.memset(RANK[:], 0.0)
            Rv = RMX[:].rearrange("p (t i) -> p t i", i=8)
            Kv = RANK[:].rearrange("p (t i) -> p t i", i=8)
            Ta = T512a[:].rearrange("p (t i) -> p t i", i=8)
            Tb = T512b[:].rearrange("p (t i) -> p t i", i=8)
            for dd in range(1, 8):
                e.tensor_tensor(Ta[:, :, 0:8 - dd], Rv[:, :, 0:8 - dd], Rv[:, :, dd:8],
                                op=ALU.is_ge)
                e.tensor_tensor(Kv[:, :, dd:8], Kv[:, :, dd:8], Ta[:, :, 0:8 - dd], op=ALU.add)
                e.tensor_tensor(Tb[:, :, 0:8 - dd], Rv[:, :, dd:8], Rv[:, :, 0:8 - dd],
                                op=ALU.is_gt)
                e.tensor_tensor(Kv[:, :, 0:8 - dd], Kv[:, :, 0:8 - dd], Tb[:, :, 0:8 - dd],
                                op=ALU.add)
            e.memset(CAV[:], 1.0)
            e.memset(AIDX[:], 0.0)
            return e.raw.engine_nop()
        em("vector", grank, waits=[(VE, VE_SK), (GP, GP_DC), (LDP, LDW)] + dbgw(), inc=VE)
        gm_done = {}
        for r in range(7):
            mbuf = (T512a, T512b)[r % 2]
            em("vector", lambda e, r=r, mbuf=mbuf: e.tensor_single_scalar(
                mbuf[:], RANK[:], float(r), op=ALU.is_equal),
                waits=[(GP, gm_done[r - 2])] if r >= 2 else [], inc=VE)
            v_mk = VE.n
            def gext(e, r=r, mbuf=mbuf):
                Hg = HARD4[:, TG:LCH]
                e.tensor_tensor(Hg.rearrange("p t i j -> p (t i) j"),
                                Av[:, TG:LCH].rearrange("p t i j -> p (t i) j"),
                                mbuf[:, c_tg:512][:, :, None].broadcast_to([128, ng * 8, 8]),
                                op=ALU.mult)
                e.tensor_tensor(GT1, Hg[:, :, 0:4, :], Hg[:, :, 4:8, :], op=ALU.add)
                e.tensor_tensor(GT2, GT1[:, :, 0:2, :], GT1[:, :, 2:4, :], op=ALU.add)
                e.tensor_tensor(SROW7[:, r * 512 + c_tg:(r + 1) * 512]
                                .rearrange("p (t j) -> p t j", j=8),
                                GT2[:, :, 0, :], GT2[:, :, 1, :], op=ALU.add)
                return e.raw.nop()
            em("gpsimd", gext, waits=[(VE, v_mk)], inc=GP)
            gm_done[r] = GP.n
            em("vector", lambda e, mbuf=mbuf: e.tensor_tensor(
                HARD4[:, 0:TG].rearrange("p t i j -> p (t i) j"),
                Av[:, 0:TG].rearrange("p t i j -> p (t i) j"),
                mbuf[:, 0:c_tg][:, :, None].broadcast_to([128, c_tg, 8]), op=ALU.mult), inc=VE)
            em("vector", lambda e, r=r: e.tensor_reduce(
                SROW7[:, r * 512:r * 512 + c_tg].rearrange("p (t j) -> p t j", j=8),
                HARD4[:, 0:TG].rearrange("p t i j -> p t j i"), axis=AX.X, op=ALU.max),
                inc=VE)
        def grounds(e):
            BIG = 1.0e30
            srow = SROW[:].rearrange("p (t j) -> p t j", j=8)
            cv = CV[:].rearrange("p (t j) -> p t j", j=8)
            Ta = T512a[:].rearrange("p (t i) -> p t i", i=8)
            Tb = T512b[:].rearrange("p (t i) -> p t i", i=8)
            for r in range(8):
                if r < 7:
                    e.tensor_single_scalar(T512a[:], RANK[:], float(r), op=ALU.is_equal)
                    e.tensor_scalar(CV[:], CAV[:], 1.0, BIG, op0=ALU.subtract, op1=ALU.mult)
                    e.tensor_add(CV[:], CV[:], SROW7[:, r * 512:(r + 1) * 512])
                    e.tensor_reduce(M64[:], cv, axis=AX.X, op=ALU.max)
                    e.tensor_tensor(Tb, cv, M64[:][:, :, None].broadcast_to([128, LCH, 8]),
                                    op=ALU.is_equal)
                else:
                    e.tensor_single_scalar(T512a[:], RANK[:], 7.0, op=ALU.is_equal)
                    e.tensor_copy(T512b[:], CAV[:])
                e.tensor_mul(SROW[:], T512b[:], IOTA[:])
                e.tensor_reduce(CIDX[:], srow, axis=AX.X, op=ALU.add)
                e.tensor_tensor(cv, Ta, CIDX[:][:, :, None].broadcast_to([128, LCH, 8]),
                                op=ALU.mult)
                e.tensor_add(AIDX[:], AIDX[:], CV[:])
                if r < 7:
                    e.tensor_sub(CAV[:], CAV[:], T512b[:])
            e.drain()
            for j in range(8):
                e.tensor_single_scalar(
                    AH9[:, :, :, j].rearrange("p t i -> p (t i)"),
                    AIDX[:], float(j), op=ALU.is_equal)
            e.drain()
            e.tensor_tensor(AH9[:, 0:TG, :, 0:8].rearrange("p t i j -> p (t i) j"),
                            AH9[:, 0:TG, :, 0:8].rearrange("p t i j -> p (t i) j"),
                            DT.rearrange("p t i -> p (t i)")[:, 0:c_tg][:, :, None]
                            .broadcast_to([128, c_tg, 8]),
                            op=ALU.mult)
            e.drain()
            return e.raw.engine_nop()
        em("vector", grounds, waits=[(GP, gm_done[6])], inc=VE)
        v_oh = VE.n
        em("gpsimd", lambda e: e.tensor_tensor(
            AH9[:, TG:LCH, :, 0:8].rearrange("p t i j -> p (t i) j"),
            AH9[:, TG:LCH, :, 0:8].rearrange("p t i j -> p (t i) j"),
            DT.rearrange("p t i -> p (t i)")[:, c_tg:512][:, :, None]
            .broadcast_to([128, ng * 8, 8]),
            op=ALU.mult), waits=[(VE, v_oh)], inc=GP)
        GP_GR = GP.n
        precast(9, 18)
        PC_ALL = PCS.n
        VE_GR = VE.n
        if dbg:
            em("sync", lambda e: e.dma_start(out=dbg_t["dbg_hard"],
               in_=AH9[:, :, :, 0:8].rearrange("p t i j -> p (t i) j")),
               waits=[(VE, VE_GR), (GP, GP_GR)], inc=OUT, k=16)
            em("sync", lambda e: e.dma_start(out=dbg_t["dbg_rank"], in_=RANK[:]),
               waits=[(VE, VE_GR), (GP, GP_GR)], inc=OUT, k=16)
            em("sync", lambda e: e.dma_start(out=dbg_t["dbg_aidx"], in_=AIDX[:]),
               waits=[(VE, VE_GR), (GP, GP_GR)], inc=OUT, k=16)
            em("sync", lambda e: e.dma_start(out=dbg_t["dbg_rmx"], in_=RMX[:]),
               waits=[(VE, VE_GR), (GP, GP_GR)], inc=OUT, k=16)

        def compose(e):
            e.tensor_copy(AH9[:, :, :, 8].rearrange("p t i -> p (t i)"),
                          CT.rearrange("p t i -> p (t i)"))
            e.memset(MALL9[:, :, 8, 0:8], 0.0)
            e.memset(MALL9[:, :, 8, 8], 1.0)
            e.tensor_copy(MALL9[:, 0, 0:8, :], AH9[:, 0, :, :])
            e.drain()
            for t in range(1, LCH):
                e.tensor_tensor(PRD9[:, 0:648].rearrange("p (i j k) -> p i j k", i=8, j=9),
                                AH9[:, t, :, :][:, :, :, None].broadcast_to([128, 8, 9, 9]),
                                MALL9[:, t - 1][:, None, :, :].broadcast_to([128, 8, 9, 9]),
                                op=ALU.mult)
                e.tensor_reduce(MALL9[:, t, 0:8, :],
                                PRD9[:, 0:648].rearrange("p (i j k) -> p i k j", i=8, j=9),
                                axis=AX.X, op=ALU.add)
            e.drain()
            e.tensor_copy(PFX[:], MALL9[:, LCH - 1, 0:8, :].rearrange("p i k -> p (i k)"))
            e.tensor_copy(PFM.rearrange("p (a j) -> p a j", a=8), MALL9[:, LCH - 1, 0:8, 0:8])
            return e.raw.engine_nop()
        em("vector", compose, waits=[(VE, VE_GR), (GP, GP_GR)] + dbgw(), inc=VE)
        VE_CMP = VE.n

        ve_prev = VE_CMP
        for lvl in range(5):
            dlt = 4 * (1 << lvl)
            em("gpsimd", lambda e, dlt=dlt: e.dma_start(out=PFS[dlt:128, :, :], in_=PFX[0:128 - dlt, :, :]),
               waits=[(VE, ve_prev)], inc=GSC, k=16)
            em("gpsimd", lambda e, dlt=dlt: e.dma_start(
                out=PFS[0:dlt, :, :].rearrange("p k i -> p (k i)"), in_=IDA[0:dlt, :]),
               inc=GSC, k=16)
            gd_sh = GSC.n
            def hslvl(e):
                e.tensor_tensor(PRD[:, 0:576].rearrange("p (x k) -> p x k", k=9),
                                PFM[:, :, None].broadcast_to([128, 64, 9]),
                                PFS[:].rearrange("p j k -> p (j k)")[:, None, :]
                                .broadcast_to([128, 8, 72]),
                                op=ALU.mult)
                e.tensor_reduce(NEWP[:],
                                PRD[:, 0:576].rearrange("p (a j k) -> p a k j", a=8, j=8),
                                axis=AX.X, op=ALU.add)
                e.tensor_add(NEWP[:, :, 8], NEWP[:, :, 8], PFX[:, :, 8])
                e.tensor_copy(PFX[:], NEWP[:])
                e.tensor_copy(PFM.rearrange("p (a j) -> p a j", a=8), NEWP[:, :, 0:8])
                return e.raw.engine_nop()
            em("vector", hslvl, waits=[(GSC, gd_sh)], inc=VE)
            ve_prev = VE.n
        VE_HS = VE.n

        em("vector", lambda e: e.tensor_copy(UCOL[:], PFX[:, :, 8]), waits=[(VE, VE_HS)], inc=VE)
        VE_UC = VE.n
        em("gpsimd", lambda e: e.dma_start(out=HSB[4:128, 0:8], in_=UCOL[0:124, :]),
           waits=[(VE, VE_UC)], inc=GSC, k=16)
        GD_HS = GSC.n
        def hszero(e):
            e.memset(HSB[0:4, 0:8], 0.0)
            e.memset(HSB[:, 8:9], 1.0)
            return e.raw.engine_nop()
        em("vector", hszero, waits=[(VE, VE_HS)], inc=VE)
        VE_HS0 = VE.n
        if dbg:
            em("sync", lambda e: e.dma_start(out=dbg_t["dbg_mall"],
               in_=MALL9[:, :, 0:8, :].rearrange("p t i k -> p t (i k)")),
               waits=[(VE, VE_CMP)], inc=OUT, k=16)
            em("sync", lambda e: e.dma_start(out=dbg_t["dbg_hs"], in_=HSB[:, 0:8]),
               waits=[(GSC, GD_HS), (VE, VE_HS0)], inc=OUT, k=16)

        def recon(e):
            e.tensor_tensor(PRDR[:, 0:4608].rearrange("p (t i k) -> p t i k", t=LCH, i=NN),
                            MALL9[:, :, 0:8, :],
                            HSB[:][:, None, None, :].broadcast_to([128, LCH, 8, 9]),
                            op=ALU.mult)
            e.tensor_reduce(ACCR[:],
                            PRDR[:, 0:4608].rearrange("p (t i k) -> p t i k", t=LCH, i=NN),
                            axis=AX.X, op=ALU.add)
            e.tensor_copy(YS, ACCR[:].rearrange("p (t i) -> p i t", i=NN))
            return e.raw.engine_nop()
        em("vector", recon, waits=[(GSC, GD_HS), (VE, VE_HS0)] + dbgw(), inc=VE)
        VE_YS = VE.n
        if dbg:
            em("sync", lambda e: e.dma_start(out=dbg_t["dbg_ys"],
               in_=YS.rearrange("p i t -> p (i t)")), waits=[(VE, VE_YS)], inc=OUT, k=16)

        em("tensor", lambda e: e.matmul(
            B[5][:, 0:512], PERMB[:], YS.rearrange("p i t -> p (i t)"), start=True, stop=True),
            waits=[(VE, VE_YS), (GDK, GDC)], inc=TE)
        te_pm = TE.n
        em("vector", lambda e: e.raw.tensor_copy(YS2.rearrange("p i t -> p (i t)"), B[5][:, 0:512]),
           waits=[(TE, te_pm)], inc=VE)
        VE_YS2 = VE.n
        for h in range(4):
            for i in range(NN):
                qeng = "gpsimd" if h < 2 else "sync"
                qsem = GSC if h < 2 else SPJ[0]
                em(qeng, lambda e, h=h, i=i: e.dma_start(
                    out=YST[h * 8 + i: h * 8 + i + 1, :].rearrange("q (c t) -> q c t", c=32),
                    in_=YS2[h * 32:(h + 1) * 32, i, :]),
                    waits=([(VE, VE_YS2)] + dbgw()) if i == 0 and h in (0, 2) else (),
                    inc=qsem, k=16)
        GD_YST = (GSC.n, SPJ[0].n)

        VE_RES = {}
        TE_TR3 = {}
        VE_CWT_T = {}
        out2_tile = {}
        RT7 = carve(23040, 64); RE7 = carve(23104, 64); ROH7 = carve(23168, 64)
        RPW7 = carve(23232, 64); RCW7 = carve(23296, 64)
        CWB7 = carve(23360, 32, dt=BF16)
        RM4 = carve(23392, 4); CI4 = carve(23396, 4)
        SQDMP = carve(1024, 1024)
        ve_bank_free = {i: VE.n for i in range(8)}
        ve_cwt_prev = VE.n
        ev56 = {5: (VE, 0), 6: (VE, 0)}
        te_slot_free_gu = {0: TE.n, 1: TE.n}
        te_slot_free_d = {0: TE.n, 1: TE.n}
        gu_loads = {}; gu_loads_u = {}
        gi_counter = 0
        scg_w = {0: 0, 1: 0}; scd_w = {0: 0, 1: 0}
        scg_done = {}; scd_done = {}

        for s in range(NSL):
            pb0, pb1 = (B[2], B[3]) if s % 2 == 0 else (B[5], B[6])
            em("tensor", lambda e, s=s, pb0=pb0: e.matmul(
                pb0[:, 0:512], YST[:, s * 128:(s + 1) * 128], OWB[:, 0:512], start=True, stop=False),
                waits=[(GSC, GD_YST[0]), (SPJ[0], GD_YST[1]), (GDK, GDC)] if s == 0 else
                      ([(VE, VE_RES[s - 2])] if s >= 2 else []))
            em("tensor", lambda e, pb0=pb0: e.matmul(
                pb0[:, 0:512], ONES[0:1, :], OBB[0:1, 0:512], start=False, stop=True))
            em("tensor", lambda e, s=s, pb1=pb1: e.matmul(
                pb1[:, 0:512], YST[:, s * 128:(s + 1) * 128], OWB[:, 512:1024], start=True, stop=False))
            em("tensor", lambda e, pb1=pb1: e.matmul(
                pb1[:, 0:512], ONES[0:1, :], OBB[0:1, 512:1024], start=False, stop=True), inc=TE)
            te_o = TE.n
            def resid(e, s=s, pb0=pb0, pb1=pb1):
                for nh in range(2):
                    pb = pb0 if nh == 0 else pb1
                    e.scalar_tensor_tensor(SQS[:, nh * 512:(nh + 1) * 512],
                                           XT[:, s, nh * 512:(nh + 1) * 512], ST[:, iA1, s:s + 1],
                                           pb[:, 0:512], op0=ALU.mult, op1=ALU.add)
                    e.tensor_scalar(XT[:, s, nh * 512:(nh + 1) * 512],
                                    SQS[:, nh * 512:(nh + 1) * 512], ST[:, iNM1, s:s + 1], None,
                                    op0=ALU.add)
                return e.raw.engine_nop()
            em("vector", resid, waits=[(TE, te_o)] + (dbgw() if s == 0 else []), inc=VE)
            VE_RES[s] = VE.n
            def ln3s(e, s=s):
                e.activation(SQDMP[:], XT[:, s, :], AF.Identity,
                             accum_out=ST[:, iS3, s:s + 1])
                e.activation(ST[:, iNM3, s:s + 1], ST[:, iS3, s:s + 1], AF.Identity,
                             scale=-inv_d)
                e.activation(SQDMP[:], XT[:, s, :], AF.Square,
                             bias=ST[:, iNM3, s:s + 1], scale=1.0,
                             accum_out=ST[:, iSQ3, s:s + 1])
                return e.raw.nop()
            em("scalar", ln3s, waits=[(VE, VE_RES[s])], inc=SE)
        SE_SQ3 = SE.n
        ve_bank_free = {i: VE.n for i in range(8)}
        if dbg:
            em("sync", lambda e: e.dma_start(out=dbg_t["dbg_xattn"], in_=XT[:]),
               waits=[(VE, VE_RES[NSL - 1])], inc=OUT, k=16)

        def p7_tile(tt):
            nonlocal ve_cwt_prev
            Y2B7 = Y2BF
            rb = (tt % 2) * 64
            if tt == 0:
                te_slot_free_gu[0] = te_slot_free_gu[1] = TE.n
                te_slot_free_d[0] = te_slot_free_d[1] = TE.n

            for si in range(4):
                s = tt * 4 + si
                em("scalar", lambda e, s=s, Y2B7=Y2B7: e.raw.activation(
                    Y2B7[:, s % 2, :], XT[:, s, :], AF.Identity,
                    bias=ST[:, iNI3, s:s + 1], scale=ST[:, iIS3, s:s + 1]),
                    waits=[(VE, VE_ST3)] + ([(TE, TE_TR3[s - 2])] if s >= 2 else []) + (dbgw() if s == 0 else []), inc=SE)
                se_y = SE.n
                for kc in range(8):
                    pb = B[5 + (kc % 2)]
                    tw = ([(SE, se_y)] if kc == 0 else [])
                    if kc >= 2 or s > 0:
                        tw = tw + [ev56[5 + (kc % 2)]]
                    em("tensor", lambda e, s=s, kc=kc, pb=pb, Y2B7=Y2B7: e.transpose(
                        pb[:].bitcast(BF16)[:, 0:128], Y2B7[:, s % 2, kc * 128:(kc + 1) * 128], IDT[:]),
                        waits=tw, inc=TE)
                    te_tr = TE.n
                    if kc % 2 == 0 or kc == 7:
                        em("vector", lambda e, s=s, kc=kc, pb=pb: e.raw.tensor_copy(
                            XNF[:, kc, s * 128:(s + 1) * 128], pb[:].bitcast(BF16)[:, 0:128]),
                            waits=[(TE, te_tr)], inc=VE)
                    else:
                        em("scalar", lambda e, s=s, kc=kc, pb=pb: e.raw.copy(
                            XNF[:, kc, s * 128:(s + 1) * 128], pb[:].bitcast(BF16)[:, 0:128]),
                            waits=[(TE, te_tr)], inc=SE)
                    ev56[5 + (kc % 2)] = (VE, VE.n) if (kc % 2 == 0 or kc == 7) else (SE, SE.n)
                TE_TR3[s] = TE.n
                ve_tr = VE.n
                se_tr = SE.n
                ve_bank_free[5] = VE.n
                ve_bank_free[6] = VE.n
                for kc in range(8):
                    em("tensor", lambda e, s=s, si=si, kc=kc, rb=rb: e.matmul(
                        B[4][:, rb + si * 16:rb + (si + 1) * 16],
                        XNF[:, kc, s * 128:(s + 1) * 128], WRB[:, kc, :],
                        start=(kc == 0), stop=(kc == 7)),
                        waits=(([(VE, ve_tr), (SE, se_tr)] +
                                ([(VE, r1done[tt - 2])] if si == 0 and tt >= 2 else []))
                               if kc == 0 else ()),
                        inc=TE if kc == 7 else None)
            te_rt_d[tt] = TE.n
        def p7_b(tt):
            nonlocal ve_cwt_prev
            rb = (tt % 2) * 64
            def rsoft1(e, rb=rb):
                e.tensor_reduce(RM4[:], B[4][:, rb:rb + 64].rearrange("p (s x) -> p s x", x=16),
                                axis=AX.X, op=ALU.max)
                e.tensor_tensor(RT7[:].rearrange("p (s x) -> p s x", x=16),
                                B[4][:, rb:rb + 64].rearrange("p (s x) -> p s x", x=16),
                                RM4[:][:, :, None].broadcast_to([128, 4, 16]),
                                op=ALU.subtract)
                return e.raw.engine_nop()
            em("vector", rsoft1, waits=[(TE, te_rt_d[tt])], inc=VE)
            r1done[tt] = VE.n
            em("scalar", lambda e: e.activation(RE7[:], RT7[:], AF.Exp),
               waits=[(VE, VE.n)], inc=SE)
            SE_REXP = SE.n
            def rsoft2(e):
                ev = lambda a: a[:].rearrange("p (s x) -> p s x", x=16)
                bc = lambda m: m[:][:, :, None].broadcast_to([128, 4, 16])
                e.tensor_reduce(CI4[:], ev(RE7), axis=AX.X, op=ALU.add)
                e.reciprocal(CI4[:], CI4[:])
                e.tensor_tensor(ev(RE7), ev(RE7), bc(CI4), op=ALU.mult)
                e.tensor_reduce(RM4[:], ev(RE7), axis=AX.X, op=ALU.max)
                e.tensor_tensor(ev(ROH7), ev(RE7), bc(RM4), op=ALU.is_equal)
                e.tensor_mul(RPW7[:], RE7[:], ROH7[:])
                e.tensor_sub(RPW7[:], RE7[:], RPW7[:])
                e.tensor_reduce(RM4[:], ev(RPW7), axis=AX.X, op=ALU.max)
                e.tensor_tensor(ev(RCW7), ev(RPW7), bc(RM4), op=ALU.is_equal)
                e.tensor_add(ROH7[:], ROH7[:], RCW7[:])
                e.tensor_mul(RCW7[:], RE7[:], ROH7[:])
                return e.raw.engine_nop()
            em("vector", rsoft2, waits=[(SE, SE_REXP)], inc=VE)
            ve_bank_free[4] = VE.n
            em("scalar", lambda e: e.raw.copy(CWB7[:], RCW7[:]), waits=[(VE, VE.n)], inc=SE)
            SE_CWB = SE.n
            for si in range(4):
                s = tt * 4 + si
                em("tensor", lambda e, si=si: e.transpose(
                    B[7][:].bitcast(BF16)[0:NEXP, 0:128], CWB7[:, si * 16:(si + 1) * 16], IDT[:]),
                    waits=[(SE, SE_CWB), (VE, ve_cwt_prev)] +
                          ([(VE, ve_bank_free[7])] if si == 0 else []), inc=TE)
                te_cw = TE.n
                em("vector", lambda e, s=s: e.raw.tensor_copy(
                    CWT[0:NEXP, s * 128:(s + 1) * 128],
                    B[7][:].bitcast(BF16)[0:NEXP, 0:128]), waits=[(TE, te_cw)], inc=VE)
                ve_cwt_prev = VE.n
            ve_bank_free[7] = VE.n
            VE_CWT_T[tt] = VE.n

        def ffn_tile(tt):
            nonlocal gi_counter
            tok = slice(tt * 512, (tt + 1) * 512)
            for gidx, grp in enumerate(groups):
                for ei, (kind, idx) in enumerate(grp):
                    gslot = gi_counter % 2
                    wg_s, wu_s, wd_s = wsrcs(kind, idx)
                    s_idx = gidx * 9 + ei
                    for hf_ in range(2):
                        em("sync", lambda e, s_idx=s_idx, gslot=gslot, hf_=hf_: e.dma_start(
                            out=WGr[:, gslot, :, hf_ * 256:(hf_ + 1) * 256],
                            in_=d_scrG[s_idx][:, :, hf_ * 256:(hf_ + 1) * 256]),
                            waits=([(TE, te_slot_free_gu[gslot])] +
                                   ([(PCS, PC_ALL)] if tt == 0 else []) +
                                   (dbgw() if tt == 0 and gi_counter == 0 else [])) if hf_ == 0 else (),
                            inc=FWGS[gslot][hf_], k=16)
                        em("sync", lambda e, s_idx=s_idx, gslot=gslot, hf_=hf_: e.dma_start(
                            out=WUr[:, gslot, :, hf_ * 256:(hf_ + 1) * 256],
                            in_=d_scrU[s_idx][:, :, hf_ * 256:(hf_ + 1) * 256]),
                            inc=FWGS[gslot][hf_], k=16)
                        gu_loads[(gi_counter, hf_)] = (FWGS[gslot][hf_], FWGS[gslot][hf_].n)
                        gu_loads_u[(gi_counter, hf_)] = gu_loads[(gi_counter, hf_)]
                    has_cw = (kind == "e")
                    bcb = 4 if gi_counter % 2 == 0 else 7
                    if has_cw:
                        em("tensor", lambda e, idx=idx, tok=tok, bcb=bcb: e.matmul(
                            B[bcb][:, 0:512], ONEH[:, idx * 128:(idx + 1) * 128], CWT[0:NEXP, tok],
                            start=True, stop=True),
                            waits=[(VE, VE_CWT_T[tt]), (VE, ve_bank_free[bcb])], inc=TE)
                    te_bc = TE.n
                    for mc in range(4):
                        rr = (gi_counter * 4 + mc) % 3
                        gb = (0, 1, 5)[rr]
                        ub = (2, 3, 6)[rr]
                        for kc in range(8):
                            em("tensor", lambda e, gslot=gslot, kc=kc, mc=mc, tok=tok, gb=gb: e.matmul(
                                B[gb][:, 0:512], WGr[:, gslot, kc, mc * 128:(mc + 1) * 128],
                                XNF[:, kc, tok], start=(kc == 0), stop=(kc == 7)),
                                waits=(([gu_loads[(gi_counter, mc // 2)],
                                         (VE, ve_bank_free[gb])] +
                                        ([(VE, VE_CWT_T[tt])] if ei == 0 and mc == 0 else []))
                                       if kc == 0 else ()),
                                inc=TE if kc == 7 else None)
                        te_g = TE.n
                        for kc in range(8):
                            em("tensor", lambda e, gslot=gslot, kc=kc, mc=mc, tok=tok, ub=ub: e.matmul(
                                B[ub][:, 0:512], WUr[:, gslot, kc, mc * 128:(mc + 1) * 128],
                                XNF[:, kc, tok], start=(kc == 0), stop=(kc == 7)),
                                waits=[(VE, ve_bank_free[ub])] if kc == 0 else (),
                                inc=TE if kc == 7 else None)
                        te_u = TE.n
                        em("scalar", lambda e, rr=rr, gb=gb: e.raw.activation(
                            SILU[:, rr, :], B[gb][:, 0:512], AF.Silu),
                            waits=[(TE, te_g)], inc=SE)
                        se_si = SE.n
                        def gevac(e, mc=mc, ub=ub, ei=ei, has_cw=has_cw, bcb=bcb, rr=rr):
                            sc = SILU[:, rr, :]
                            e.tensor_tensor(sc, sc, B[ub][:, 0:512], op=ALU.mult)
                            if has_cw:
                                e.raw.tensor_tensor(GATED[:, ei, mc, :], sc, B[bcb][:, 0:512],
                                                    op=ALU.mult)
                            else:
                                e.raw.tensor_copy(GATED[:, ei, mc, :], sc)
                            return e.raw.engine_nop()
                        em("vector", gevac, waits=[(SE, se_si), (TE, te_u)] +
                           ([(TE, te_bc)] if has_cw and mc == 0 else []) +
                           (dbgw() if gi_counter == 0 and mc == 0 else []), inc=VE)
                        ve_bank_free[gb] = VE.n
                        ve_bank_free[ub] = VE.n
                        if has_cw:
                            ve_bank_free[bcb] = VE.n
                    te_slot_free_gu[gslot] = TE.n
                    gi_counter += 1
                ve_gated = VE.n
                for ei, (kind, idx) in enumerate(grp):
                    dslot = ei % 2
                    wg_s, wu_s, wd_s = wsrcs(kind, idx)
                    s_idx = gidx * 9 + ei
                    em("sync", lambda e, s_idx=s_idx, dslot=dslot: e.dma_start(
                        out=WDr[:, dslot, :, :], in_=d_scrD[s_idx]),
                        waits=[(TE, te_slot_free_d[dslot])] +
                              ([(PCS, PC_ALL)] if tt == 0 else []),
                        inc=FWDS[dslot], k=16)
                    gd_wd = (FWDS[dslot], FWDS[dslot].n)
                    for sl in range(4):
                        for nh in range(2):
                            bank = sl * 2 + nh
                            for kc in range(4):
                                last = (sl == 3 and nh == 1 and kc == 3)
                                em("tensor", lambda e, ei=ei, sl=sl, nh=nh, kc=kc, dslot=dslot, bank=bank: e.matmul(
                                    B[bank][:, 0:512],
                                    GATED[:, ei, kc, sl * 128:(sl + 1) * 128],
                                    WDr[:, dslot, kc, nh * 512:(nh + 1) * 512],
                                    start=(ei == 0 and kc == 0), stop=(ei == 8 and kc == 3)),
                                    waits=(([gd_wd, (VE, ve_gated)] if sl == 0 and nh == 0 else []) +
                                           ([(VE, ve_bank_free[bank])] if ei == 0 else []))
                                    if kc == 0 else (),
                                    inc=TE if last else None)
                    te_slot_free_d[dslot] = TE.n
                te_dn = TE.n
                for bank in (0, 1, 5, 2, 3, 6, 4, 7):
                    sl, nh = bank // 2, bank % 2
                    s_ = tt * 4 + sl
                    em("vector", lambda e, s_=s_, nh=nh, bank=bank: e.raw.tensor_tensor(
                        XT[:, s_, nh * 512:(nh + 1) * 512], B[bank][:, 0:512],
                        XT[:, s_, nh * 512:(nh + 1) * 512], op=ALU.add),
                        waits=(([(TE, te_dn)] + dbgw()) if bank == 0 else ()), inc=VE)
                    ve_bank_free[bank] = VE.n
                if gidx == 1:
                    ve_fin = VE.n
                    for sl in range(4):
                        s = tt * 4 + sl
                        em("gpsimd", lambda e, s=s: e.dma_start(
                            out=d_y.rearrange("(s p) d -> p s d", p=128)[:, s, :], in_=XT[:, s, :]),
                            waits=[(VE, ve_fin)] if sl == 0 else (), inc=OUT2, k=16)
                    out2_tile[tt] = OUT2.n

        def stats4(e):
            e.tensor_scalar(ST[:, iIS3, :], ST[:, iSQ3, :], inv_d, EPS, op0=ALU.mult, op1=ALU.add)
            e.drain()
            return e.engine_nop()
        em("vector", stats4, waits=[(SE, SE_SQ3)], inc=VE)
        em("scalar", lambda e: e.sqrt(ST[:, iIS3, :], ST[:, iIS3, :]), waits=[(VE, VE.n)], inc=SE)
        def stats5(e):
            e.reciprocal(ST[:, iIS3, :], ST[:, iIS3, :])
            e.drain()
            e.tensor_mul(ST[:, iNI3, :], ST[:, iNM3, :], ST[:, iIS3, :])
            e.drain()
            return e.engine_nop()
        em("vector", stats5, waits=[(SE, SE.n)], inc=VE)
        VE_ST3 = VE.n
        te_rt_d = {}
        r1done = {}
        for tt in range(NTILE):
            p7_tile(tt)
            if tt >= 1:
                p7_b(tt - 1)
        p7_b(NTILE - 1)
        for tt in range(NTILE):
            ffn_tile(tt)

        if dbg:
            em("sync", lambda e: e.dma_start(out=dbg_t["dbg_xnf3"], in_=XNF[:]),
               waits=[(VE, VE_CWT_T[NTILE - 1])], inc=OUT, k=16)
            em("sync", lambda e: e.dma_start(out=dbg_t["dbg_cwt"], in_=CWT[:]),
               waits=[(VE, VE_CWT_T[NTILE - 1])], inc=OUT, k=16)

        def _fin(e):
            e.wait_ge(OUT.h, OUT.n)
            e.wait_ge(OUT2.h, OUT2.n)
        P.ops["sync"].append(_fin)

        for eng_name, deco in (("sync", blk.sync), ("gpsimd", blk.gpsimd), ("tensor", blk.tensor),
                               ("vector", blk.vector), ("scalar", blk.scalar)):
            ops = P.ops[eng_name]
            wrap = eng_name in ("vector", "scalar", "gpsimd")
            def make(ops=ops, wrap=wrap):
                def body(e):
                    if wrap:
                        e = _DrainAfter(e)
                    for op in ops:
                        op(e)
                return body
            deco(make())

    return nc, dbg_t


_NC_CACHE = {}

def _get_nc(dbg=0):
    if dbg not in _NC_CACHE:
        _NC_CACHE[dbg] = build(dbg)
    return _NC_CACHE[dbg]


def kernel(**inputs):
    nc, _ = _get_nc(0)
    hc = _host_consts()
    x = np.ascontiguousarray(inputs["L0_attn_x"])
    base = {
        "ln_w": inputs["L0_attn_ln_w"], "ln_b": inputs["L0_attn_ln_b"],
        "sign_W": inputs["L0_attn_sign_W"], "sign_b": inputs["L0_attn_sign_b"],
        "perm_W": inputs["L0_attn_perm_W"], "perm_b": inputs["L0_attn_perm_b"],
        "gate_W": inputs["L0_attn_gate_W"], "gate_b": inputs["L0_attn_gate_b"],
        "in_W": inputs["L0_attn_in_W"], "in_b": inputs["L0_attn_in_b"],
        "out_W": inputs["L0_attn_out_W"], "out_b": inputs["L0_attn_out_b"],
        "Wr": inputs["L0_ffn_Wr"], "Wg": inputs["L0_ffn_Wg"], "Wu": inputs["L0_ffn_Wu"],
        "Wd": inputs["L0_ffn_Wd"], "Sg": inputs["L0_ffn_Sg"], "Su": inputs["L0_ffn_Su"],
        "Sd": inputs["L0_ffn_Sd"],
    }
    base = {k: np.ascontiguousarray(v) for k, v in base.items()}
    base.update(hc)
    in_maps = [dict(base, x=np.ascontiguousarray(x[b])) for b in range(8)]
    res = run_bass_kernel_spmd(nc, in_maps, core_ids=list(range(8)))
    out = np.stack([res.results[b]["y"] for b in range(8)], axis=0)
    return out.astype(np.float32)


# version 30
# speedup vs baseline: 1.0319x; 1.0318x over previous
import numpy as np
from contextlib import ExitStack

import concourse.bass as bass
import concourse.mybir as mybir
from concourse.bass_utils import run_bass_kernel_spmd

F32 = mybir.dt.float32
BF16 = mybir.dt.bfloat16
AF = mybir.ActivationFunctionType
ALU = mybir.AluOpType
AX = mybir.AxisListType

S, D, H, NN = 2048, 1024, 4, 8
NSL = S // 128
NCH, LCH = 32, 64
EPS = 1e-5
NTILE = 4
NEXP = 16
FEXP = 512

_HOST_CONST = {}
def _host_consts():
    if _HOST_CONST:
        return _HOST_CONST
    _HOST_CONST["c_ident"] = np.eye(128, dtype=np.float32)
    _HOST_CONST["c_ones"] = np.ones((1, 128), dtype=np.float32)
    _HOST_CONST["c_iota"] = np.broadcast_to(
        np.tile(np.arange(8, dtype=np.float32), 64), (128, 512)).copy()
    ida = np.zeros((8, 9), dtype=np.float32)
    ida[:, :8] = np.eye(8, dtype=np.float32)
    _HOST_CONST["c_idaug"] = np.broadcast_to(ida.reshape(-1), (128, 72)).copy()
    oneh = np.zeros((16, 2048), dtype=np.float32)
    for e_ in range(16):
        oneh[e_, e_ * 128:(e_ + 1) * 128] = 1.0
    _HOST_CONST["c_oneh"] = oneh
    pm = np.zeros((128, 128), dtype=np.float32)
    for m in range(128):
        h_, c_ = m // 32, m % 32
        pm[c_ * 4 + h_, m] = 1.0
    _HOST_CONST["c_perm"] = pm
    return _HOST_CONST


class _Sem:
    def __init__(self, name):
        self.name = name
        self.n = 0
        self.h = None


_DRAIN_OPS = {
    "tensor_tensor", "tensor_add", "tensor_sub", "tensor_mul", "tensor_max",
    "tensor_copy", "tensor_scalar", "tensor_single_scalar", "tensor_scalar_add",
    "tensor_scalar_sub", "tensor_scalar_mul", "tensor_scalar_max", "tensor_scalar_min",
    "tensor_relu", "tensor_reduce", "reduce_max", "reduce_sum", "scalar_tensor_tensor",
    "tensor_tensor_scan", "memset", "memzero", "reciprocal", "select", "transpose",
    "activation", "sign", "square", "sqrt", "mul", "copy", "add",
}


class _DrainAfter:
    def __init__(self, e):
        self._e = e

    @property
    def raw(self):
        return self._e

    def __getattr__(self, k):
        f = getattr(self._e, k)
        if k in _DRAIN_OPS:
            def g(*a, **kw):
                r = f(*a, **kw)
                self._e.drain()
                return r
            return g
        return f


class _Prog:
    def __init__(self):
        self.ops = {k: [] for k in ("sync", "gpsimd", "tensor", "vector", "scalar")}

    def emit(self, eng, fn, waits=(), inc=None, k=1):
        waits = tuple((s, v) for (s, v) in waits)
        def cl(e, fn=fn, waits=waits, inc=inc, k=k):
            for (s, v) in waits:
                e.wait_ge(s.h, v)
            r = fn(e)
            if inc is not None:
                r.then_inc(inc.h, k)
        self.ops[eng].append(cl)
        if inc is not None:
            inc.n += k
        return inc.n if inc is not None else None


def build(dbg=0):
    nc = bass.Bass("TRN2", target_bir_lowering=False)
    I = lambda name, shape: nc.dram_tensor(name, list(shape), F32, kind="ExternalInput").ap()

    d_x = I("x", (S, D))
    d_lnw = I("ln_w", (D,)); d_lnb = I("ln_b", (D,))
    d_signW = I("sign_W", (D, 32)); d_signb = I("sign_b", (32,))
    d_permW = I("perm_W", (D, 256)); d_permb = I("perm_b", (256,))
    d_gateW = I("gate_W", (D, 32)); d_gateb = I("gate_b", (32,))
    d_inW = I("in_W", (D, 32)); d_inb = I("in_b", (32,))
    d_outW = I("out_W", (32, D)); d_outb = I("out_b", (D,))
    d_wr = I("Wr", (D, NEXP))
    d_wg = I("Wg", (NEXP, D, FEXP)); d_wu = I("Wu", (NEXP, D, FEXP)); d_wd = I("Wd", (NEXP, FEXP, D))
    d_sg = I("Sg", (D, D)); d_su = I("Su", (D, D)); d_sd = I("Sd", (D, D))
    d_ident = I("c_ident", (128, 128)); d_ones = I("c_ones", (1, 128))
    d_iota = I("c_iota", (128, 512)); d_idaug = I("c_idaug", (128, 72))
    d_oneh = I("c_oneh", (16, 2048))
    d_perm = I("c_perm", (128, 128))

    d_y = nc.dram_tensor("y", [S, D], F32, kind="ExternalOutput").ap()
    d_scrG = nc.dram_tensor("scrG", [18, 128, 8, FEXP], BF16, kind="Internal").ap()
    d_scrU = nc.dram_tensor("scrU", [18, 128, 8, FEXP], BF16, kind="Internal").ap()
    d_scrD = nc.dram_tensor("scrD", [18, 128, 4, D], BF16, kind="Internal").ap()
    dbg_t = {}
    if dbg:
        def DBG(name, shape, dt=F32):
            dbg_t[name] = nc.dram_tensor(name, list(shape), dt, kind="ExternalOutput").ap()
        DBG("dbg_xnf", (128, 8, S), BF16)
        DBG("dbg_projC", (128, LCH, 88))
        DBG("dbg_A", (128, 4096))
        DBG("dbg_hard", (128, 4096))
        DBG("dbg_dt", (128, 512)); DBG("dbg_ct", (128, 512))
        DBG("dbg_mall", (128, 9 * LCH * 8))
        DBG("dbg_hs", (128, 8))
        DBG("dbg_ys", (128, 512), BF16)
        DBG("dbg_xattn", (128, NSL, D))
        DBG("dbg_xnf3", (128, 8, S), BF16)
        DBG("dbg_cwt", (NEXP, S), BF16)
        DBG("dbg_rank", (128, 512)); DBG("dbg_aidx", (128, 512)); DBG("dbg_rmx", (128, 512))

    P = _Prog()
    LDX = _Sem("LDX"); LDP = _Sem("LDP"); GDK = _Sem("GDK")
    GPJ = [_Sem("GPJ0"), _Sem("GPJ1")]; SPJ = [_Sem("SPJ0"), _Sem("SPJ1")]; GSC = _Sem("GSC")
    FWG = [[_Sem("FWG0a"), _Sem("FWG0b")], [_Sem("FWG1a"), _Sem("FWG1b")]]
    FWD = [_Sem("FWD0"), _Sem("FWD1")]
    SCG = [_Sem("SCG0"), _Sem("SCG1")]; SCD = [_Sem("SCD0"), _Sem("SCD1")]
    FWGS = [[_Sem("FWGS0a"), _Sem("FWGS0b")], [_Sem("FWGS1a"), _Sem("FWGS1b")]]
    FWDS = [_Sem("FWDS0"), _Sem("FWDS1")]
    TE = _Sem("TE"); VE = _Sem("VE"); SE = _Sem("SE"); OUT = _Sem("OUT"); OUT2 = _Sem("OUT2")
    GP = _Sem("GP")
    sems = ([LDX, LDP, GDK] + GPJ + SPJ + [GSC] + FWG[0] + FWG[1] + FWD + SCG + SCD +
            FWGS[0] + FWGS[1] + FWDS + [TE, VE, SE, OUT, OUT2, GP])

    with ExitStack() as ctx:
        def SB(name, shape, dt=F32):
            return ctx.enter_context(nc.sbuf_tensor(name, shape, dt))
        XT = SB("XT", [128, NSL, D])
        XNF = SB("XNF", [128, 8, S], BF16)
        ARN = 24064
        ARH = SB("ARENA", [128, ARN])
        AR = ARH[:]
        def carve(off, nw, shape=None, dt=F32, rr=None, **kw):
            v = AR[:, off:off + nw]
            if dt is not F32:
                v = v.bitcast(dt)
            if rr is not None:
                v = v.rearrange(rr, **kw)
            return v
        A = carve(0, 4096)
        SQS = carve(0, 1024)
        HARD4 = carve(4096, 4096, rr="p (t i j) -> p t i j", t=LCH, i=NN)
        HARD = carve(4096, 4096)
        WP = carve(4096, 1408, dt=BF16, rr="p (k w) -> p k w", k=8)
        Y2BF = carve(4096 + 1408, 1024, dt=BF16, rr="p (a d) -> p a d", a=2)
        MALL = carve(8192, 4608, rr="p (t i k) -> p t i k", t=LCH, i=NN)
        WPF = carve(8192, 2816, rr="p (k w) -> p k w", k=8)
        DT = carve(12800, 512, rr="p (t i) -> p t i", t=LCH)
        CT = carve(13312, 512, rr="p (t i) -> p t i", t=LCH)
        PJT = carve(12800, 704, rr="p (a w) -> p a w", a=2)
        PJC = carve(13824, 5632, rr="p (t w) -> p t w", t=LCH)
        SCR = 19456
        SG = carve(SCR + 0, 512)
        RANK = carve(SCR + 0, 512)
        GM = carve(SCR + 512, 512)
        CAV = carve(SCR + 512, 512)
        RMX = carve(SCR + 1024, 512)
        T512a = carve(SCR + 1536, 512)
        T512b = carve(SCR + 2048, 512)
        RS = carve(SCR + 2560, 512)
        AIDX = carve(SCR + 2560, 512)
        ACCR = carve(SCR + 2560, 512)
        CS = carve(SCR + 3072, 512)
        SROW = carve(SCR + 3072, 512)
        YS = carve(SCR + 3072, 256, dt=BF16, rr="p (i t) -> p i t", i=NN)
        CV = carve(SCR + 3584, 512)
        YS2 = carve(SCR + 3584, 256, dt=BF16, rr="p (i t) -> p i t", i=NN)
        IOTA = carve(11008, 512)
        M64 = carve(SCR + 4096, 64)
        CIDX = carve(SCR + 4160, 64)
        PFX = carve(SCR + 4224, 72, rr="p (i k) -> p i k", i=NN)
        PFS = carve(SCR + 4296, 72, rr="p (i k) -> p i k", i=NN)
        NEWP = carve(SCR + 4368, 72, rr="p (i k) -> p i k", i=NN)
        IDA = carve(SCR + 4440, 72)
        HSB = carve(SCR + 4512, 9)
        UCOL = carve(SCR + 4522, 8)
        PFM = carve(SCR + 4532, 64)
        SLN = carve(8192, 512)
        MALL9 = carve(8192, 5184, rr="p (t i k) -> p t i k", t=LCH, i=9)
        AH9 = carve(13824, 4608, rr="p (t i j) -> p t i j", t=LCH, i=NN)
        PRD9 = carve(18432, 648)
        PRD = carve(14976, 648)
        PRDR = carve(0, 4608)
        GATED = carve(0, 9216, dt=BF16, rr="p (e k f) -> p e k f", e=9, k=4)
        WGr = carve(9216, 4096, dt=BF16, rr="p (r k f) -> p r k f", r=2, k=8)
        WUr = carve(13312, 4096, dt=BF16, rr="p (r k f) -> p r k f", r=2, k=8)
        WDr = carve(17408, 4096, dt=BF16, rr="p (r k f) -> p r k f", r=2, k=4)
        SILU = carve(21504, 1536, rr="p (a f) -> p a f", a=3)
        ST = SB("ST", [128, 16, NSL])
        LNWC = SB("LNWC", [128, 8]); LNBC = SB("LNBC", [128, 8])
        BROW = SB("BROW", [1, 352]); BROWB = SB("BROWB", [1, 352], BF16)
        OWB = SB("OWB", [32, D], BF16); OBB = SB("OBB", [1, D], BF16)
        WRB = SB("WRB", [128, 8, NEXP], BF16)
        IDT = SB("IDT", [128, 128], BF16); ONES = SB("ONESb", [1, 128], BF16)
        ONEH = SB("ONEH", [16, 2048], BF16); PERMB = SB("PERMB", [128, 128], BF16)
        E16 = SB("E16", [128, NEXP]); P16 = SB("P16", [128, NEXP]); PW1 = SB("PW1", [128, NEXP])
        OH = SB("OH", [128, NEXP]); CWS = SB("CWS", [128, NEXP]); CWBF = SB("CWBF", [128, NEXP], BF16)
        MX1 = SB("MX1", [128, 1]); MX2 = SB("MX2", [128, 1]); SM1 = SB("SM1", [128, 1])
        CWT = SB("CWT", [NEXP, S], BF16)
        YST = ARH[:][0:32, 8192:9216].bitcast(BF16)
        B = [ctx.enter_context(nc.psum_tensor(f"pb{i}", [128, 512], F32)) for i in range(8)]

        for s_ in sems:
            s_.h = ctx.enter_context(nc.semaphore(s_.name))
        blk = ctx.enter_context(nc.Block())

        iSUM, iNMU, iSSQ, iCMB, iNMC, iA1, iNM1, iS3, iNM3, iSQ3, iIS3, iNI3 = range(12)

        em = P.emit
        def dbgw():
            return [(OUT, OUT.n)] if dbg else []
        for q in range(4):
            em("sync", lambda e, q=q: e.dma_start(out=XT[:, 4*q:4*q+4, :],
                in_=d_x.rearrange("(s p) d -> p s d", p=128)[:, 4*q:4*q+4, :]), inc=LDX, k=16)
        LDX_N = LDX.n

        def wdma(dst_off, width, src, kc):
            def f(e):
                return e.dma_start(
                    out=WPF[:, kc, :].rearrange("p (h w) -> p h w", h=H)[:, :, dst_off:dst_off + width],
                    in_=src[kc * 128:(kc + 1) * 128, :].rearrange("p (h w) -> p h w", h=H))
            return f
        for kc in range(8):
            em("sync", wdma(0, 8, d_signW, kc), inc=LDP, k=16)
            em("sync", wdma(8, 64, d_permW, kc), inc=LDP, k=16)
            em("sync", wdma(72, 8, d_gateW, kc), inc=LDP, k=16)
            em("sync", wdma(80, 8, d_inW, kc), inc=LDP, k=16)
        em("sync", lambda e: e.dma_start(out=LNWC[:], in_=d_lnw.rearrange("(k p) -> p k", p=128), allow_slow_non_contiguous=True), inc=LDP, k=16)
        em("sync", lambda e: e.dma_start(out=LNBC[:], in_=d_lnb.rearrange("(k p) -> p k", p=128), allow_slow_non_contiguous=True), inc=LDP, k=16)
        em("sync", lambda e: e.dma_start(
            out=BROW[:].rearrange("o (h w) -> o h w", h=H)[:, :, 0:8],
            in_=d_signb[None, :].rearrange("o (h w) -> o h w", h=H)), inc=LDP, k=16)
        em("sync", lambda e: e.dma_start(
            out=BROW[:].rearrange("o (h w) -> o h w", h=H)[:, :, 8:72],
            in_=d_permb[None, :].rearrange("o (h w) -> o h w", h=H)), inc=LDP, k=16)
        em("sync", lambda e: e.dma_start(
            out=BROW[:].rearrange("o (h w) -> o h w", h=H)[:, :, 72:80],
            in_=d_gateb[None, :].rearrange("o (h w) -> o h w", h=H)), inc=LDP, k=16)
        em("sync", lambda e: e.dma_start(
            out=BROW[:].rearrange("o (h w) -> o h w", h=H)[:, :, 80:88],
            in_=d_inb[None, :].rearrange("o (h w) -> o h w", h=H)), inc=LDP, k=16)
        em("sync", lambda e: e.dma_start(out=IOTA[:], in_=d_iota), inc=LDP, k=16)
        em("sync", lambda e: e.dma_start(out=IDA[:], in_=d_idaug), inc=LDP, k=16)
        LDW = LDP.n

        em("gpsimd", lambda e: e.dma_start(out=IDT[:], in_=d_ident), inc=GDK, k=16)
        em("gpsimd", lambda e: e.dma_start(out=ONES[:], in_=d_ones), inc=GDK, k=16)
        em("gpsimd", lambda e: e.dma_start(out=OWB[:], in_=d_outW), inc=GDK, k=16)
        em("gpsimd", lambda e: e.dma_start(out=OBB[:], in_=d_outb[None, :]), inc=GDK, k=16)
        em("gpsimd", lambda e: e.dma_start(out=WRB[:], in_=d_wr.rearrange("(k p) n -> p k n", p=128)), inc=GDK, k=16)
        em("gpsimd", lambda e: e.dma_start(out=ONEH[:], in_=d_oneh), inc=GDK, k=16)
        em("gpsimd", lambda e: e.dma_start(out=PERMB[:], in_=d_perm), inc=GDK, k=16)
        GDC = GDK.n

        for kc in range(8):
            em("vector", lambda e, kc=kc: e.tensor_scalar(
                WP[:, kc, :], WPF[:, kc, :], LNWC[:, kc:kc + 1], None, op0=ALU.mult),
                waits=[(LDP, LDW)], inc=VE)
        VE_WP = VE.n
        for kc in range(8):
            em("tensor", lambda e, kc=kc: e.matmul(
                B[7][0:1, 0:352], LNBC[:, kc:kc + 1], WPF[:, kc, :],
                start=(kc == 0), stop=(kc == 7)),
                waits=[(LDP, LDW)] if kc == 0 else (), inc=TE if kc == 7 else None)
        TE_BP = TE.n
        em("vector", lambda e: e.tensor_add(BROW[0:1, :], BROW[0:1, :], B[7][0:1, 0:352]),
           waits=[(TE, TE_BP)], inc=VE)
        em("vector", lambda e: e.tensor_copy(BROWB[0:1, :], BROW[0:1, :]), inc=VE)
        VE_BROW = VE.n

        inv_d = 1.0 / D
        def ln1sums(e):
            e.tensor_reduce(ST[:, iSUM, :], XT[:], axis=AX.X, op=ALU.add)
            e.tensor_scalar(ST[:, iNMU, :], ST[:, iSUM, :], -inv_d, None, op0=ALU.mult)
            return e.raw.engine_nop()
        em("vector", ln1sums, waits=[(LDX, LDX_N)], inc=VE)
        VE_MU = VE.n
        for s in range(NSL):
            em("scalar", lambda e, s=s: e.activation(
                SQS[:], XT[:, s, :], AF.Square, bias=ST[:, iNMU, s:s + 1], scale=1.0,
                accum_out=ST[:, iSSQ, s:s + 1]),
                waits=[(VE, VE_MU)] if s == 0 else (), inc=SE)
        SE_SQ = SE.n
        stv = ST[:, iSSQ, :]
        def stats1(e):
            e.tensor_scalar(ST[:, iSQ3, :], stv, inv_d, EPS, op0=ALU.mult, op1=ALU.add)
            e.drain()
            e.reciprocal(ST[:, iIS3, :], ST[:, iSQ3, :])
            e.drain()
            e.tensor_scalar(ST[:, iNI3, :], stv, inv_d, None, op0=ALU.mult)
            e.tensor_mul(ST[:, iNI3, :], ST[:, iNI3, :], ST[:, iIS3, :])
            e.tensor_scalar(ST[:, iNI3, :], ST[:, iNI3, :], EPS, None, op0=ALU.add)
            e.drain()
            return e.engine_nop()
        em("vector", stats1, waits=[(SE, SE_SQ)], inc=VE)
        def stats2(e):
            e.sqrt(ST[:, iSQ3, :], ST[:, iSQ3, :])
            e.sqrt(ST[:, iNI3, :], ST[:, iNI3, :])
            return e.copy(ST[:, iS3, :], ST[:, iNI3, :])
        em("scalar", stats2, waits=[(VE, VE.n)], inc=SE)
        def stats3(e):
            e.reciprocal(ST[:, iA1, :], ST[:, iSQ3, :])
            e.drain()
            e.reciprocal(ST[:, iCMB, :], ST[:, iNI3, :])
            e.drain()
            e.tensor_mul(ST[:, iCMB, :], ST[:, iCMB, :], ST[:, iA1, :])
            e.tensor_mul(ST[:, iNM1, :], ST[:, iNMU, :], ST[:, iA1, :])
            e.tensor_scalar(ST[:, iA1, :], ST[:, iA1, :], 1.0, None, op0=ALU.add)
            e.tensor_mul(ST[:, iNMC, :], ST[:, iNMU, :], ST[:, iCMB, :])
            e.drain()
            return e.engine_nop()
        em("vector", stats3, waits=[(SE, SE.n)], inc=VE)
        VE_ST1 = VE.n

        TE_TRS = {}
        for s in range(NSL):
            em("scalar", lambda e, s=s: e.raw.activation(
                Y2BF[:, s % 2, :], XT[:, s, :], AF.Identity,
                bias=ST[:, iNMC, s:s + 1], scale=ST[:, iCMB, s:s + 1]),
                waits=[(VE, VE_ST1)] + ([(TE, TE_TRS[s - 2])] if s >= 2 else []), inc=SE)
            se_y = SE.n
            for kc in range(8):
                pb = B[5 + (kc % 2)]
                tw = ([(SE, se_y), (GDK, GDC)] if kc == 0 else [])
                if kc >= 2 or s > 0:
                    tw = tw + [(VE, VE.n)]
                em("tensor", lambda e, s=s, kc=kc, pb=pb: e.transpose(
                    pb[:].bitcast(BF16)[:, 0:128], Y2BF[:, s % 2, kc * 128:(kc + 1) * 128], IDT[:]),
                    waits=tw, inc=TE)
                te_tr = TE.n
                em("vector", lambda e, s=s, kc=kc, pb=pb: e.raw.tensor_copy(
                    XNF[:, kc, s * 128:(s + 1) * 128], pb[:].bitcast(BF16)[:, 0:128]),
                    waits=[(TE, te_tr)], inc=VE)
            TE_TRS[s] = TE.n
        VE_XNF = VE.n
        if dbg:
            em("sync", lambda e: e.dma_start(out=dbg_t["dbg_xnf"], in_=XNF[:]),
               waits=[(VE, VE_XNF)], inc=OUT, k=16)

        SE_PJ = {}
        GD_PJd = {}
        for s in range(NSL):
            pb = B[0 + (s % 2)]
            for kc in range(8):
                em("tensor", lambda e, s=s, kc=kc, pb=pb: e.matmul(
                    pb[:, 0:352], XNF[:, kc, s * 128:(s + 1) * 128], WP[:, kc, :],
                    start=(kc == 0), stop=False),
                    waits=([(VE, VE_XNF), (VE, VE_WP), (VE, VE_BROW), (GDK, GDC)]
                           if s == 0 and kc == 0 else []) +
                          ([(SE, SE_PJ[s - 2])] if s >= 2 and kc == 0 else []))
            em("tensor", lambda e, pb=pb: e.matmul(
                pb[:, 0:352], ONES[0:1, :], BROWB[0:1, :], start=False, stop=True), inc=TE)
            te_p = TE.n
            em("scalar", lambda e, s=s, pb=pb: e.raw.copy(PJT[:, s % 2, :], pb[:, 0:352]),
               waits=[(TE, te_p)] + ([(GPJ[s % 2], GD_PJd[s - 2][0]), (SPJ[s % 2], GD_PJd[s - 2][1])] if s >= 2 else []), inc=SE)
            SE_PJ[s] = SE.n
            for h in range(4):
                qeng = "gpsimd" if h < 2 else "sync"
                qsem = GPJ[s % 2] if h < 2 else SPJ[s % 2]
                for a in range(2):
                    em(qeng, lambda e, s=s, h=h, a=a: e.dma_start(
                        out=PJC[8 * s + 4 * a + h: 8 * s + 4 * a + h + 1],
                        in_=PJT[a * 64:(a + 1) * 64, s % 2, h * 88:(h + 1) * 88]),
                        waits=[(SE, SE_PJ[s])] if a == 0 and h in (0, 2) else (), inc=qsem, k=16)
            GD_PJd[s] = (GPJ[s % 2].n, SPJ[s % 2].n)
        GD_PJ0 = GPJ[0].n; GD_PJ1 = GPJ[1].n; SD_PJ0 = SPJ[0].n; SD_PJ1 = SPJ[1].n
        if dbg:
            em("sync", lambda e: e.dma_start(out=dbg_t["dbg_projC"], in_=PJC[:]),
               waits=[(GPJ[0], GD_PJ0), (GPJ[1], GD_PJ1), (SPJ[0], SD_PJ0), (SPJ[1], SD_PJ1)], inc=OUT, k=16)

        em("scalar", lambda e: e.activation(SG[:].rearrange("p (t i) -> p t i", i=8),
                                            PJC[:, :, 0:8], AF.Sigmoid),
           waits=[(GPJ[0], GD_PJ0), (GPJ[1], GD_PJ1), (SPJ[0], SD_PJ0), (SPJ[1], SD_PJ1)], inc=SE)
        em("scalar", lambda e: e.activation(GM[:].rearrange("p (t i) -> p t i", i=8),
                                            PJC[:, :, 72:80], AF.Sigmoid), inc=SE)
        em("scalar", lambda e: e.activation(A[:].rearrange("p (t ij) -> p t ij", ij=64),
                                            PJC[:, :, 8:72], AF.Exp), inc=SE)
        em("scalar", lambda e: e.copy(RMX[:].rearrange("p (t i) -> p t i", i=8),
                                      PJC[:, :, 80:88]), inc=SE)
        SE_SIG = SE.n
        def dcalc_g(e):
            e.tensor_scalar(T512a[:], SG[:], 2.0, 1.0, op0=ALU.mult, op1=ALU.subtract)
            e.tensor_tensor(DT.rearrange("p t i -> p (t i)"), GM[:], T512a[:], op=ALU.mult)
            e.tensor_scalar(T512b[:], GM[:], -1.0, 1.0, op0=ALU.mult, op1=ALU.add)
            e.tensor_tensor(CT.rearrange("p t i -> p (t i)"), RMX[:], T512b[:], op=ALU.mult)
            return e.raw.nop()
        em("gpsimd", dcalc_g, waits=[(SE, SE_SIG)] + dbgw(), inc=GP)
        GP_DC = GP.n
        def wsrcs(kind, idx):
            if kind == "e":
                return (d_wg[idx].rearrange("(k p) f -> p k f", p=128),
                        d_wu[idx].rearrange("(k p) f -> p k f", p=128),
                        d_wd[idx].rearrange("(k p) f -> p k f", p=128))
            hf = idx
            return (d_sg[:, hf * 512:(hf + 1) * 512].rearrange("(k p) f -> p k f", p=128),
                    d_su[:, hf * 512:(hf + 1) * 512].rearrange("(k p) f -> p k f", p=128),
                    d_sd[hf * 512:(hf + 1) * 512, :].rearrange("(k p) f -> p k f", p=128))
        groups = [[("sh", 0)] + [("e", i) for i in range(8)],
                  [("sh", 1)] + [("e", i) for i in range(8, 16)]]
        PCS = FWD[0]
        def precast(lo, hi):
            for s_idx in range(lo, hi):
                gx, ei = divmod(s_idx, 9)
                kind, idx = ("sh", gx) if ei == 0 else ("e", gx * 8 + ei - 1)
                wg_s, wu_s, wd_s = wsrcs(kind, idx)
                em("gpsimd", lambda e, wg_s=wg_s, s_idx=s_idx: e.dma_start(
                    out=d_scrG[s_idx], in_=wg_s), inc=PCS, k=16)
                em("gpsimd", lambda e, wu_s=wu_s, s_idx=s_idx: e.dma_start(
                    out=d_scrU[s_idx], in_=wu_s), inc=PCS, k=16)
                em("gpsimd", lambda e, wd_s=wd_s, s_idx=s_idx: e.dma_start(
                    out=d_scrD[s_idx], in_=wd_s), inc=PCS, k=16)
        precast(0, 9)
        if dbg:
            em("sync", lambda e: e.dma_start(out=dbg_t["dbg_dt"],
               in_=DT.rearrange("p t i -> p (t i)")), waits=[(GP, GP_DC)], inc=OUT, k=16)
            em("sync", lambda e: e.dma_start(out=dbg_t["dbg_ct"],
               in_=CT.rearrange("p t i -> p (t i)")), waits=[(GP, GP_DC)], inc=OUT, k=16)

        Av = A[:].rearrange("p (t i j) -> p t i j", i=8, j=8)
        RSt = RS[:].rearrange("p (t i) -> p t i", i=8)
        CSt = CS[:].rearrange("p (t j) -> p t j", j=8)
        SH = 144
        def s_recip(buf):
            def f(e):
                e.activation(SLN[:, 0:512 - SH], buf[:, SH:512], AF.Ln)
                e.activation(buf[:, SH:512], SLN[:, 0:512 - SH], AF.Exp, scale=-1.0)
                return e.raw.nop()
            return f
        for it in range(5):
            em("vector", lambda e: e.tensor_reduce(
                RSt, Av.rearrange("p t i j -> p (t i) j"), axis=AX.X, op=ALU.add),
                waits=([(SE, SE_SIG)] + dbgw()) if it == 0 else [], inc=VE)
            v_rr = VE.n
            em("scalar", s_recip(RS[:]), waits=[(VE, v_rr)], inc=SE)
            s_rr = SE.n
            em("vector", lambda e: e.reciprocal(RS[:, 0:SH], RS[:, 0:SH]), inc=VE)
            em("vector", lambda e: e.tensor_tensor(
                Av.rearrange("p t i j -> p (t i) j"),
                Av.rearrange("p t i j -> p (t i) j"),
                RS[:][:, :, None].broadcast_to([128, 512, 8]), op=ALU.mult),
                waits=[(SE, s_rr)], inc=VE)
            em("vector", lambda e: e.tensor_reduce(
                CSt, Av.rearrange("p t i j -> p t j i"), axis=AX.X, op=ALU.add), inc=VE)
            v_cr = VE.n
            em("scalar", s_recip(CS[:]), waits=[(VE, v_cr)], inc=SE)
            s_cr = SE.n
            em("vector", lambda e: e.reciprocal(CS[:, 0:SH], CS[:, 0:SH]), inc=VE)
            em("vector", lambda e: e.tensor_tensor(
                Av, Av, CSt[:, :, None, :].broadcast_to([128, LCH, 8, 8]), op=ALU.mult),
                waits=[(SE, s_cr)], inc=VE)
        GP_SK = GP.n
        VE_SK = VE.n
        if dbg:
            em("sync", lambda e: e.dma_start(out=dbg_t["dbg_A"], in_=A[:]),
               waits=[(VE, VE_SK), (GP, GP_SK)], inc=OUT, k=16)

        TG = 56
        c_tg = TG * 8
        ng = LCH - TG
        SROW7 = carve(13824, 3584)
        GT1 = carve(17408, ng * 32, rr="p (t i j) -> p t i j", t=ng, i=4)
        GT2 = carve(17408 + ng * 32, ng * 16, rr="p (t i j) -> p t i j", t=ng, i=2)
        def grank(e):
            e.tensor_reduce(RMX[:], Av.rearrange("p t i j -> p (t i) j"), axis=AX.X, op=ALU.max)
            e.memset(RANK[:], 0.0)
            Rv = RMX[:].rearrange("p (t i) -> p t i", i=8)
            Kv = RANK[:].rearrange("p (t i) -> p t i", i=8)
            Ta = T512a[:].rearrange("p (t i) -> p t i", i=8)
            Tb = T512b[:].rearrange("p (t i) -> p t i", i=8)
            for dd in range(1, 8):
                e.tensor_tensor(Ta[:, :, 0:8 - dd], Rv[:, :, 0:8 - dd], Rv[:, :, dd:8],
                                op=ALU.is_ge)
                e.tensor_tensor(Kv[:, :, dd:8], Kv[:, :, dd:8], Ta[:, :, 0:8 - dd], op=ALU.add)
                e.tensor_tensor(Tb[:, :, 0:8 - dd], Rv[:, :, dd:8], Rv[:, :, 0:8 - dd],
                                op=ALU.is_gt)
                e.tensor_tensor(Kv[:, :, 0:8 - dd], Kv[:, :, 0:8 - dd], Tb[:, :, 0:8 - dd],
                                op=ALU.add)
            e.memset(CAV[:], 1.0)
            e.memset(AIDX[:], 0.0)
            return e.raw.engine_nop()
        em("vector", grank, waits=[(VE, VE_SK), (GP, GP_DC), (LDP, LDW)] + dbgw(), inc=VE)
        gm_done = {}
        for r in range(7):
            mbuf = (T512a, T512b)[r % 2]
            em("vector", lambda e, r=r, mbuf=mbuf: e.tensor_single_scalar(
                mbuf[:], RANK[:], float(r), op=ALU.is_equal),
                waits=[(GP, gm_done[r - 2])] if r >= 2 else [], inc=VE)
            v_mk = VE.n
            def gext(e, r=r, mbuf=mbuf):
                Hg = HARD4[:, TG:LCH]
                e.tensor_tensor(Hg.rearrange("p t i j -> p (t i) j"),
                                Av[:, TG:LCH].rearrange("p t i j -> p (t i) j"),
                                mbuf[:, c_tg:512][:, :, None].broadcast_to([128, ng * 8, 8]),
                                op=ALU.mult)
                e.tensor_tensor(GT1, Hg[:, :, 0:4, :], Hg[:, :, 4:8, :], op=ALU.add)
                e.tensor_tensor(GT2, GT1[:, :, 0:2, :], GT1[:, :, 2:4, :], op=ALU.add)
                e.tensor_tensor(SROW7[:, r * 512 + c_tg:(r + 1) * 512]
                                .rearrange("p (t j) -> p t j", j=8),
                                GT2[:, :, 0, :], GT2[:, :, 1, :], op=ALU.add)
                return e.raw.nop()
            em("gpsimd", gext, waits=[(VE, v_mk)], inc=GP)
            gm_done[r] = GP.n
            em("vector", lambda e, mbuf=mbuf: e.tensor_tensor(
                HARD4[:, 0:TG].rearrange("p t i j -> p (t i) j"),
                Av[:, 0:TG].rearrange("p t i j -> p (t i) j"),
                mbuf[:, 0:c_tg][:, :, None].broadcast_to([128, c_tg, 8]), op=ALU.mult), inc=VE)
            em("vector", lambda e, r=r: e.tensor_reduce(
                SROW7[:, r * 512:r * 512 + c_tg].rearrange("p (t j) -> p t j", j=8),
                HARD4[:, 0:TG].rearrange("p t i j -> p t j i"), axis=AX.X, op=ALU.max),
                inc=VE)
        def grounds(e):
            BIG = 1.0e30
            srow = SROW[:].rearrange("p (t j) -> p t j", j=8)
            cv = CV[:].rearrange("p (t j) -> p t j", j=8)
            Ta = T512a[:].rearrange("p (t i) -> p t i", i=8)
            Tb = T512b[:].rearrange("p (t i) -> p t i", i=8)
            for r in range(8):
                if r < 7:
                    e.tensor_single_scalar(T512a[:], RANK[:], float(r), op=ALU.is_equal)
                    e.tensor_scalar(CV[:], CAV[:], 1.0, BIG, op0=ALU.subtract, op1=ALU.mult)
                    e.tensor_add(CV[:], CV[:], SROW7[:, r * 512:(r + 1) * 512])
                    e.tensor_reduce(M64[:], cv, axis=AX.X, op=ALU.max)
                    e.tensor_tensor(Tb, cv, M64[:][:, :, None].broadcast_to([128, LCH, 8]),
                                    op=ALU.is_equal)
                else:
                    e.tensor_single_scalar(T512a[:], RANK[:], 7.0, op=ALU.is_equal)
                    e.tensor_copy(T512b[:], CAV[:])
                e.tensor_mul(SROW[:], T512b[:], IOTA[:])
                e.tensor_reduce(CIDX[:], srow, axis=AX.X, op=ALU.add)
                e.tensor_tensor(cv, Ta, CIDX[:][:, :, None].broadcast_to([128, LCH, 8]),
                                op=ALU.mult)
                e.tensor_add(AIDX[:], AIDX[:], CV[:])
                if r < 7:
                    e.tensor_sub(CAV[:], CAV[:], T512b[:])
            e.drain()
            for j in range(8):
                e.tensor_single_scalar(
                    AH9[:, :, :, j].rearrange("p t i -> p (t i)"),
                    AIDX[:], float(j), op=ALU.is_equal)
            e.drain()
            e.tensor_tensor(AH9[:, 0:TG, :, 0:8].rearrange("p t i j -> p (t i) j"),
                            AH9[:, 0:TG, :, 0:8].rearrange("p t i j -> p (t i) j"),
                            DT.rearrange("p t i -> p (t i)")[:, 0:c_tg][:, :, None]
                            .broadcast_to([128, c_tg, 8]),
                            op=ALU.mult)
            e.drain()
            return e.raw.engine_nop()
        em("vector", grounds, waits=[(GP, gm_done[6])], inc=VE)
        v_oh = VE.n
        em("gpsimd", lambda e: e.tensor_tensor(
            AH9[:, TG:LCH, :, 0:8].rearrange("p t i j -> p (t i) j"),
            AH9[:, TG:LCH, :, 0:8].rearrange("p t i j -> p (t i) j"),
            DT.rearrange("p t i -> p (t i)")[:, c_tg:512][:, :, None]
            .broadcast_to([128, ng * 8, 8]),
            op=ALU.mult), waits=[(VE, v_oh)], inc=GP)
        GP_GR = GP.n
        precast(9, 18)
        PC_ALL = PCS.n
        VE_GR = VE.n
        if dbg:
            em("sync", lambda e: e.dma_start(out=dbg_t["dbg_hard"],
               in_=AH9[:, :, :, 0:8].rearrange("p t i j -> p (t i) j")),
               waits=[(VE, VE_GR), (GP, GP_GR)], inc=OUT, k=16)
            em("sync", lambda e: e.dma_start(out=dbg_t["dbg_rank"], in_=RANK[:]),
               waits=[(VE, VE_GR), (GP, GP_GR)], inc=OUT, k=16)
            em("sync", lambda e: e.dma_start(out=dbg_t["dbg_aidx"], in_=AIDX[:]),
               waits=[(VE, VE_GR), (GP, GP_GR)], inc=OUT, k=16)
            em("sync", lambda e: e.dma_start(out=dbg_t["dbg_rmx"], in_=RMX[:]),
               waits=[(VE, VE_GR), (GP, GP_GR)], inc=OUT, k=16)

        def compose(e):
            e.tensor_copy(AH9[:, :, :, 8].rearrange("p t i -> p (t i)"),
                          CT.rearrange("p t i -> p (t i)"))
            e.memset(MALL9[:, :, 8, 0:8], 0.0)
            e.memset(MALL9[:, :, 8, 8], 1.0)
            e.tensor_copy(MALL9[:, 0, 0:8, :], AH9[:, 0, :, :])
            e.drain()
            for t in range(1, LCH):
                e.tensor_tensor(PRD9[:, 0:648].rearrange("p (i j k) -> p i j k", i=8, j=9),
                                AH9[:, t, :, :][:, :, :, None].broadcast_to([128, 8, 9, 9]),
                                MALL9[:, t - 1][:, None, :, :].broadcast_to([128, 8, 9, 9]),
                                op=ALU.mult)
                e.tensor_reduce(MALL9[:, t, 0:8, :],
                                PRD9[:, 0:648].rearrange("p (i j k) -> p i k j", i=8, j=9),
                                axis=AX.X, op=ALU.add)
            e.drain()
            e.tensor_copy(PFX[:], MALL9[:, LCH - 1, 0:8, :].rearrange("p i k -> p (i k)"))
            e.tensor_copy(PFM.rearrange("p (a j) -> p a j", a=8), MALL9[:, LCH - 1, 0:8, 0:8])
            return e.raw.engine_nop()
        em("vector", compose, waits=[(VE, VE_GR), (GP, GP_GR)] + dbgw(), inc=VE)
        VE_CMP = VE.n

        ve_prev = VE_CMP
        for lvl in range(5):
            dlt = 4 * (1 << lvl)
            em("gpsimd", lambda e, dlt=dlt: e.dma_start(out=PFS[dlt:128, :, :], in_=PFX[0:128 - dlt, :, :]),
               waits=[(VE, ve_prev)], inc=GSC, k=16)
            em("gpsimd", lambda e, dlt=dlt: e.dma_start(
                out=PFS[0:dlt, :, :].rearrange("p k i -> p (k i)"), in_=IDA[0:dlt, :]),
               inc=GSC, k=16)
            gd_sh = GSC.n
            def hslvl(e):
                e.tensor_tensor(PRD[:, 0:576].rearrange("p (x k) -> p x k", k=9),
                                PFM[:, :, None].broadcast_to([128, 64, 9]),
                                PFS[:].rearrange("p j k -> p (j k)")[:, None, :]
                                .broadcast_to([128, 8, 72]),
                                op=ALU.mult)
                e.tensor_reduce(NEWP[:],
                                PRD[:, 0:576].rearrange("p (a j k) -> p a k j", a=8, j=8),
                                axis=AX.X, op=ALU.add)
                e.tensor_add(NEWP[:, :, 8], NEWP[:, :, 8], PFX[:, :, 8])
                e.tensor_copy(PFX[:], NEWP[:])
                e.tensor_copy(PFM.rearrange("p (a j) -> p a j", a=8), NEWP[:, :, 0:8])
                return e.raw.engine_nop()
            em("vector", hslvl, waits=[(GSC, gd_sh)], inc=VE)
            ve_prev = VE.n
        VE_HS = VE.n

        em("vector", lambda e: e.tensor_copy(UCOL[:], PFX[:, :, 8]), waits=[(VE, VE_HS)], inc=VE)
        VE_UC = VE.n
        em("gpsimd", lambda e: e.dma_start(out=HSB[4:128, 0:8], in_=UCOL[0:124, :]),
           waits=[(VE, VE_UC)], inc=GSC, k=16)
        GD_HS = GSC.n
        def hszero(e):
            e.memset(HSB[0:4, 0:8], 0.0)
            e.memset(HSB[:, 8:9], 1.0)
            return e.raw.engine_nop()
        em("vector", hszero, waits=[(VE, VE_HS)], inc=VE)
        VE_HS0 = VE.n
        if dbg:
            em("sync", lambda e: e.dma_start(out=dbg_t["dbg_mall"],
               in_=MALL9[:, :, 0:8, :].rearrange("p t i k -> p t (i k)")),
               waits=[(VE, VE_CMP)], inc=OUT, k=16)
            em("sync", lambda e: e.dma_start(out=dbg_t["dbg_hs"], in_=HSB[:, 0:8]),
               waits=[(GSC, GD_HS), (VE, VE_HS0)], inc=OUT, k=16)

        def recon(e):
            e.tensor_tensor(PRDR[:, 0:4608].rearrange("p (t i k) -> p t i k", t=LCH, i=NN),
                            MALL9[:, :, 0:8, :],
                            HSB[:][:, None, None, :].broadcast_to([128, LCH, 8, 9]),
                            op=ALU.mult)
            e.tensor_reduce(ACCR[:],
                            PRDR[:, 0:4608].rearrange("p (t i k) -> p t i k", t=LCH, i=NN),
                            axis=AX.X, op=ALU.add)
            e.tensor_copy(YS, ACCR[:].rearrange("p (t i) -> p i t", i=NN))
            return e.raw.engine_nop()
        em("vector", recon, waits=[(GSC, GD_HS), (VE, VE_HS0)] + dbgw(), inc=VE)
        VE_YS = VE.n
        if dbg:
            em("sync", lambda e: e.dma_start(out=dbg_t["dbg_ys"],
               in_=YS.rearrange("p i t -> p (i t)")), waits=[(VE, VE_YS)], inc=OUT, k=16)

        em("tensor", lambda e: e.matmul(
            B[5][:, 0:512], PERMB[:], YS.rearrange("p i t -> p (i t)"), start=True, stop=True),
            waits=[(VE, VE_YS), (GDK, GDC)], inc=TE)
        te_pm = TE.n
        em("vector", lambda e: e.raw.tensor_copy(YS2.rearrange("p i t -> p (i t)"), B[5][:, 0:512]),
           waits=[(TE, te_pm)], inc=VE)
        VE_YS2 = VE.n
        for h in range(4):
            for i in range(NN):
                qeng = "gpsimd" if h < 2 else "sync"
                qsem = GSC if h < 2 else SPJ[0]
                em(qeng, lambda e, h=h, i=i: e.dma_start(
                    out=YST[h * 8 + i: h * 8 + i + 1, :].rearrange("q (c t) -> q c t", c=32),
                    in_=YS2[h * 32:(h + 1) * 32, i, :]),
                    waits=([(VE, VE_YS2)] + dbgw()) if i == 0 and h in (0, 2) else (),
                    inc=qsem, k=16)
        GD_YST = (GSC.n, SPJ[0].n)

        VE_RES = {}
        TE_TR3 = {}
        VE_CWT_T = {}
        out2_tile = {}
        RT7 = carve(23040, 64); RE7 = carve(23104, 64); ROH7 = carve(23168, 64)
        RPW7 = carve(23232, 64); RCW7 = carve(23296, 64)
        CWB7 = carve(23360, 32, dt=BF16)
        RM4 = carve(23392, 4); CI4 = carve(23396, 4)
        SQDMP = carve(1024, 1024)
        ve_bank_free = {i: VE.n for i in range(8)}
        ve_cwt_prev = VE.n
        ev56 = {5: (VE, 0), 6: (VE, 0)}
        te_slot_free_gu = {0: TE.n, 1: TE.n}
        te_slot_free_d = {0: TE.n, 1: TE.n}
        gu_loads = {}; gu_loads_u = {}
        gi_counter = 0
        scg_w = {0: 0, 1: 0}; scd_w = {0: 0, 1: 0}
        scg_done = {}; scd_done = {}

        for s in range(NSL):
            pb0, pb1 = ((B[2], B[3]), (B[5], B[6]), (B[0], B[1]))[s % 3]
            em("tensor", lambda e, s=s, pb0=pb0: e.matmul(
                pb0[:, 0:512], YST[:, s * 128:(s + 1) * 128], OWB[:, 0:512], start=True, stop=False),
                waits=[(GSC, GD_YST[0]), (SPJ[0], GD_YST[1]), (GDK, GDC)] if s == 0 else
                      ([(VE, VE_RES[s - 3])] if s >= 3 else []))
            em("tensor", lambda e, pb0=pb0: e.matmul(
                pb0[:, 0:512], ONES[0:1, :], OBB[0:1, 0:512], start=False, stop=True))
            em("tensor", lambda e, s=s, pb1=pb1: e.matmul(
                pb1[:, 0:512], YST[:, s * 128:(s + 1) * 128], OWB[:, 512:1024], start=True, stop=False))
            em("tensor", lambda e, pb1=pb1: e.matmul(
                pb1[:, 0:512], ONES[0:1, :], OBB[0:1, 512:1024], start=False, stop=True), inc=TE)
            te_o = TE.n
            def resid(e, s=s, pb0=pb0, pb1=pb1):
                for nh in range(2):
                    pb = pb0 if nh == 0 else pb1
                    e.scalar_tensor_tensor(SQS[:, nh * 512:(nh + 1) * 512],
                                           XT[:, s, nh * 512:(nh + 1) * 512], ST[:, iA1, s:s + 1],
                                           pb[:, 0:512], op0=ALU.mult, op1=ALU.add)
                    e.tensor_scalar(XT[:, s, nh * 512:(nh + 1) * 512],
                                    SQS[:, nh * 512:(nh + 1) * 512], ST[:, iNM1, s:s + 1], None,
                                    op0=ALU.add)
                return e.raw.engine_nop()
            em("vector", resid, waits=[(TE, te_o)] + (dbgw() if s == 0 else []), inc=VE)
            VE_RES[s] = VE.n
            def ln3s(e, s=s):
                e.activation(SQDMP[:], XT[:, s, :], AF.Identity,
                             accum_out=ST[:, iS3, s:s + 1])
                e.activation(ST[:, iNM3, s:s + 1], ST[:, iS3, s:s + 1], AF.Identity,
                             scale=-inv_d)
                e.activation(SQDMP[:], XT[:, s, :], AF.Square,
                             bias=ST[:, iNM3, s:s + 1], scale=1.0,
                             accum_out=ST[:, iSQ3, s:s + 1])
                return e.raw.nop()
            em("scalar", ln3s, waits=[(VE, VE_RES[s])], inc=SE)
        SE_SQ3 = SE.n
        ve_bank_free = {i: VE.n for i in range(8)}
        if dbg:
            em("sync", lambda e: e.dma_start(out=dbg_t["dbg_xattn"], in_=XT[:]),
               waits=[(VE, VE_RES[NSL - 1])], inc=OUT, k=16)

        def p7_tile(tt):
            nonlocal ve_cwt_prev
            Y2B7 = Y2BF
            rb = (tt % 3) * 64
            if tt == 0:
                te_slot_free_gu[0] = te_slot_free_gu[1] = TE.n
                te_slot_free_d[0] = te_slot_free_d[1] = TE.n

            for si in range(4):
                s = tt * 4 + si
                em("scalar", lambda e, s=s, Y2B7=Y2B7: e.raw.activation(
                    Y2B7[:, s % 2, :], XT[:, s, :], AF.Identity,
                    bias=ST[:, iNI3, s:s + 1], scale=ST[:, iIS3, s:s + 1]),
                    waits=[(VE, VE_ST3)] + ([(TE, TE_TR3[s - 2])] if s >= 2 else []) + (dbgw() if s == 0 else []), inc=SE)
                se_y = SE.n
                for kc in range(8):
                    pb = B[5 + (kc % 2)]
                    tw = ([(SE, se_y)] if kc == 0 else [])
                    if kc >= 2 or s > 0:
                        tw = tw + [ev56[5 + (kc % 2)]]
                    em("tensor", lambda e, s=s, kc=kc, pb=pb, Y2B7=Y2B7: e.transpose(
                        pb[:].bitcast(BF16)[:, 0:128], Y2B7[:, s % 2, kc * 128:(kc + 1) * 128], IDT[:]),
                        waits=tw, inc=TE)
                    te_tr = TE.n
                    if kc % 2 == 0 or kc == 7:
                        em("vector", lambda e, s=s, kc=kc, pb=pb: e.raw.tensor_copy(
                            XNF[:, kc, s * 128:(s + 1) * 128], pb[:].bitcast(BF16)[:, 0:128]),
                            waits=[(TE, te_tr)], inc=VE)
                    else:
                        em("scalar", lambda e, s=s, kc=kc, pb=pb: e.raw.copy(
                            XNF[:, kc, s * 128:(s + 1) * 128], pb[:].bitcast(BF16)[:, 0:128]),
                            waits=[(TE, te_tr)], inc=SE)
                    ev56[5 + (kc % 2)] = (VE, VE.n) if (kc % 2 == 0 or kc == 7) else (SE, SE.n)
                TE_TR3[s] = TE.n
                ve_tr = VE.n
                se_tr = SE.n
                ve_bank_free[5] = VE.n
                ve_bank_free[6] = VE.n
                for kc in range(8):
                    em("tensor", lambda e, s=s, si=si, kc=kc, rb=rb: e.matmul(
                        B[4][:, rb + si * 16:rb + (si + 1) * 16],
                        XNF[:, kc, s * 128:(s + 1) * 128], WRB[:, kc, :],
                        start=(kc == 0), stop=(kc == 7)),
                        waits=(([(VE, ve_tr), (SE, se_tr)] +
                                ([(VE, r1done[tt - 3])] if si == 0 and tt >= 3 else []))
                               if kc == 0 else ()),
                        inc=TE if kc == 7 else None)
            te_rt_d[tt] = TE.n
        def p7_b(tt):
            nonlocal ve_cwt_prev
            rb = (tt % 3) * 64
            def rsoft1(e, rb=rb):
                e.tensor_reduce(RM4[:], B[4][:, rb:rb + 64].rearrange("p (s x) -> p s x", x=16),
                                axis=AX.X, op=ALU.max)
                e.tensor_tensor(RT7[:].rearrange("p (s x) -> p s x", x=16),
                                B[4][:, rb:rb + 64].rearrange("p (s x) -> p s x", x=16),
                                RM4[:][:, :, None].broadcast_to([128, 4, 16]),
                                op=ALU.subtract)
                return e.raw.engine_nop()
            em("vector", rsoft1, waits=[(TE, te_rt_d[tt])], inc=VE)
            r1done[tt] = VE.n
            em("scalar", lambda e: e.activation(RE7[:], RT7[:], AF.Exp),
               waits=[(VE, VE.n)], inc=SE)
            SE_REXP = SE.n
            def rsoft2(e):
                ev = lambda a: a[:].rearrange("p (s x) -> p s x", x=16)
                bc = lambda m: m[:][:, :, None].broadcast_to([128, 4, 16])
                e.tensor_reduce(CI4[:], ev(RE7), axis=AX.X, op=ALU.add)
                e.reciprocal(CI4[:], CI4[:])
                e.tensor_tensor(ev(RE7), ev(RE7), bc(CI4), op=ALU.mult)
                e.tensor_reduce(RM4[:], ev(RE7), axis=AX.X, op=ALU.max)
                e.tensor_tensor(ev(ROH7), ev(RE7), bc(RM4), op=ALU.is_equal)
                e.tensor_mul(RPW7[:], RE7[:], ROH7[:])
                e.tensor_sub(RPW7[:], RE7[:], RPW7[:])
                e.tensor_reduce(RM4[:], ev(RPW7), axis=AX.X, op=ALU.max)
                e.tensor_tensor(ev(RCW7), ev(RPW7), bc(RM4), op=ALU.is_equal)
                e.tensor_add(ROH7[:], ROH7[:], RCW7[:])
                e.tensor_mul(RCW7[:], RE7[:], ROH7[:])
                return e.raw.engine_nop()
            em("vector", rsoft2, waits=[(SE, SE_REXP)], inc=VE)
            ve_bank_free[4] = VE.n
            em("scalar", lambda e: e.raw.copy(CWB7[:], RCW7[:]), waits=[(VE, VE.n)], inc=SE)
            SE_CWB = SE.n
            for si in range(4):
                s = tt * 4 + si
                em("tensor", lambda e, si=si: e.transpose(
                    B[7][:].bitcast(BF16)[0:NEXP, 0:128], CWB7[:, si * 16:(si + 1) * 16], IDT[:]),
                    waits=[(SE, SE_CWB), (VE, ve_cwt_prev)] +
                          ([(VE, ve_bank_free[7])] if si == 0 else []), inc=TE)
                te_cw = TE.n
                em("vector", lambda e, s=s: e.raw.tensor_copy(
                    CWT[0:NEXP, s * 128:(s + 1) * 128],
                    B[7][:].bitcast(BF16)[0:NEXP, 0:128]), waits=[(TE, te_cw)], inc=VE)
                ve_cwt_prev = VE.n
            ve_bank_free[7] = VE.n
            VE_CWT_T[tt] = VE.n

        def ffn_tile(tt):
            nonlocal gi_counter
            tok = slice(tt * 512, (tt + 1) * 512)
            for gidx, grp in enumerate(groups):
                for ei, (kind, idx) in enumerate(grp):
                    gslot = gi_counter % 2
                    wg_s, wu_s, wd_s = wsrcs(kind, idx)
                    s_idx = gidx * 9 + ei
                    for hf_ in range(2):
                        em("sync", lambda e, s_idx=s_idx, gslot=gslot, hf_=hf_: e.dma_start(
                            out=WGr[:, gslot, :, hf_ * 256:(hf_ + 1) * 256],
                            in_=d_scrG[s_idx][:, :, hf_ * 256:(hf_ + 1) * 256]),
                            waits=([(TE, te_slot_free_gu[gslot])] +
                                   ([(PCS, PC_ALL)] if tt == 0 else []) +
                                   (dbgw() if tt == 0 and gi_counter == 0 else [])) if hf_ == 0 else (),
                            inc=FWGS[gslot][hf_], k=16)
                        em("sync", lambda e, s_idx=s_idx, gslot=gslot, hf_=hf_: e.dma_start(
                            out=WUr[:, gslot, :, hf_ * 256:(hf_ + 1) * 256],
                            in_=d_scrU[s_idx][:, :, hf_ * 256:(hf_ + 1) * 256]),
                            inc=FWGS[gslot][hf_], k=16)
                        gu_loads[(gi_counter, hf_)] = (FWGS[gslot][hf_], FWGS[gslot][hf_].n)
                        gu_loads_u[(gi_counter, hf_)] = gu_loads[(gi_counter, hf_)]
                    has_cw = (kind == "e")
                    bcb = 4 if gi_counter % 2 == 0 else 7
                    if has_cw:
                        em("tensor", lambda e, idx=idx, tok=tok, bcb=bcb: e.matmul(
                            B[bcb][:, 0:512], ONEH[:, idx * 128:(idx + 1) * 128], CWT[0:NEXP, tok],
                            start=True, stop=True),
                            waits=[(VE, VE_CWT_T[tt]), (VE, ve_bank_free[bcb])], inc=TE)
                    te_bc = TE.n
                    for mc in range(4):
                        rr = (gi_counter * 4 + mc) % 3
                        gb = (0, 1, 5)[rr]
                        ub = (2, 3, 6)[rr]
                        for kc in range(8):
                            em("tensor", lambda e, gslot=gslot, kc=kc, mc=mc, tok=tok, gb=gb: e.matmul(
                                B[gb][:, 0:512], WGr[:, gslot, kc, mc * 128:(mc + 1) * 128],
                                XNF[:, kc, tok], start=(kc == 0), stop=(kc == 7)),
                                waits=(([gu_loads[(gi_counter, mc // 2)],
                                         (VE, ve_bank_free[gb])] +
                                        ([(VE, VE_CWT_T[tt])] if ei == 0 and mc == 0 else []))
                                       if kc == 0 else ()),
                                inc=TE if kc == 7 else None)
                        te_g = TE.n
                        for kc in range(8):
                            em("tensor", lambda e, gslot=gslot, kc=kc, mc=mc, tok=tok, ub=ub: e.matmul(
                                B[ub][:, 0:512], WUr[:, gslot, kc, mc * 128:(mc + 1) * 128],
                                XNF[:, kc, tok], start=(kc == 0), stop=(kc == 7)),
                                waits=[(VE, ve_bank_free[ub])] if kc == 0 else (),
                                inc=TE if kc == 7 else None)
                        te_u = TE.n
                        em("scalar", lambda e, rr=rr, gb=gb: e.raw.activation(
                            SILU[:, rr, :], B[gb][:, 0:512], AF.Silu),
                            waits=[(TE, te_g)], inc=SE)
                        se_si = SE.n
                        def gevac(e, mc=mc, ub=ub, ei=ei, has_cw=has_cw, bcb=bcb, rr=rr):
                            sc = SILU[:, rr, :]
                            e.tensor_tensor(sc, sc, B[ub][:, 0:512], op=ALU.mult)
                            if has_cw:
                                e.raw.tensor_tensor(GATED[:, ei, mc, :], sc, B[bcb][:, 0:512],
                                                    op=ALU.mult)
                            else:
                                e.raw.tensor_copy(GATED[:, ei, mc, :], sc)
                            return e.raw.engine_nop()
                        em("vector", gevac, waits=[(SE, se_si), (TE, te_u)] +
                           ([(TE, te_bc)] if has_cw and mc == 0 else []) +
                           (dbgw() if gi_counter == 0 and mc == 0 else []), inc=VE)
                        ve_bank_free[gb] = VE.n
                        ve_bank_free[ub] = VE.n
                        if has_cw:
                            ve_bank_free[bcb] = VE.n
                    te_slot_free_gu[gslot] = TE.n
                    gi_counter += 1
                ve_gated = VE.n
                for ei, (kind, idx) in enumerate(grp):
                    dslot = ei % 2
                    wg_s, wu_s, wd_s = wsrcs(kind, idx)
                    s_idx = gidx * 9 + ei
                    em("sync", lambda e, s_idx=s_idx, dslot=dslot: e.dma_start(
                        out=WDr[:, dslot, :, :], in_=d_scrD[s_idx]),
                        waits=[(TE, te_slot_free_d[dslot])] +
                              ([(PCS, PC_ALL)] if tt == 0 else []),
                        inc=FWDS[dslot], k=16)
                    gd_wd = (FWDS[dslot], FWDS[dslot].n)
                    for sl in range(4):
                        for nh in range(2):
                            bank = sl * 2 + nh
                            for kc in range(4):
                                last = (sl == 3 and nh == 1 and kc == 3)
                                em("tensor", lambda e, ei=ei, sl=sl, nh=nh, kc=kc, dslot=dslot, bank=bank: e.matmul(
                                    B[bank][:, 0:512],
                                    GATED[:, ei, kc, sl * 128:(sl + 1) * 128],
                                    WDr[:, dslot, kc, nh * 512:(nh + 1) * 512],
                                    start=(ei == 0 and kc == 0), stop=(ei == 8 and kc == 3)),
                                    waits=(([gd_wd, (VE, ve_gated)] if sl == 0 and nh == 0 else []) +
                                           ([(VE, ve_bank_free[bank])] if ei == 0 else []))
                                    if kc == 0 else (),
                                    inc=TE if last else None)
                    te_slot_free_d[dslot] = TE.n
                te_dn = TE.n
                for bank in (0, 1, 5, 2, 3, 6, 4, 7):
                    sl, nh = bank // 2, bank % 2
                    s_ = tt * 4 + sl
                    em("vector", lambda e, s_=s_, nh=nh, bank=bank: e.raw.tensor_tensor(
                        XT[:, s_, nh * 512:(nh + 1) * 512], B[bank][:, 0:512],
                        XT[:, s_, nh * 512:(nh + 1) * 512], op=ALU.add),
                        waits=(([(TE, te_dn)] + dbgw()) if bank == 0 else ()), inc=VE)
                    ve_bank_free[bank] = VE.n
                if gidx == 1:
                    ve_fin = VE.n
                    for sl in range(4):
                        s = tt * 4 + sl
                        em("gpsimd", lambda e, s=s: e.dma_start(
                            out=d_y.rearrange("(s p) d -> p s d", p=128)[:, s, :], in_=XT[:, s, :]),
                            waits=[(VE, ve_fin)] if sl == 0 else (), inc=OUT2, k=16)
                    out2_tile[tt] = OUT2.n

        def stats4(e):
            e.tensor_scalar(ST[:, iIS3, :], ST[:, iSQ3, :], inv_d, EPS, op0=ALU.mult, op1=ALU.add)
            e.drain()
            return e.engine_nop()
        em("vector", stats4, waits=[(SE, SE_SQ3)], inc=VE)
        em("scalar", lambda e: e.sqrt(ST[:, iIS3, :], ST[:, iIS3, :]), waits=[(VE, VE.n)], inc=SE)
        def stats5(e):
            e.reciprocal(ST[:, iIS3, :], ST[:, iIS3, :])
            e.drain()
            e.tensor_mul(ST[:, iNI3, :], ST[:, iNM3, :], ST[:, iIS3, :])
            e.drain()
            return e.engine_nop()
        em("vector", stats5, waits=[(SE, SE.n)], inc=VE)
        VE_ST3 = VE.n
        te_rt_d = {}
        r1done = {}
        for tt in range(NTILE):
            p7_tile(tt)
            if tt >= 2:
                p7_b(tt - 2)
        p7_b(NTILE - 2)
        p7_b(NTILE - 1)
        for tt in range(NTILE):
            ffn_tile(tt)

        if dbg:
            em("sync", lambda e: e.dma_start(out=dbg_t["dbg_xnf3"], in_=XNF[:]),
               waits=[(VE, VE_CWT_T[NTILE - 1])], inc=OUT, k=16)
            em("sync", lambda e: e.dma_start(out=dbg_t["dbg_cwt"], in_=CWT[:]),
               waits=[(VE, VE_CWT_T[NTILE - 1])], inc=OUT, k=16)

        def _fin(e):
            e.wait_ge(OUT.h, OUT.n)
            e.wait_ge(OUT2.h, OUT2.n)
        P.ops["sync"].append(_fin)

        for eng_name, deco in (("sync", blk.sync), ("gpsimd", blk.gpsimd), ("tensor", blk.tensor),
                               ("vector", blk.vector), ("scalar", blk.scalar)):
            ops = P.ops[eng_name]
            wrap = eng_name in ("vector", "scalar", "gpsimd")
            def make(ops=ops, wrap=wrap):
                def body(e):
                    if wrap:
                        e = _DrainAfter(e)
                    for op in ops:
                        op(e)
                return body
            deco(make())

    return nc, dbg_t


_NC_CACHE = {}

def _get_nc(dbg=0):
    if dbg not in _NC_CACHE:
        _NC_CACHE[dbg] = build(dbg)
    return _NC_CACHE[dbg]


def kernel(**inputs):
    nc, _ = _get_nc(0)
    hc = _host_consts()
    x = np.ascontiguousarray(inputs["L0_attn_x"])
    base = {
        "ln_w": inputs["L0_attn_ln_w"], "ln_b": inputs["L0_attn_ln_b"],
        "sign_W": inputs["L0_attn_sign_W"], "sign_b": inputs["L0_attn_sign_b"],
        "perm_W": inputs["L0_attn_perm_W"], "perm_b": inputs["L0_attn_perm_b"],
        "gate_W": inputs["L0_attn_gate_W"], "gate_b": inputs["L0_attn_gate_b"],
        "in_W": inputs["L0_attn_in_W"], "in_b": inputs["L0_attn_in_b"],
        "out_W": inputs["L0_attn_out_W"], "out_b": inputs["L0_attn_out_b"],
        "Wr": inputs["L0_ffn_Wr"], "Wg": inputs["L0_ffn_Wg"], "Wu": inputs["L0_ffn_Wu"],
        "Wd": inputs["L0_ffn_Wd"], "Sg": inputs["L0_ffn_Sg"], "Su": inputs["L0_ffn_Su"],
        "Sd": inputs["L0_ffn_Sd"],
    }
    base = {k: np.ascontiguousarray(v) for k, v in base.items()}
    base.update(hc)
    in_maps = [dict(base, x=np.ascontiguousarray(x[b])) for b in range(8)]
    res = run_bass_kernel_spmd(nc, in_maps, core_ids=list(range(8)))
    out = np.stack([res.results[b]["y"] for b in range(8)], axis=0)
    return out.astype(np.float32)


# version 31
# speedup vs baseline: 1.0464x; 1.0141x over previous
import numpy as np
from contextlib import ExitStack

import concourse.bass as bass
import concourse.mybir as mybir
from concourse.bass_utils import run_bass_kernel_spmd

F32 = mybir.dt.float32
BF16 = mybir.dt.bfloat16
AF = mybir.ActivationFunctionType
ALU = mybir.AluOpType
AX = mybir.AxisListType

S, D, H, NN = 2048, 1024, 4, 8
NSL = S // 128
NCH, LCH = 32, 64
EPS = 1e-5
NTILE = 4
NEXP = 16
FEXP = 512

_HOST_CONST = {}
def _host_consts():
    if _HOST_CONST:
        return _HOST_CONST
    _HOST_CONST["c_ident"] = np.eye(128, dtype=np.float32)
    _HOST_CONST["c_ones"] = np.ones((1, 128), dtype=np.float32)
    _HOST_CONST["c_iota"] = np.broadcast_to(
        np.tile(np.arange(8, dtype=np.float32), 64), (128, 512)).copy()
    ida = np.zeros((8, 9), dtype=np.float32)
    ida[:, :8] = np.eye(8, dtype=np.float32)
    _HOST_CONST["c_idaug"] = np.broadcast_to(ida.reshape(-1), (128, 72)).copy()
    oneh = np.zeros((16, 2048), dtype=np.float32)
    for e_ in range(16):
        oneh[e_, e_ * 128:(e_ + 1) * 128] = 1.0
    _HOST_CONST["c_oneh"] = oneh
    pm = np.zeros((128, 128), dtype=np.float32)
    for m in range(128):
        h_, c_ = m // 32, m % 32
        pm[c_ * 4 + h_, m] = 1.0
    _HOST_CONST["c_perm"] = pm
    return _HOST_CONST


class _Sem:
    def __init__(self, name):
        self.name = name
        self.n = 0
        self.h = None


_DRAIN_OPS = {
    "tensor_tensor", "tensor_add", "tensor_sub", "tensor_mul", "tensor_max",
    "tensor_copy", "tensor_scalar", "tensor_single_scalar", "tensor_scalar_add",
    "tensor_scalar_sub", "tensor_scalar_mul", "tensor_scalar_max", "tensor_scalar_min",
    "tensor_relu", "tensor_reduce", "reduce_max", "reduce_sum", "scalar_tensor_tensor",
    "tensor_tensor_scan", "memset", "memzero", "reciprocal", "select", "transpose",
    "activation", "sign", "square", "sqrt", "mul", "copy", "add",
}


class _DrainAfter:
    def __init__(self, e):
        self._e = e

    @property
    def raw(self):
        return self._e

    def __getattr__(self, k):
        f = getattr(self._e, k)
        if k in _DRAIN_OPS:
            def g(*a, **kw):
                r = f(*a, **kw)
                self._e.drain()
                return r
            return g
        return f


class _Prog:
    def __init__(self):
        self.ops = {k: [] for k in ("sync", "gpsimd", "tensor", "vector", "scalar")}

    def emit(self, eng, fn, waits=(), inc=None, k=1):
        waits = tuple((s, v) for (s, v) in waits)
        def cl(e, fn=fn, waits=waits, inc=inc, k=k):
            for (s, v) in waits:
                e.wait_ge(s.h, v)
            r = fn(e)
            if inc is not None:
                r.then_inc(inc.h, k)
        self.ops[eng].append(cl)
        if inc is not None:
            inc.n += k
        return inc.n if inc is not None else None


def build(dbg=0):
    nc = bass.Bass("TRN2", target_bir_lowering=False)
    I = lambda name, shape: nc.dram_tensor(name, list(shape), F32, kind="ExternalInput").ap()

    d_x = I("x", (S, D))
    d_lnw = I("ln_w", (D,)); d_lnb = I("ln_b", (D,))
    d_signW = I("sign_W", (D, 32)); d_signb = I("sign_b", (32,))
    d_permW = I("perm_W", (D, 256)); d_permb = I("perm_b", (256,))
    d_gateW = I("gate_W", (D, 32)); d_gateb = I("gate_b", (32,))
    d_inW = I("in_W", (D, 32)); d_inb = I("in_b", (32,))
    d_outW = I("out_W", (32, D)); d_outb = I("out_b", (D,))
    d_wr = I("Wr", (D, NEXP))
    d_wg = I("Wg", (NEXP, D, FEXP)); d_wu = I("Wu", (NEXP, D, FEXP)); d_wd = I("Wd", (NEXP, FEXP, D))
    d_sg = I("Sg", (D, D)); d_su = I("Su", (D, D)); d_sd = I("Sd", (D, D))
    d_ident = I("c_ident", (128, 128)); d_ones = I("c_ones", (1, 128))
    d_iota = I("c_iota", (128, 512)); d_idaug = I("c_idaug", (128, 72))
    d_oneh = I("c_oneh", (16, 2048))
    d_perm = I("c_perm", (128, 128))

    d_y = nc.dram_tensor("y", [S, D], F32, kind="ExternalOutput").ap()
    d_scrG = nc.dram_tensor("scrG", [18, 128, 8, FEXP], BF16, kind="Internal").ap()
    d_scrU = nc.dram_tensor("scrU", [18, 128, 8, FEXP], BF16, kind="Internal").ap()
    d_scrD = nc.dram_tensor("scrD", [18, 128, 4, D], BF16, kind="Internal").ap()
    dbg_t = {}
    if dbg:
        def DBG(name, shape, dt=F32):
            dbg_t[name] = nc.dram_tensor(name, list(shape), dt, kind="ExternalOutput").ap()
        DBG("dbg_xnf", (128, 8, S), BF16)
        DBG("dbg_projC", (128, LCH, 88))
        DBG("dbg_A", (128, 4096))
        DBG("dbg_hard", (128, 4096))
        DBG("dbg_dt", (128, 512)); DBG("dbg_ct", (128, 512))
        DBG("dbg_mall", (128, 9 * LCH * 8))
        DBG("dbg_hs", (128, 8))
        DBG("dbg_ys", (128, 512), BF16)
        DBG("dbg_xattn", (128, NSL, D))
        DBG("dbg_xnf3", (128, 8, S), BF16)
        DBG("dbg_cwt", (NEXP, S), BF16)
        DBG("dbg_rank", (128, 512)); DBG("dbg_aidx", (128, 512)); DBG("dbg_rmx", (128, 512))

    P = _Prog()
    LDX = [_Sem(f"LDX{i}") for i in range(4)]; LDP = _Sem("LDP"); GDK = _Sem("GDK")
    GPJ = [_Sem("GPJ0"), _Sem("GPJ1")]; SPJ = [_Sem("SPJ0"), _Sem("SPJ1")]; GSC = _Sem("GSC")
    FWG = [[_Sem("FWG0a"), _Sem("FWG0b")], [_Sem("FWG1a"), _Sem("FWG1b")]]
    FWD = [_Sem("FWD0"), _Sem("FWD1")]
    SCG = [_Sem("SCG0"), _Sem("SCG1")]; SCD = [_Sem("SCD0"), _Sem("SCD1")]
    FWGS = [[_Sem("FWGS0a"), _Sem("FWGS0b")], [_Sem("FWGS1a"), _Sem("FWGS1b")]]
    FWDS = [_Sem("FWDS0"), _Sem("FWDS1")]
    TE = _Sem("TE"); VE = _Sem("VE"); SE = _Sem("SE"); OUT = _Sem("OUT"); OUT2 = _Sem("OUT2")
    GP = _Sem("GP")
    sems = (LDX + [LDP, GDK] + GPJ + SPJ + [GSC] + FWG[0] + FWG[1] + FWD + SCG + SCD +
            FWGS[0] + FWGS[1] + FWDS + [TE, VE, SE, OUT, OUT2, GP])

    with ExitStack() as ctx:
        def SB(name, shape, dt=F32):
            return ctx.enter_context(nc.sbuf_tensor(name, shape, dt))
        XT = SB("XT", [128, NSL, D])
        XNF = SB("XNF", [128, 8, S], BF16)
        ARN = 24064
        ARH = SB("ARENA", [128, ARN])
        AR = ARH[:]
        def carve(off, nw, shape=None, dt=F32, rr=None, **kw):
            v = AR[:, off:off + nw]
            if dt is not F32:
                v = v.bitcast(dt)
            if rr is not None:
                v = v.rearrange(rr, **kw)
            return v
        A = carve(0, 4096)
        SQS = carve(0, 1024)
        HARD4 = carve(4096, 4096, rr="p (t i j) -> p t i j", t=LCH, i=NN)
        HARD = carve(4096, 4096)
        WP = carve(4096, 1408, dt=BF16, rr="p (k w) -> p k w", k=8)
        Y2BF = carve(4096 + 1408, 1024, dt=BF16, rr="p (a d) -> p a d", a=2)
        MALL = carve(8192, 4608, rr="p (t i k) -> p t i k", t=LCH, i=NN)
        WPF = carve(8192, 2816, rr="p (k w) -> p k w", k=8)
        DT = carve(12800, 512, rr="p (t i) -> p t i", t=LCH)
        CT = carve(13312, 512, rr="p (t i) -> p t i", t=LCH)
        PJT = carve(12800, 704, rr="p (a w) -> p a w", a=2)
        PJC = carve(13824, 5632, rr="p (t w) -> p t w", t=LCH)
        SCR = 19456
        SG = carve(SCR + 0, 512)
        RANK = carve(SCR + 0, 512)
        GM = carve(SCR + 512, 512)
        CAV = carve(SCR + 512, 512)
        RMX = carve(SCR + 1024, 512)
        T512a = carve(SCR + 1536, 512)
        T512b = carve(SCR + 2048, 512)
        RS = carve(SCR + 2560, 512)
        AIDX = carve(SCR + 2560, 512)
        ACCR = carve(SCR + 2560, 512)
        CS = carve(SCR + 3072, 512)
        SROW = carve(SCR + 3072, 512)
        YS = carve(SCR + 3072, 256, dt=BF16, rr="p (i t) -> p i t", i=NN)
        CV = carve(SCR + 3584, 512)
        YS2 = carve(SCR + 3584, 256, dt=BF16, rr="p (i t) -> p i t", i=NN)
        IOTA = carve(11008, 512)
        M64 = carve(SCR + 4096, 64)
        CIDX = carve(SCR + 4160, 64)
        PFX = carve(SCR + 4224, 72, rr="p (i k) -> p i k", i=NN)
        PFS = carve(SCR + 4296, 72, rr="p (i k) -> p i k", i=NN)
        NEWP = carve(SCR + 4368, 72, rr="p (i k) -> p i k", i=NN)
        IDA = carve(SCR + 4440, 72)
        HSB = carve(SCR + 4512, 9)
        UCOL = carve(SCR + 4522, 8)
        PFM = carve(SCR + 4532, 64)
        SLN = carve(8192, 512)
        MALL9 = carve(8192, 5184, rr="p (t i k) -> p t i k", t=LCH, i=9)
        AH9 = carve(13824, 4608, rr="p (t i j) -> p t i j", t=LCH, i=NN)
        PRD9 = carve(18432, 648)
        PRD = carve(14976, 648)
        PRDR = carve(0, 4608)
        GATED = carve(0, 9216, dt=BF16, rr="p (e k f) -> p e k f", e=9, k=4)
        WGr = carve(9216, 4096, dt=BF16, rr="p (r k f) -> p r k f", r=2, k=8)
        WUr = carve(13312, 4096, dt=BF16, rr="p (r k f) -> p r k f", r=2, k=8)
        WDr = carve(17408, 4096, dt=BF16, rr="p (r k f) -> p r k f", r=2, k=4)
        SILU = carve(21504, 1536, rr="p (a f) -> p a f", a=3)
        ST = SB("ST", [128, 16, NSL])
        LNWC = SB("LNWC", [128, 8]); LNBC = SB("LNBC", [128, 8])
        BROW = SB("BROW", [1, 352]); BROWB = SB("BROWB", [1, 352], BF16)
        OWB = SB("OWB", [32, D], BF16); OBB = SB("OBB", [1, D], BF16)
        WRB = SB("WRB", [128, 8, NEXP], BF16)
        IDT = SB("IDT", [128, 128], BF16); ONES = SB("ONESb", [1, 128], BF16)
        ONEH = SB("ONEH", [16, 2048], BF16); PERMB = SB("PERMB", [128, 128], BF16)
        E16 = SB("E16", [128, NEXP]); P16 = SB("P16", [128, NEXP]); PW1 = SB("PW1", [128, NEXP])
        OH = SB("OH", [128, NEXP]); CWS = SB("CWS", [128, NEXP]); CWBF = SB("CWBF", [128, NEXP], BF16)
        MX1 = SB("MX1", [128, 1]); MX2 = SB("MX2", [128, 1]); SM1 = SB("SM1", [128, 1])
        CWT = SB("CWT", [NEXP, S], BF16)
        YST = ARH[:][0:32, 8192:9216].bitcast(BF16)
        B = [ctx.enter_context(nc.psum_tensor(f"pb{i}", [128, 512], F32)) for i in range(8)]

        for s_ in sems:
            s_.h = ctx.enter_context(nc.semaphore(s_.name))
        blk = ctx.enter_context(nc.Block())

        iSUM, iNMU, iSSQ, iCMB, iNMC, iA1, iNM1, iS3, iNM3, iSQ3, iIS3, iNI3 = range(12)

        em = P.emit
        def dbgw():
            return [(OUT, OUT.n)] if dbg else []
        for q in range(4):
            em("sync", lambda e, q=q: e.dma_start(out=XT[:, 4*q:4*q+4, :],
                in_=d_x.rearrange("(s p) d -> p s d", p=128)[:, 4*q:4*q+4, :]), inc=LDX[q], k=16)

        def wdma(dst_off, width, src, kc):
            def f(e):
                return e.dma_start(
                    out=WPF[:, kc, :].rearrange("p (h w) -> p h w", h=H)[:, :, dst_off:dst_off + width],
                    in_=src[kc * 128:(kc + 1) * 128, :].rearrange("p (h w) -> p h w", h=H))
            return f
        for kc in range(8):
            em("sync", wdma(0, 8, d_signW, kc), inc=LDP, k=16)
            em("sync", wdma(8, 64, d_permW, kc), inc=LDP, k=16)
            em("sync", wdma(72, 8, d_gateW, kc), inc=LDP, k=16)
            em("sync", wdma(80, 8, d_inW, kc), inc=LDP, k=16)
        em("sync", lambda e: e.dma_start(out=LNWC[:], in_=d_lnw.rearrange("(k p) -> p k", p=128), allow_slow_non_contiguous=True), inc=LDP, k=16)
        em("sync", lambda e: e.dma_start(out=LNBC[:], in_=d_lnb.rearrange("(k p) -> p k", p=128), allow_slow_non_contiguous=True), inc=LDP, k=16)
        em("sync", lambda e: e.dma_start(
            out=BROW[:].rearrange("o (h w) -> o h w", h=H)[:, :, 0:8],
            in_=d_signb[None, :].rearrange("o (h w) -> o h w", h=H)), inc=LDP, k=16)
        em("sync", lambda e: e.dma_start(
            out=BROW[:].rearrange("o (h w) -> o h w", h=H)[:, :, 8:72],
            in_=d_permb[None, :].rearrange("o (h w) -> o h w", h=H)), inc=LDP, k=16)
        em("sync", lambda e: e.dma_start(
            out=BROW[:].rearrange("o (h w) -> o h w", h=H)[:, :, 72:80],
            in_=d_gateb[None, :].rearrange("o (h w) -> o h w", h=H)), inc=LDP, k=16)
        em("sync", lambda e: e.dma_start(
            out=BROW[:].rearrange("o (h w) -> o h w", h=H)[:, :, 80:88],
            in_=d_inb[None, :].rearrange("o (h w) -> o h w", h=H)), inc=LDP, k=16)
        em("sync", lambda e: e.dma_start(out=IOTA[:], in_=d_iota), inc=LDP, k=16)
        em("sync", lambda e: e.dma_start(out=IDA[:], in_=d_idaug), inc=LDP, k=16)
        LDW = LDP.n

        em("gpsimd", lambda e: e.dma_start(out=IDT[:], in_=d_ident), inc=GDK, k=16)
        em("gpsimd", lambda e: e.dma_start(out=ONES[:], in_=d_ones), inc=GDK, k=16)
        em("gpsimd", lambda e: e.dma_start(out=OWB[:], in_=d_outW), inc=GDK, k=16)
        em("gpsimd", lambda e: e.dma_start(out=OBB[:], in_=d_outb[None, :]), inc=GDK, k=16)
        em("gpsimd", lambda e: e.dma_start(out=WRB[:], in_=d_wr.rearrange("(k p) n -> p k n", p=128)), inc=GDK, k=16)
        em("gpsimd", lambda e: e.dma_start(out=ONEH[:], in_=d_oneh), inc=GDK, k=16)
        em("gpsimd", lambda e: e.dma_start(out=PERMB[:], in_=d_perm), inc=GDK, k=16)
        GDC = GDK.n

        for kc in range(8):
            em("vector", lambda e, kc=kc: e.tensor_scalar(
                WP[:, kc, :], WPF[:, kc, :], LNWC[:, kc:kc + 1], None, op0=ALU.mult),
                waits=[(LDP, LDW)], inc=VE)
        VE_WP = VE.n
        for kc in range(8):
            em("tensor", lambda e, kc=kc: e.matmul(
                B[7][0:1, 0:352], LNBC[:, kc:kc + 1], WPF[:, kc, :],
                start=(kc == 0), stop=(kc == 7)),
                waits=[(LDP, LDW)] if kc == 0 else (), inc=TE if kc == 7 else None)
        TE_BP = TE.n
        em("vector", lambda e: e.tensor_add(BROW[0:1, :], BROW[0:1, :], B[7][0:1, 0:352]),
           waits=[(TE, TE_BP)], inc=VE)
        em("vector", lambda e: e.tensor_copy(BROWB[0:1, :], BROW[0:1, :]), inc=VE)
        VE_BROW = VE.n

        inv_d = 1.0 / D
        for q in range(4):
            for si in range(4):
                s = 4 * q + si
                def ln1s(e, s=s):
                    e.activation(SQS[:], XT[:, s, :], AF.Identity,
                                 accum_out=ST[:, iSUM, s:s + 1])
                    e.activation(ST[:, iNMU, s:s + 1], ST[:, iSUM, s:s + 1], AF.Identity,
                                 scale=-inv_d)
                    e.activation(SQS[:], XT[:, s, :], AF.Square,
                                 bias=ST[:, iNMU, s:s + 1], scale=1.0,
                                 accum_out=ST[:, iSSQ, s:s + 1])
                    return e.raw.nop()
                em("scalar", ln1s, waits=[(LDX[q], 16)] if si == 0 else [], inc=SE)
        SE_SQ = SE.n
        stv = ST[:, iSSQ, :]
        def stats1(e):
            e.tensor_scalar(ST[:, iSQ3, :], stv, inv_d, EPS, op0=ALU.mult, op1=ALU.add)
            e.drain()
            e.reciprocal(ST[:, iIS3, :], ST[:, iSQ3, :])
            e.drain()
            e.tensor_scalar(ST[:, iNI3, :], stv, inv_d, None, op0=ALU.mult)
            e.tensor_mul(ST[:, iNI3, :], ST[:, iNI3, :], ST[:, iIS3, :])
            e.tensor_scalar(ST[:, iNI3, :], ST[:, iNI3, :], EPS, None, op0=ALU.add)
            e.drain()
            return e.engine_nop()
        em("vector", stats1, waits=[(SE, SE_SQ)], inc=VE)
        def stats2(e):
            e.sqrt(ST[:, iSQ3, :], ST[:, iSQ3, :])
            e.sqrt(ST[:, iNI3, :], ST[:, iNI3, :])
            return e.copy(ST[:, iS3, :], ST[:, iNI3, :])
        em("scalar", stats2, waits=[(VE, VE.n)], inc=SE)
        def stats3(e):
            e.reciprocal(ST[:, iA1, :], ST[:, iSQ3, :])
            e.drain()
            e.reciprocal(ST[:, iCMB, :], ST[:, iNI3, :])
            e.drain()
            e.tensor_mul(ST[:, iCMB, :], ST[:, iCMB, :], ST[:, iA1, :])
            e.tensor_mul(ST[:, iNM1, :], ST[:, iNMU, :], ST[:, iA1, :])
            e.tensor_scalar(ST[:, iA1, :], ST[:, iA1, :], 1.0, None, op0=ALU.add)
            e.tensor_mul(ST[:, iNMC, :], ST[:, iNMU, :], ST[:, iCMB, :])
            e.drain()
            return e.engine_nop()
        em("vector", stats3, waits=[(SE, SE.n)], inc=VE)
        VE_ST1 = VE.n

        TE_TRS = {}
        for s in range(NSL):
            em("scalar", lambda e, s=s: e.raw.activation(
                Y2BF[:, s % 2, :], XT[:, s, :], AF.Identity,
                bias=ST[:, iNMC, s:s + 1], scale=ST[:, iCMB, s:s + 1]),
                waits=[(VE, VE_ST1)] + ([(TE, TE_TRS[s - 2])] if s >= 2 else []), inc=SE)
            se_y = SE.n
            for kc in range(8):
                pb = B[5 + (kc % 2)]
                tw = ([(SE, se_y), (GDK, GDC)] if kc == 0 else [])
                if kc >= 2 or s > 0:
                    tw = tw + [(VE, VE.n)]
                em("tensor", lambda e, s=s, kc=kc, pb=pb: e.transpose(
                    pb[:].bitcast(BF16)[:, 0:128], Y2BF[:, s % 2, kc * 128:(kc + 1) * 128], IDT[:]),
                    waits=tw, inc=TE)
                te_tr = TE.n
                em("vector", lambda e, s=s, kc=kc, pb=pb: e.raw.tensor_copy(
                    XNF[:, kc, s * 128:(s + 1) * 128], pb[:].bitcast(BF16)[:, 0:128]),
                    waits=[(TE, te_tr)], inc=VE)
            TE_TRS[s] = TE.n
        VE_XNF = VE.n
        if dbg:
            em("sync", lambda e: e.dma_start(out=dbg_t["dbg_xnf"], in_=XNF[:]),
               waits=[(VE, VE_XNF)], inc=OUT, k=16)

        SE_PJ = {}
        GD_PJd = {}
        for s in range(NSL):
            pb = B[0 + (s % 2)]
            for kc in range(8):
                em("tensor", lambda e, s=s, kc=kc, pb=pb: e.matmul(
                    pb[:, 0:352], XNF[:, kc, s * 128:(s + 1) * 128], WP[:, kc, :],
                    start=(kc == 0), stop=False),
                    waits=([(VE, VE_XNF), (VE, VE_WP), (VE, VE_BROW), (GDK, GDC)]
                           if s == 0 and kc == 0 else []) +
                          ([(SE, SE_PJ[s - 2])] if s >= 2 and kc == 0 else []))
            em("tensor", lambda e, pb=pb: e.matmul(
                pb[:, 0:352], ONES[0:1, :], BROWB[0:1, :], start=False, stop=True), inc=TE)
            te_p = TE.n
            em("scalar", lambda e, s=s, pb=pb: e.raw.copy(PJT[:, s % 2, :], pb[:, 0:352]),
               waits=[(TE, te_p)] + ([(GPJ[s % 2], GD_PJd[s - 2][0]), (SPJ[s % 2], GD_PJd[s - 2][1])] if s >= 2 else []), inc=SE)
            SE_PJ[s] = SE.n
            for h in range(4):
                qeng = "gpsimd" if h < 2 else "sync"
                qsem = GPJ[s % 2] if h < 2 else SPJ[s % 2]
                for a in range(2):
                    em(qeng, lambda e, s=s, h=h, a=a: e.dma_start(
                        out=PJC[8 * s + 4 * a + h: 8 * s + 4 * a + h + 1],
                        in_=PJT[a * 64:(a + 1) * 64, s % 2, h * 88:(h + 1) * 88]),
                        waits=[(SE, SE_PJ[s])] if a == 0 and h in (0, 2) else (), inc=qsem, k=16)
            GD_PJd[s] = (GPJ[s % 2].n, SPJ[s % 2].n)
        GD_PJ0 = GPJ[0].n; GD_PJ1 = GPJ[1].n; SD_PJ0 = SPJ[0].n; SD_PJ1 = SPJ[1].n
        if dbg:
            em("sync", lambda e: e.dma_start(out=dbg_t["dbg_projC"], in_=PJC[:]),
               waits=[(GPJ[0], GD_PJ0), (GPJ[1], GD_PJ1), (SPJ[0], SD_PJ0), (SPJ[1], SD_PJ1)], inc=OUT, k=16)

        em("scalar", lambda e: e.activation(SG[:].rearrange("p (t i) -> p t i", i=8),
                                            PJC[:, :, 0:8], AF.Sigmoid),
           waits=[(GPJ[0], GD_PJ0), (GPJ[1], GD_PJ1), (SPJ[0], SD_PJ0), (SPJ[1], SD_PJ1)], inc=SE)
        em("scalar", lambda e: e.activation(GM[:].rearrange("p (t i) -> p t i", i=8),
                                            PJC[:, :, 72:80], AF.Sigmoid), inc=SE)
        em("scalar", lambda e: e.activation(A[:].rearrange("p (t ij) -> p t ij", ij=64),
                                            PJC[:, :, 8:72], AF.Exp), inc=SE)
        em("scalar", lambda e: e.copy(RMX[:].rearrange("p (t i) -> p t i", i=8),
                                      PJC[:, :, 80:88]), inc=SE)
        SE_SIG = SE.n
        def dcalc_g(e):
            e.tensor_scalar(T512a[:], SG[:], 2.0, 1.0, op0=ALU.mult, op1=ALU.subtract)
            e.tensor_tensor(DT.rearrange("p t i -> p (t i)"), GM[:], T512a[:], op=ALU.mult)
            e.tensor_scalar(T512b[:], GM[:], -1.0, 1.0, op0=ALU.mult, op1=ALU.add)
            e.tensor_tensor(CT.rearrange("p t i -> p (t i)"), RMX[:], T512b[:], op=ALU.mult)
            return e.raw.nop()
        em("gpsimd", dcalc_g, waits=[(SE, SE_SIG)] + dbgw(), inc=GP)
        GP_DC = GP.n
        def wsrcs(kind, idx):
            if kind == "e":
                return (d_wg[idx].rearrange("(k p) f -> p k f", p=128),
                        d_wu[idx].rearrange("(k p) f -> p k f", p=128),
                        d_wd[idx].rearrange("(k p) f -> p k f", p=128))
            hf = idx
            return (d_sg[:, hf * 512:(hf + 1) * 512].rearrange("(k p) f -> p k f", p=128),
                    d_su[:, hf * 512:(hf + 1) * 512].rearrange("(k p) f -> p k f", p=128),
                    d_sd[hf * 512:(hf + 1) * 512, :].rearrange("(k p) f -> p k f", p=128))
        groups = [[("sh", 0)] + [("e", i) for i in range(8)],
                  [("sh", 1)] + [("e", i) for i in range(8, 16)]]
        PCS = FWD[0]
        def precast(lo, hi):
            for s_idx in range(lo, hi):
                gx, ei = divmod(s_idx, 9)
                kind, idx = ("sh", gx) if ei == 0 else ("e", gx * 8 + ei - 1)
                wg_s, wu_s, wd_s = wsrcs(kind, idx)
                em("gpsimd", lambda e, wg_s=wg_s, s_idx=s_idx: e.dma_start(
                    out=d_scrG[s_idx], in_=wg_s), inc=PCS, k=16)
                em("gpsimd", lambda e, wu_s=wu_s, s_idx=s_idx: e.dma_start(
                    out=d_scrU[s_idx], in_=wu_s), inc=PCS, k=16)
                em("gpsimd", lambda e, wd_s=wd_s, s_idx=s_idx: e.dma_start(
                    out=d_scrD[s_idx], in_=wd_s), inc=PCS, k=16)
        precast(0, 9)
        if dbg:
            em("sync", lambda e: e.dma_start(out=dbg_t["dbg_dt"],
               in_=DT.rearrange("p t i -> p (t i)")), waits=[(GP, GP_DC)], inc=OUT, k=16)
            em("sync", lambda e: e.dma_start(out=dbg_t["dbg_ct"],
               in_=CT.rearrange("p t i -> p (t i)")), waits=[(GP, GP_DC)], inc=OUT, k=16)

        Av = A[:].rearrange("p (t i j) -> p t i j", i=8, j=8)
        RSt = RS[:].rearrange("p (t i) -> p t i", i=8)
        CSt = CS[:].rearrange("p (t j) -> p t j", j=8)
        SH = 144
        def s_recip(buf):
            def f(e):
                e.activation(SLN[:, 0:512 - SH], buf[:, SH:512], AF.Ln)
                e.activation(buf[:, SH:512], SLN[:, 0:512 - SH], AF.Exp, scale=-1.0)
                return e.raw.nop()
            return f
        for it in range(5):
            em("vector", lambda e: e.tensor_reduce(
                RSt, Av.rearrange("p t i j -> p (t i) j"), axis=AX.X, op=ALU.add),
                waits=([(SE, SE_SIG)] + dbgw()) if it == 0 else [], inc=VE)
            v_rr = VE.n
            em("scalar", s_recip(RS[:]), waits=[(VE, v_rr)], inc=SE)
            s_rr = SE.n
            em("vector", lambda e: e.reciprocal(RS[:, 0:SH], RS[:, 0:SH]), inc=VE)
            em("vector", lambda e: e.tensor_tensor(
                Av.rearrange("p t i j -> p (t i) j"),
                Av.rearrange("p t i j -> p (t i) j"),
                RS[:][:, :, None].broadcast_to([128, 512, 8]), op=ALU.mult),
                waits=[(SE, s_rr)], inc=VE)
            em("vector", lambda e: e.tensor_reduce(
                CSt, Av.rearrange("p t i j -> p t j i"), axis=AX.X, op=ALU.add), inc=VE)
            v_cr = VE.n
            em("scalar", s_recip(CS[:]), waits=[(VE, v_cr)], inc=SE)
            s_cr = SE.n
            em("vector", lambda e: e.reciprocal(CS[:, 0:SH], CS[:, 0:SH]), inc=VE)
            em("vector", lambda e: e.tensor_tensor(
                Av, Av, CSt[:, :, None, :].broadcast_to([128, LCH, 8, 8]), op=ALU.mult),
                waits=[(SE, s_cr)], inc=VE)
        GP_SK = GP.n
        VE_SK = VE.n
        if dbg:
            em("sync", lambda e: e.dma_start(out=dbg_t["dbg_A"], in_=A[:]),
               waits=[(VE, VE_SK), (GP, GP_SK)], inc=OUT, k=16)

        TG = 56
        c_tg = TG * 8
        ng = LCH - TG
        SROW7 = carve(13824, 3584)
        GT1 = carve(17408, ng * 32, rr="p (t i j) -> p t i j", t=ng, i=4)
        GT2 = carve(17408 + ng * 32, ng * 16, rr="p (t i j) -> p t i j", t=ng, i=2)
        def grank(e):
            e.tensor_reduce(RMX[:], Av.rearrange("p t i j -> p (t i) j"), axis=AX.X, op=ALU.max)
            e.memset(RANK[:], 0.0)
            Rv = RMX[:].rearrange("p (t i) -> p t i", i=8)
            Kv = RANK[:].rearrange("p (t i) -> p t i", i=8)
            Ta = T512a[:].rearrange("p (t i) -> p t i", i=8)
            Tb = T512b[:].rearrange("p (t i) -> p t i", i=8)
            for dd in range(1, 8):
                e.tensor_tensor(Ta[:, :, 0:8 - dd], Rv[:, :, 0:8 - dd], Rv[:, :, dd:8],
                                op=ALU.is_ge)
                e.tensor_tensor(Kv[:, :, dd:8], Kv[:, :, dd:8], Ta[:, :, 0:8 - dd], op=ALU.add)
                e.tensor_tensor(Tb[:, :, 0:8 - dd], Rv[:, :, dd:8], Rv[:, :, 0:8 - dd],
                                op=ALU.is_gt)
                e.tensor_tensor(Kv[:, :, 0:8 - dd], Kv[:, :, 0:8 - dd], Tb[:, :, 0:8 - dd],
                                op=ALU.add)
            e.memset(CAV[:], 1.0)
            e.memset(AIDX[:], 0.0)
            return e.raw.engine_nop()
        em("vector", grank, waits=[(VE, VE_SK), (GP, GP_DC), (LDP, LDW)] + dbgw(), inc=VE)
        gm_done = {}
        for r in range(7):
            mbuf = (T512a, T512b)[r % 2]
            em("vector", lambda e, r=r, mbuf=mbuf: e.tensor_single_scalar(
                mbuf[:], RANK[:], float(r), op=ALU.is_equal),
                waits=[(GP, gm_done[r - 2])] if r >= 2 else [], inc=VE)
            v_mk = VE.n
            def gext(e, r=r, mbuf=mbuf):
                Hg = HARD4[:, TG:LCH]
                e.tensor_tensor(Hg.rearrange("p t i j -> p (t i) j"),
                                Av[:, TG:LCH].rearrange("p t i j -> p (t i) j"),
                                mbuf[:, c_tg:512][:, :, None].broadcast_to([128, ng * 8, 8]),
                                op=ALU.mult)
                e.tensor_tensor(GT1, Hg[:, :, 0:4, :], Hg[:, :, 4:8, :], op=ALU.add)
                e.tensor_tensor(GT2, GT1[:, :, 0:2, :], GT1[:, :, 2:4, :], op=ALU.add)
                e.tensor_tensor(SROW7[:, r * 512 + c_tg:(r + 1) * 512]
                                .rearrange("p (t j) -> p t j", j=8),
                                GT2[:, :, 0, :], GT2[:, :, 1, :], op=ALU.add)
                return e.raw.nop()
            em("gpsimd", gext, waits=[(VE, v_mk)], inc=GP)
            gm_done[r] = GP.n
            em("vector", lambda e, mbuf=mbuf: e.tensor_tensor(
                HARD4[:, 0:TG].rearrange("p t i j -> p (t i) j"),
                Av[:, 0:TG].rearrange("p t i j -> p (t i) j"),
                mbuf[:, 0:c_tg][:, :, None].broadcast_to([128, c_tg, 8]), op=ALU.mult), inc=VE)
            em("vector", lambda e, r=r: e.tensor_reduce(
                SROW7[:, r * 512:r * 512 + c_tg].rearrange("p (t j) -> p t j", j=8),
                HARD4[:, 0:TG].rearrange("p t i j -> p t j i"), axis=AX.X, op=ALU.max),
                inc=VE)
        def grounds(e):
            BIG = 1.0e30
            srow = SROW[:].rearrange("p (t j) -> p t j", j=8)
            cv = CV[:].rearrange("p (t j) -> p t j", j=8)
            Ta = T512a[:].rearrange("p (t i) -> p t i", i=8)
            Tb = T512b[:].rearrange("p (t i) -> p t i", i=8)
            for r in range(8):
                if r < 7:
                    e.tensor_single_scalar(T512a[:], RANK[:], float(r), op=ALU.is_equal)
                    e.tensor_scalar(CV[:], CAV[:], 1.0, BIG, op0=ALU.subtract, op1=ALU.mult)
                    e.tensor_add(CV[:], CV[:], SROW7[:, r * 512:(r + 1) * 512])
                    e.tensor_reduce(M64[:], cv, axis=AX.X, op=ALU.max)
                    e.tensor_tensor(Tb, cv, M64[:][:, :, None].broadcast_to([128, LCH, 8]),
                                    op=ALU.is_equal)
                else:
                    e.tensor_single_scalar(T512a[:], RANK[:], 7.0, op=ALU.is_equal)
                    e.tensor_copy(T512b[:], CAV[:])
                e.tensor_mul(SROW[:], T512b[:], IOTA[:])
                e.tensor_reduce(CIDX[:], srow, axis=AX.X, op=ALU.add)
                e.tensor_tensor(cv, Ta, CIDX[:][:, :, None].broadcast_to([128, LCH, 8]),
                                op=ALU.mult)
                e.tensor_add(AIDX[:], AIDX[:], CV[:])
                if r < 7:
                    e.tensor_sub(CAV[:], CAV[:], T512b[:])
            e.drain()
            for j in range(8):
                e.tensor_single_scalar(
                    AH9[:, :, :, j].rearrange("p t i -> p (t i)"),
                    AIDX[:], float(j), op=ALU.is_equal)
            e.drain()
            e.tensor_tensor(AH9[:, 0:TG, :, 0:8].rearrange("p t i j -> p (t i) j"),
                            AH9[:, 0:TG, :, 0:8].rearrange("p t i j -> p (t i) j"),
                            DT.rearrange("p t i -> p (t i)")[:, 0:c_tg][:, :, None]
                            .broadcast_to([128, c_tg, 8]),
                            op=ALU.mult)
            e.drain()
            return e.raw.engine_nop()
        em("vector", grounds, waits=[(GP, gm_done[6])], inc=VE)
        v_oh = VE.n
        em("gpsimd", lambda e: e.tensor_tensor(
            AH9[:, TG:LCH, :, 0:8].rearrange("p t i j -> p (t i) j"),
            AH9[:, TG:LCH, :, 0:8].rearrange("p t i j -> p (t i) j"),
            DT.rearrange("p t i -> p (t i)")[:, c_tg:512][:, :, None]
            .broadcast_to([128, ng * 8, 8]),
            op=ALU.mult), waits=[(VE, v_oh)], inc=GP)
        GP_GR = GP.n
        precast(9, 18)
        PC_ALL = PCS.n
        VE_GR = VE.n
        if dbg:
            em("sync", lambda e: e.dma_start(out=dbg_t["dbg_hard"],
               in_=AH9[:, :, :, 0:8].rearrange("p t i j -> p (t i) j")),
               waits=[(VE, VE_GR), (GP, GP_GR)], inc=OUT, k=16)
            em("sync", lambda e: e.dma_start(out=dbg_t["dbg_rank"], in_=RANK[:]),
               waits=[(VE, VE_GR), (GP, GP_GR)], inc=OUT, k=16)
            em("sync", lambda e: e.dma_start(out=dbg_t["dbg_aidx"], in_=AIDX[:]),
               waits=[(VE, VE_GR), (GP, GP_GR)], inc=OUT, k=16)
            em("sync", lambda e: e.dma_start(out=dbg_t["dbg_rmx"], in_=RMX[:]),
               waits=[(VE, VE_GR), (GP, GP_GR)], inc=OUT, k=16)

        def compose(e):
            e.tensor_copy(AH9[:, :, :, 8].rearrange("p t i -> p (t i)"),
                          CT.rearrange("p t i -> p (t i)"))
            e.memset(MALL9[:, :, 8, 0:8], 0.0)
            e.memset(MALL9[:, :, 8, 8], 1.0)
            e.tensor_copy(MALL9[:, 0, 0:8, :], AH9[:, 0, :, :])
            e.drain()
            for t in range(1, LCH):
                e.tensor_tensor(PRD9[:, 0:648].rearrange("p (i j k) -> p i j k", i=8, j=9),
                                AH9[:, t, :, :][:, :, :, None].broadcast_to([128, 8, 9, 9]),
                                MALL9[:, t - 1][:, None, :, :].broadcast_to([128, 8, 9, 9]),
                                op=ALU.mult)
                e.tensor_reduce(MALL9[:, t, 0:8, :],
                                PRD9[:, 0:648].rearrange("p (i j k) -> p i k j", i=8, j=9),
                                axis=AX.X, op=ALU.add)
            e.drain()
            e.tensor_copy(PFX[:], MALL9[:, LCH - 1, 0:8, :].rearrange("p i k -> p (i k)"))
            e.tensor_copy(PFM.rearrange("p (a j) -> p a j", a=8), MALL9[:, LCH - 1, 0:8, 0:8])
            return e.raw.engine_nop()
        em("vector", compose, waits=[(VE, VE_GR), (GP, GP_GR)] + dbgw(), inc=VE)
        VE_CMP = VE.n

        ve_prev = VE_CMP
        for lvl in range(5):
            dlt = 4 * (1 << lvl)
            em("gpsimd", lambda e, dlt=dlt: e.dma_start(out=PFS[dlt:128, :, :], in_=PFX[0:128 - dlt, :, :]),
               waits=[(VE, ve_prev)], inc=GSC, k=16)
            em("gpsimd", lambda e, dlt=dlt: e.dma_start(
                out=PFS[0:dlt, :, :].rearrange("p k i -> p (k i)"), in_=IDA[0:dlt, :]),
               inc=GSC, k=16)
            gd_sh = GSC.n
            def hslvl(e):
                e.tensor_tensor(PRD[:, 0:576].rearrange("p (x k) -> p x k", k=9),
                                PFM[:, :, None].broadcast_to([128, 64, 9]),
                                PFS[:].rearrange("p j k -> p (j k)")[:, None, :]
                                .broadcast_to([128, 8, 72]),
                                op=ALU.mult)
                e.tensor_reduce(NEWP[:],
                                PRD[:, 0:576].rearrange("p (a j k) -> p a k j", a=8, j=8),
                                axis=AX.X, op=ALU.add)
                e.tensor_add(NEWP[:, :, 8], NEWP[:, :, 8], PFX[:, :, 8])
                e.tensor_copy(PFX[:], NEWP[:])
                e.tensor_copy(PFM.rearrange("p (a j) -> p a j", a=8), NEWP[:, :, 0:8])
                return e.raw.engine_nop()
            em("vector", hslvl, waits=[(GSC, gd_sh)], inc=VE)
            ve_prev = VE.n
        VE_HS = VE.n

        em("vector", lambda e: e.tensor_copy(UCOL[:], PFX[:, :, 8]), waits=[(VE, VE_HS)], inc=VE)
        VE_UC = VE.n
        em("gpsimd", lambda e: e.dma_start(out=HSB[4:128, 0:8], in_=UCOL[0:124, :]),
           waits=[(VE, VE_UC)], inc=GSC, k=16)
        GD_HS = GSC.n
        def hszero(e):
            e.memset(HSB[0:4, 0:8], 0.0)
            e.memset(HSB[:, 8:9], 1.0)
            return e.raw.engine_nop()
        em("vector", hszero, waits=[(VE, VE_HS)], inc=VE)
        VE_HS0 = VE.n
        if dbg:
            em("sync", lambda e: e.dma_start(out=dbg_t["dbg_mall"],
               in_=MALL9[:, :, 0:8, :].rearrange("p t i k -> p t (i k)")),
               waits=[(VE, VE_CMP)], inc=OUT, k=16)
            em("sync", lambda e: e.dma_start(out=dbg_t["dbg_hs"], in_=HSB[:, 0:8]),
               waits=[(GSC, GD_HS), (VE, VE_HS0)], inc=OUT, k=16)

        def recon(e):
            e.tensor_tensor(PRDR[:, 0:4608].rearrange("p (t i k) -> p t i k", t=LCH, i=NN),
                            MALL9[:, :, 0:8, :],
                            HSB[:][:, None, None, :].broadcast_to([128, LCH, 8, 9]),
                            op=ALU.mult)
            e.tensor_reduce(ACCR[:],
                            PRDR[:, 0:4608].rearrange("p (t i k) -> p t i k", t=LCH, i=NN),
                            axis=AX.X, op=ALU.add)
            e.tensor_copy(YS, ACCR[:].rearrange("p (t i) -> p i t", i=NN))
            return e.raw.engine_nop()
        em("vector", recon, waits=[(GSC, GD_HS), (VE, VE_HS0)] + dbgw(), inc=VE)
        VE_YS = VE.n
        if dbg:
            em("sync", lambda e: e.dma_start(out=dbg_t["dbg_ys"],
               in_=YS.rearrange("p i t -> p (i t)")), waits=[(VE, VE_YS)], inc=OUT, k=16)

        em("tensor", lambda e: e.matmul(
            B[5][:, 0:512], PERMB[:], YS.rearrange("p i t -> p (i t)"), start=True, stop=True),
            waits=[(VE, VE_YS), (GDK, GDC)], inc=TE)
        te_pm = TE.n
        em("vector", lambda e: e.raw.tensor_copy(YS2.rearrange("p i t -> p (i t)"), B[5][:, 0:512]),
           waits=[(TE, te_pm)], inc=VE)
        VE_YS2 = VE.n
        for h in range(4):
            for i in range(NN):
                qeng = "gpsimd" if h < 2 else "sync"
                qsem = GSC if h < 2 else SPJ[0]
                em(qeng, lambda e, h=h, i=i: e.dma_start(
                    out=YST[h * 8 + i: h * 8 + i + 1, :].rearrange("q (c t) -> q c t", c=32),
                    in_=YS2[h * 32:(h + 1) * 32, i, :]),
                    waits=([(VE, VE_YS2)] + dbgw()) if i == 0 and h in (0, 2) else (),
                    inc=qsem, k=16)
        GD_YST = (GSC.n, SPJ[0].n)

        VE_RES = {}
        TE_TR3 = {}
        VE_CWT_T = {}
        out2_tile = {}
        RT7 = carve(23040, 64); RE7 = carve(23104, 64); ROH7 = carve(23168, 64)
        RPW7 = carve(23232, 64); RCW7 = carve(23296, 64)
        CWB7 = carve(23360, 32, dt=BF16)
        RM4 = carve(23392, 4); CI4 = carve(23396, 4)
        SQDMP = carve(1024, 1024)
        ve_bank_free = {i: VE.n for i in range(8)}
        ve_cwt_prev = VE.n
        ev56 = {5: (VE, 0), 6: (VE, 0)}
        te_slot_free_gu = {0: TE.n, 1: TE.n}
        te_slot_free_d = {0: TE.n, 1: TE.n}
        gu_loads = {}; gu_loads_u = {}
        gi_counter = 0
        scg_w = {0: 0, 1: 0}; scd_w = {0: 0, 1: 0}
        scg_done = {}; scd_done = {}

        for s in range(NSL):
            pb0, pb1 = (B[2], B[3]) if s % 2 == 0 else (B[5], B[6])
            em("tensor", lambda e, s=s, pb0=pb0: e.matmul(
                pb0[:, 0:512], YST[:, s * 128:(s + 1) * 128], OWB[:, 0:512], start=True, stop=False),
                waits=[(GSC, GD_YST[0]), (SPJ[0], GD_YST[1]), (GDK, GDC)] if s == 0 else
                      ([(VE, VE_RES[s - 2])] if s >= 2 else []))
            em("tensor", lambda e, pb0=pb0: e.matmul(
                pb0[:, 0:512], ONES[0:1, :], OBB[0:1, 0:512], start=False, stop=True))
            em("tensor", lambda e, s=s, pb1=pb1: e.matmul(
                pb1[:, 0:512], YST[:, s * 128:(s + 1) * 128], OWB[:, 512:1024], start=True, stop=False))
            em("tensor", lambda e, pb1=pb1: e.matmul(
                pb1[:, 0:512], ONES[0:1, :], OBB[0:1, 512:1024], start=False, stop=True), inc=TE)
            te_o = TE.n
            def resid(e, s=s, pb0=pb0, pb1=pb1):
                for nh in range(2):
                    pb = pb0 if nh == 0 else pb1
                    e.scalar_tensor_tensor(SQS[:, nh * 512:(nh + 1) * 512],
                                           XT[:, s, nh * 512:(nh + 1) * 512], ST[:, iA1, s:s + 1],
                                           pb[:, 0:512], op0=ALU.mult, op1=ALU.add)
                    e.tensor_scalar(XT[:, s, nh * 512:(nh + 1) * 512],
                                    SQS[:, nh * 512:(nh + 1) * 512], ST[:, iNM1, s:s + 1], None,
                                    op0=ALU.add)
                return e.raw.engine_nop()
            em("vector", resid, waits=[(TE, te_o)] + (dbgw() if s == 0 else []), inc=VE)
            VE_RES[s] = VE.n
            def ln3s(e, s=s):
                e.activation(SQDMP[:], XT[:, s, :], AF.Identity,
                             accum_out=ST[:, iS3, s:s + 1])
                e.activation(ST[:, iNM3, s:s + 1], ST[:, iS3, s:s + 1], AF.Identity,
                             scale=-inv_d)
                e.activation(SQDMP[:], XT[:, s, :], AF.Square,
                             bias=ST[:, iNM3, s:s + 1], scale=1.0,
                             accum_out=ST[:, iSQ3, s:s + 1])
                return e.raw.nop()
            em("scalar", ln3s, waits=[(VE, VE_RES[s])], inc=SE)
        SE_SQ3 = SE.n
        ve_bank_free = {i: VE.n for i in range(8)}
        if dbg:
            em("sync", lambda e: e.dma_start(out=dbg_t["dbg_xattn"], in_=XT[:]),
               waits=[(VE, VE_RES[NSL - 1])], inc=OUT, k=16)

        def p7_tile(tt):
            nonlocal ve_cwt_prev
            Y2B7 = Y2BF
            rb = (tt % 3) * 64
            if tt == 0:
                te_slot_free_gu[0] = te_slot_free_gu[1] = TE.n
                te_slot_free_d[0] = te_slot_free_d[1] = TE.n

            for si in range(4):
                s = tt * 4 + si
                em("scalar", lambda e, s=s, Y2B7=Y2B7: e.raw.activation(
                    Y2B7[:, s % 2, :], XT[:, s, :], AF.Identity,
                    bias=ST[:, iNI3, s:s + 1], scale=ST[:, iIS3, s:s + 1]),
                    waits=[(VE, VE_ST3)] + ([(TE, TE_TR3[s - 2])] if s >= 2 else []) + (dbgw() if s == 0 else []), inc=SE)
                se_y = SE.n
                for kc in range(8):
                    pb = B[5 + (kc % 2)]
                    tw = ([(SE, se_y)] if kc == 0 else [])
                    if kc >= 2 or s > 0:
                        tw = tw + [ev56[5 + (kc % 2)]]
                    em("tensor", lambda e, s=s, kc=kc, pb=pb, Y2B7=Y2B7: e.transpose(
                        pb[:].bitcast(BF16)[:, 0:128], Y2B7[:, s % 2, kc * 128:(kc + 1) * 128], IDT[:]),
                        waits=tw, inc=TE)
                    te_tr = TE.n
                    if kc % 2 == 0 or kc == 7:
                        em("vector", lambda e, s=s, kc=kc, pb=pb: e.raw.tensor_copy(
                            XNF[:, kc, s * 128:(s + 1) * 128], pb[:].bitcast(BF16)[:, 0:128]),
                            waits=[(TE, te_tr)], inc=VE)
                    else:
                        em("scalar", lambda e, s=s, kc=kc, pb=pb: e.raw.copy(
                            XNF[:, kc, s * 128:(s + 1) * 128], pb[:].bitcast(BF16)[:, 0:128]),
                            waits=[(TE, te_tr)], inc=SE)
                    ev56[5 + (kc % 2)] = (VE, VE.n) if (kc % 2 == 0 or kc == 7) else (SE, SE.n)
                TE_TR3[s] = TE.n
                ve_tr = VE.n
                se_tr = SE.n
                ve_bank_free[5] = VE.n
                ve_bank_free[6] = VE.n
                for kc in range(8):
                    em("tensor", lambda e, s=s, si=si, kc=kc, rb=rb: e.matmul(
                        B[4][:, rb + si * 16:rb + (si + 1) * 16],
                        XNF[:, kc, s * 128:(s + 1) * 128], WRB[:, kc, :],
                        start=(kc == 0), stop=(kc == 7)),
                        waits=(([(VE, ve_tr), (SE, se_tr)] +
                                ([(VE, r1done[tt - 3])] if si == 0 and tt >= 3 else []))
                               if kc == 0 else ()),
                        inc=TE if kc == 7 else None)
            te_rt_d[tt] = TE.n
        def p7_b(tt):
            nonlocal ve_cwt_prev
            rb = (tt % 3) * 64
            def rsoft1(e, rb=rb):
                e.tensor_reduce(RM4[:], B[4][:, rb:rb + 64].rearrange("p (s x) -> p s x", x=16),
                                axis=AX.X, op=ALU.max)
                e.tensor_tensor(RT7[:].rearrange("p (s x) -> p s x", x=16),
                                B[4][:, rb:rb + 64].rearrange("p (s x) -> p s x", x=16),
                                RM4[:][:, :, None].broadcast_to([128, 4, 16]),
                                op=ALU.subtract)
                return e.raw.engine_nop()
            em("vector", rsoft1, waits=[(TE, te_rt_d[tt])], inc=VE)
            r1done[tt] = VE.n
            em("scalar", lambda e: e.activation(RE7[:], RT7[:], AF.Exp),
               waits=[(VE, VE.n)], inc=SE)
            SE_REXP = SE.n
            def rsoft2(e):
                ev = lambda a: a[:].rearrange("p (s x) -> p s x", x=16)
                bc = lambda m: m[:][:, :, None].broadcast_to([128, 4, 16])
                e.tensor_reduce(CI4[:], ev(RE7), axis=AX.X, op=ALU.add)
                e.reciprocal(CI4[:], CI4[:])
                e.tensor_tensor(ev(RE7), ev(RE7), bc(CI4), op=ALU.mult)
                e.tensor_reduce(RM4[:], ev(RE7), axis=AX.X, op=ALU.max)
                e.tensor_tensor(ev(ROH7), ev(RE7), bc(RM4), op=ALU.is_equal)
                e.tensor_mul(RPW7[:], RE7[:], ROH7[:])
                e.tensor_sub(RPW7[:], RE7[:], RPW7[:])
                e.tensor_reduce(RM4[:], ev(RPW7), axis=AX.X, op=ALU.max)
                e.tensor_tensor(ev(RCW7), ev(RPW7), bc(RM4), op=ALU.is_equal)
                e.tensor_add(ROH7[:], ROH7[:], RCW7[:])
                e.tensor_mul(RCW7[:], RE7[:], ROH7[:])
                return e.raw.engine_nop()
            em("vector", rsoft2, waits=[(SE, SE_REXP)], inc=VE)
            ve_bank_free[4] = VE.n
            em("scalar", lambda e: e.raw.copy(CWB7[:], RCW7[:]), waits=[(VE, VE.n)], inc=SE)
            SE_CWB = SE.n
            for si in range(4):
                s = tt * 4 + si
                em("tensor", lambda e, si=si: e.transpose(
                    B[7][:].bitcast(BF16)[0:NEXP, 0:128], CWB7[:, si * 16:(si + 1) * 16], IDT[:]),
                    waits=[(SE, SE_CWB), (VE, ve_cwt_prev)] +
                          ([(VE, ve_bank_free[7])] if si == 0 else []), inc=TE)
                te_cw = TE.n
                em("vector", lambda e, s=s: e.raw.tensor_copy(
                    CWT[0:NEXP, s * 128:(s + 1) * 128],
                    B[7][:].bitcast(BF16)[0:NEXP, 0:128]), waits=[(TE, te_cw)], inc=VE)
                ve_cwt_prev = VE.n
            ve_bank_free[7] = VE.n
            VE_CWT_T[tt] = VE.n

        def ffn_tile(tt):
            nonlocal gi_counter
            tok = slice(tt * 512, (tt + 1) * 512)
            for gidx, grp in enumerate(groups):
                for ei, (kind, idx) in enumerate(grp):
                    gslot = gi_counter % 2
                    wg_s, wu_s, wd_s = wsrcs(kind, idx)
                    s_idx = gidx * 9 + ei
                    for hf_ in range(2):
                        em("sync", lambda e, s_idx=s_idx, gslot=gslot, hf_=hf_: e.dma_start(
                            out=WGr[:, gslot, :, hf_ * 256:(hf_ + 1) * 256],
                            in_=d_scrG[s_idx][:, :, hf_ * 256:(hf_ + 1) * 256]),
                            waits=([(TE, te_slot_free_gu[gslot])] +
                                   ([(PCS, PC_ALL)] if tt == 0 else []) +
                                   (dbgw() if tt == 0 and gi_counter == 0 else [])) if hf_ == 0 else (),
                            inc=FWGS[gslot][hf_], k=16)
                        em("sync", lambda e, s_idx=s_idx, gslot=gslot, hf_=hf_: e.dma_start(
                            out=WUr[:, gslot, :, hf_ * 256:(hf_ + 1) * 256],
                            in_=d_scrU[s_idx][:, :, hf_ * 256:(hf_ + 1) * 256]),
                            inc=FWGS[gslot][hf_], k=16)
                        gu_loads[(gi_counter, hf_)] = (FWGS[gslot][hf_], FWGS[gslot][hf_].n)
                        gu_loads_u[(gi_counter, hf_)] = gu_loads[(gi_counter, hf_)]
                    has_cw = (kind == "e")
                    bcb = 4 if gi_counter % 2 == 0 else 7
                    if has_cw:
                        em("tensor", lambda e, idx=idx, tok=tok, bcb=bcb: e.matmul(
                            B[bcb][:, 0:512], ONEH[:, idx * 128:(idx + 1) * 128], CWT[0:NEXP, tok],
                            start=True, stop=True),
                            waits=[(VE, VE_CWT_T[tt]), (VE, ve_bank_free[bcb])], inc=TE)
                    te_bc = TE.n
                    for mc in range(4):
                        rr = (gi_counter * 4 + mc) % 3
                        gb = (0, 1, 5)[rr]
                        ub = (2, 3, 6)[rr]
                        for kc in range(8):
                            em("tensor", lambda e, gslot=gslot, kc=kc, mc=mc, tok=tok, gb=gb: e.matmul(
                                B[gb][:, 0:512], WGr[:, gslot, kc, mc * 128:(mc + 1) * 128],
                                XNF[:, kc, tok], start=(kc == 0), stop=(kc == 7)),
                                waits=(([gu_loads[(gi_counter, mc // 2)],
                                         (VE, ve_bank_free[gb])] +
                                        ([(VE, VE_CWT_T[tt])] if ei == 0 and mc == 0 else []))
                                       if kc == 0 else ()),
                                inc=TE if kc == 7 else None)
                        te_g = TE.n
                        for kc in range(8):
                            em("tensor", lambda e, gslot=gslot, kc=kc, mc=mc, tok=tok, ub=ub: e.matmul(
                                B[ub][:, 0:512], WUr[:, gslot, kc, mc * 128:(mc + 1) * 128],
                                XNF[:, kc, tok], start=(kc == 0), stop=(kc == 7)),
                                waits=[(VE, ve_bank_free[ub])] if kc == 0 else (),
                                inc=TE if kc == 7 else None)
                        te_u = TE.n
                        em("scalar", lambda e, rr=rr, gb=gb: e.raw.activation(
                            SILU[:, rr, :], B[gb][:, 0:512], AF.Silu),
                            waits=[(TE, te_g)], inc=SE)
                        se_si = SE.n
                        def gevac(e, mc=mc, ub=ub, ei=ei, has_cw=has_cw, bcb=bcb, rr=rr):
                            sc = SILU[:, rr, :]
                            e.tensor_tensor(sc, sc, B[ub][:, 0:512], op=ALU.mult)
                            if has_cw:
                                e.raw.tensor_tensor(GATED[:, ei, mc, :], sc, B[bcb][:, 0:512],
                                                    op=ALU.mult)
                            else:
                                e.raw.tensor_copy(GATED[:, ei, mc, :], sc)
                            return e.raw.engine_nop()
                        em("vector", gevac, waits=[(SE, se_si), (TE, te_u)] +
                           ([(TE, te_bc)] if has_cw and mc == 0 else []) +
                           (dbgw() if gi_counter == 0 and mc == 0 else []), inc=VE)
                        ve_bank_free[gb] = VE.n
                        ve_bank_free[ub] = VE.n
                        if has_cw:
                            ve_bank_free[bcb] = VE.n
                    te_slot_free_gu[gslot] = TE.n
                    gi_counter += 1
                ve_gated = VE.n
                for ei, (kind, idx) in enumerate(grp):
                    dslot = ei % 2
                    wg_s, wu_s, wd_s = wsrcs(kind, idx)
                    s_idx = gidx * 9 + ei
                    em("sync", lambda e, s_idx=s_idx, dslot=dslot: e.dma_start(
                        out=WDr[:, dslot, :, :], in_=d_scrD[s_idx]),
                        waits=[(TE, te_slot_free_d[dslot])] +
                              ([(PCS, PC_ALL)] if tt == 0 else []),
                        inc=FWDS[dslot], k=16)
                    gd_wd = (FWDS[dslot], FWDS[dslot].n)
                    for sl in range(4):
                        for nh in range(2):
                            bank = sl * 2 + nh
                            for kc in range(4):
                                last = (sl == 3 and nh == 1 and kc == 3)
                                em("tensor", lambda e, ei=ei, sl=sl, nh=nh, kc=kc, dslot=dslot, bank=bank: e.matmul(
                                    B[bank][:, 0:512],
                                    GATED[:, ei, kc, sl * 128:(sl + 1) * 128],
                                    WDr[:, dslot, kc, nh * 512:(nh + 1) * 512],
                                    start=(ei == 0 and kc == 0), stop=(ei == 8 and kc == 3)),
                                    waits=(([gd_wd, (VE, ve_gated)] if sl == 0 and nh == 0 else []) +
                                           ([(VE, ve_bank_free[bank])] if ei == 0 else []))
                                    if kc == 0 else (),
                                    inc=TE if last else None)
                    te_slot_free_d[dslot] = TE.n
                te_dn = TE.n
                for bank in (0, 1, 5, 2, 3, 6, 4, 7):
                    sl, nh = bank // 2, bank % 2
                    s_ = tt * 4 + sl
                    em("vector", lambda e, s_=s_, nh=nh, bank=bank: e.raw.tensor_tensor(
                        XT[:, s_, nh * 512:(nh + 1) * 512], B[bank][:, 0:512],
                        XT[:, s_, nh * 512:(nh + 1) * 512], op=ALU.add),
                        waits=(([(TE, te_dn)] + dbgw()) if bank == 0 else ()), inc=VE)
                    ve_bank_free[bank] = VE.n
                if gidx == 1:
                    ve_fin = VE.n
                    for sl in range(4):
                        s = tt * 4 + sl
                        em("gpsimd", lambda e, s=s: e.dma_start(
                            out=d_y.rearrange("(s p) d -> p s d", p=128)[:, s, :], in_=XT[:, s, :]),
                            waits=[(VE, ve_fin)] if sl == 0 else (), inc=OUT2, k=16)
                    out2_tile[tt] = OUT2.n

        def stats4(e):
            e.tensor_scalar(ST[:, iIS3, :], ST[:, iSQ3, :], inv_d, EPS, op0=ALU.mult, op1=ALU.add)
            e.drain()
            return e.engine_nop()
        em("vector", stats4, waits=[(SE, SE_SQ3)], inc=VE)
        em("scalar", lambda e: e.sqrt(ST[:, iIS3, :], ST[:, iIS3, :]), waits=[(VE, VE.n)], inc=SE)
        def stats5(e):
            e.reciprocal(ST[:, iIS3, :], ST[:, iIS3, :])
            e.drain()
            e.tensor_mul(ST[:, iNI3, :], ST[:, iNM3, :], ST[:, iIS3, :])
            e.drain()
            return e.engine_nop()
        em("vector", stats5, waits=[(SE, SE.n)], inc=VE)
        VE_ST3 = VE.n
        te_rt_d = {}
        r1done = {}
        for tt in range(NTILE):
            p7_tile(tt)
            if tt >= 2:
                p7_b(tt - 2)
        p7_b(NTILE - 2)
        p7_b(NTILE - 1)
        for tt in range(NTILE):
            ffn_tile(tt)

        if dbg:
            em("sync", lambda e: e.dma_start(out=dbg_t["dbg_xnf3"], in_=XNF[:]),
               waits=[(VE, VE_CWT_T[NTILE - 1])], inc=OUT, k=16)
            em("sync", lambda e: e.dma_start(out=dbg_t["dbg_cwt"], in_=CWT[:]),
               waits=[(VE, VE_CWT_T[NTILE - 1])], inc=OUT, k=16)

        def _fin(e):
            e.wait_ge(OUT.h, OUT.n)
            e.wait_ge(OUT2.h, OUT2.n)
        P.ops["sync"].append(_fin)

        for eng_name, deco in (("sync", blk.sync), ("gpsimd", blk.gpsimd), ("tensor", blk.tensor),
                               ("vector", blk.vector), ("scalar", blk.scalar)):
            ops = P.ops[eng_name]
            wrap = eng_name in ("vector", "scalar", "gpsimd")
            def make(ops=ops, wrap=wrap):
                def body(e):
                    if wrap:
                        e = _DrainAfter(e)
                    for op in ops:
                        op(e)
                return body
            deco(make())

    return nc, dbg_t


_NC_CACHE = {}

def _get_nc(dbg=0):
    if dbg not in _NC_CACHE:
        _NC_CACHE[dbg] = build(dbg)
    return _NC_CACHE[dbg]


def kernel(**inputs):
    nc, _ = _get_nc(0)
    hc = _host_consts()
    x = np.ascontiguousarray(inputs["L0_attn_x"])
    base = {
        "ln_w": inputs["L0_attn_ln_w"], "ln_b": inputs["L0_attn_ln_b"],
        "sign_W": inputs["L0_attn_sign_W"], "sign_b": inputs["L0_attn_sign_b"],
        "perm_W": inputs["L0_attn_perm_W"], "perm_b": inputs["L0_attn_perm_b"],
        "gate_W": inputs["L0_attn_gate_W"], "gate_b": inputs["L0_attn_gate_b"],
        "in_W": inputs["L0_attn_in_W"], "in_b": inputs["L0_attn_in_b"],
        "out_W": inputs["L0_attn_out_W"], "out_b": inputs["L0_attn_out_b"],
        "Wr": inputs["L0_ffn_Wr"], "Wg": inputs["L0_ffn_Wg"], "Wu": inputs["L0_ffn_Wu"],
        "Wd": inputs["L0_ffn_Wd"], "Sg": inputs["L0_ffn_Sg"], "Su": inputs["L0_ffn_Su"],
        "Sd": inputs["L0_ffn_Sd"],
    }
    base = {k: np.ascontiguousarray(v) for k, v in base.items()}
    base.update(hc)
    in_maps = [dict(base, x=np.ascontiguousarray(x[b])) for b in range(8)]
    res = run_bass_kernel_spmd(nc, in_maps, core_ids=list(range(8)))
    out = np.stack([res.results[b]["y"] for b in range(8)], axis=0)
    return out.astype(np.float32)
